# version 03
# baseline (speedup 1.0000x reference)
import contextlib
import numpy as np
import ml_dtypes
import concourse.bass as bass
import concourse.mybir as mybir
from concourse.bass_utils import run_bass_kernel_spmd

F32 = mybir.dt.float32
BF16 = mybir.dt.bfloat16
I32 = mybir.dt.int32
AF = mybir.ActivationFunctionType
OP = mybir.AluOpType
AX = mybir.AxisListType
BFNP = ml_dtypes.bfloat16
NEG = -1e30
FPAD = 16


def mkcfg(mini=False):
    c = {}
    if mini:
        c.update(B=1, Tloc=256, D=512, NKH=4, E=4, ED=256, dbg=True)
    else:
        c.update(B=4, Tloc=4096, D=2048, NKH=16, E=8, ED=2048, dbg=False)
    c["C"] = 128
    c["NKHL"] = c["NKH"] // 2
    c["VHL"] = c["NKHL"] * 2
    c["NVH"] = c["NKH"] * 2
    c["KD"] = c["NKH"] * 128
    c["VD"] = c["NVH"] * 128
    c["KCH"] = c["D"] // 128
    c["EDCH"] = c["ED"] // 128
    c["nchl"] = c["Tloc"] // c["C"]
    c["nchg"] = 2 * c["nchl"]
    c["Tpad"] = ((c["Tloc"] + FPAD + 127) // 128) * 128
    c["ncores"] = 2 * c["B"]
    c["SECL"] = c["KD"] + c["VD"]
    c["KW"] = c["NKHL"] * 128
    c["VW"] = c["VHL"] * 128
    c["TB"] = min(1024, c["Tloc"])
    c["ntb"] = c["Tloc"] // c["TB"]
    c["phases"] = 9
    return c


def _consts():
    tt = np.arange(128)[:, None]
    ss = np.arange(128)[None, :]
    return dict(
        I128f=np.eye(128, dtype=np.float32),
        Ibf=np.eye(128, dtype=np.float32).astype(BFNP),
        triUT=np.triu(np.ones((128, 128), np.float32)),
        maskM=np.where(ss < tt, 0.0, NEG).astype(np.float32),
        maskPTpos=np.where(ss >= tt, 0.0, -NEG).astype(np.float32),
        ones_r=np.ones((1, 128), np.float32),
        cvals=np.tile(np.array([[1e-5, 1e-6, -0.5, 1.0 / 128, 0.0, 0.0, 0.0, 0.0]], np.float32), (128, 1)),
        ones_cf=np.ones((128, 1), np.float32),
        ones_cb=np.ones((128, 1), np.float32).astype(BFNP),
    )


def prep_inputs(inputs, cfg):
    B, Tloc, D, NKH, E, ED = (cfg[k] for k in ("B", "Tloc", "D", "NKH", "E", "ED"))
    x = np.ascontiguousarray(inputs["L0_attn_hidden_states"], dtype=np.float32)
    gu = inputs["L0_ffn_gate_up"]
    gu_i = np.ascontiguousarray(
        gu.reshape(E, D, 2, ED // 128, 128).transpose(0, 1, 3, 2, 4)
        .reshape(E, D, 2 * ED)).astype(np.float32)
    cn = _consts()
    SECL, nchl, nchg = cfg["SECL"], cfg["nchl"], cfg["nchg"]
    KW, VW = cfg["KW"], cfg["VW"]
    Wo = np.ascontiguousarray(inputs["L0_attn_W_out"], dtype=np.float32)
    maps = []
    for c in range(cfg["ncores"]):
        b, p = c // 2, c % 2
        t0 = p * Tloc
        xs = np.zeros((Tloc + 3, D), np.float32)
        lo = max(0, t0 - 3)
        xs[3 - (t0 - lo):] = x[b, lo:t0 + Tloc]
        tab = np.zeros((8, 64), np.int64)
        for cg in range(nchg):
            blk, cl = cg // nchl, cg % nchl
            base = (blk * nchl + cl) * (2 * SECL * 128) + p * SECL * 128
            tab[0, cg] = base
            tab[1, cg] = (blk * Tloc + cl * 128) * (6 * cfg["VHL"]) + p * 3 * cfg["VHL"]
            tab[2, cg] = base + KW * 128
            tab[3, cg] = base + KW * 128
            tab[4, cg] = base + (2 * KW + VW) * 128
        for bb in range(2):
            tab[5, bb] = (bb * nchl * 2 + p) * SECL * 128
            tab[6, bb] = bb * Tloc * (6 * cfg["VHL"]) + p * 3 * cfg["VHL"]
        d = dict(
            x_slice=xs,
            Wq=np.ascontiguousarray(inputs["L0_attn_W_qkvz"], dtype=np.float32),
            Wb=np.ascontiguousarray(inputs["L0_attn_W_ba"], dtype=np.float32),
            convw=np.ascontiguousarray(inputs["L0_attn_conv_w"], dtype=np.float32),
            dtb=inputs["L0_attn_dt_bias"].reshape(1, -1).astype(np.float32),
            Alog=inputs["L0_attn_A_log"].reshape(1, -1).astype(np.float32),
            nw=inputs["L0_attn_norm_weight"].reshape(-1, 1).astype(np.float32),
            Wout_my=np.ascontiguousarray(Wo[p * (cfg["VD"] // 2):(p + 1) * (cfg["VD"] // 2)]),
            rsc=inputs["L0_ffn_router_scale"].reshape(-1, 1).astype(np.float32),
            gw=np.ascontiguousarray(inputs["L0_ffn_gate_w"], dtype=np.float32),
            gu=gu_i,
            dn=np.ascontiguousarray(inputs["L0_ffn_down"], dtype=np.float32),
            pes=inputs["L0_ffn_per_expert_scale"].reshape(1, -1).astype(np.float32),
            offs=tab.astype(np.int32),
        )
        d.update(cn)
        maps.append(d)
    return maps


ENGS = ("sync", "gpsimd", "tensor", "vector", "scalar")
QSEM = {"sync": "qS", "gpsimd": "qG", "scalar": "qC"}
ESEM = {"vector": "eV", "scalar": "eA", "tensor": "eT", "gpsimd": "eP"}
NQ = 4
SEMS = (["%s%d" % (q, i) for q in ("qS", "qG", "qC") for i in range(NQ)]
        + ["eV", "eA", "eT", "eP", "cc"])
Z = ("eV", 0)


class Plan:
    def __init__(self):
        self.steps = {e: [] for e in ENGS}
        self.cnt = {s: 0 for s in SEMS}
        self.qn = {}

    def kill(self):
        self.dead = True

    def _add(self, eng, sem, k, emit, waits):
        if getattr(self, "dead", False):
            return self.cnt[sem]
        flat = []
        for w in waits:
            if isinstance(w, list):
                flat.extend(w)
            else:
                flat.append(w)
        self.cnt[sem] += k
        self.steps[eng].append((emit, [w for w in flat if w[1] > 0], sem, k))
        return self.cnt[sem]

    def qcov(self, eng):
        return [("%s%d" % (QSEM[eng], i), self.cnt["%s%d" % (QSEM[eng], i)])
                for i in range(NQ)]

    def dma(self, eng, emit, waits=()):
        if getattr(self, "dead", False):
            return Z
        qn = self.qn.get(eng, 0)
        self.qn[eng] = qn + 1
        sem = "%s%d" % (QSEM[eng], qn % NQ)
        w = [(sem, self.cnt[sem])] + list(waits)
        return (sem, self._add(eng, sem, 16, emit, w))

    def op(self, eng, emit, waits=(), chain=True):
        sem = ESEM[eng]
        w = list(waits)
        if chain and eng in ("vector", "scalar") and self.cnt[sem] > 0:
            w.append((sem, self.cnt[sem]))
        return self._add(eng, sem, 1, emit, w)

    def cc(self, emit, waits=()):
        return self._add("gpsimd", "cc", 1, emit, waits)

    def barrier(self):
        tgt = [(s, v) for s, v in self.cnt.items() if v > 0]
        for e in ENGS:
            self.steps[e].append((None, tgt, None, 0))

    def emit(self, nc, semh):
        def runner(engname):
            def run(e):
                waited = {}
                for emitf, waits, sem, k in self.steps[engname]:
                    for snm, val in waits:
                        if waited.get(snm, 0) < val:
                            e.wait_ge(semh[snm], val)
                            waited[snm] = val
                    if emitf is not None:
                        emitf(e).then_inc(semh[sem], k)
            return run
        with nc.Block() as blk:
            blk.sync(runner("sync"))
            blk.gpsimd(runner("gpsimd"))
            blk.tensor(runner("tensor"))
            blk.vector(runner("vector"))
            blk.scalar(runner("scalar"))


def tile_plan(total, width=512):
    if total <= width:
        return [(0, total)]
    starts = [i * width for i in range(total // width)]
    if total % width:
        starts.append(total - width)
    return [(s, width) for s in starts]


def build(cfg):
    B, Tloc, D, NKH, E, ED, C = (cfg[k] for k in ("B", "Tloc", "D", "NKH", "E", "ED", "C"))
    KCH, EDCH, VHL, NKHL = cfg["KCH"], cfg["EDCH"], cfg["VHL"], cfg["NKHL"]
    nchl, nchg, Tpad, SECL = cfg["nchl"], cfg["nchg"], cfg["Tpad"], cfg["SECL"]
    NVH, KD, VD, KW, VW = cfg["NVH"], cfg["KD"], cfg["VD"], cfg["KW"], cfg["VW"]
    TB, ntb = cfg["TB"], cfg["ntb"]
    dbg = cfg["dbg"]
    WBA = 4 * NKH + E
    GBW = 3 * VHL
    NLEV = 6

    nc = bass.Bass("TRN2", target_bir_lowering=False)
    io = {}
    for nm, shp, dt in [
        ("x_slice", [Tloc + 3, D], F32), ("Wq", [D, 6 * NKH * 128], F32),
        ("Wb", [D, 4 * NKH], F32), ("convw", [4 * NKH * 128, 4], F32),
        ("dtb", [1, NVH], F32), ("Alog", [1, NVH], F32), ("nw", [128, 1], F32),
        ("Wout_my", [VD // 2, D], F32), ("rsc", [D, 1], F32), ("gw", [D, E], F32),
        ("gu", [E, D, 2 * ED], F32), ("dn", [E, ED, D], F32), ("pes", [1, E], F32),
        ("offs", [8, 64], I32), ("I128f", [128, 128], F32), ("Ibf", [128, 128], BF16),
        ("triUT", [128, 128], F32), ("maskM", [128, 128], F32),
        ("maskPTpos", [128, 128], F32), ("ones_r", [1, 128], F32),
        ("ones_cf", [128, 1], F32), ("ones_cb", [128, 1], BF16),
        ("cvals", [128, 8], F32),
    ]:
        io[nm] = nc.dram_tensor(nm, shp, dt, kind="ExternalInput")
    io["out_my"] = nc.dram_tensor("out_my", [Tloc, D], F32, kind="ExternalOutput")
    dr = {
        "xn_cc": nc.dram_tensor("xn_cc", [KCH, Tpad, 128], BF16),
        "rfac": nc.dram_tensor("rfac_rows", [Tpad, 1], F32),
        "contrib": nc.dram_tensor("contrib_cb", [nchl, 2 * SECL, 128], BF16),
        "gcb": nc.dram_tensor("gath_cb", [2, nchl, 2 * SECL, 128], BF16),
        "cgb": nc.dram_tensor("contrib_gb", [Tloc, 2 * GBW], F32),
        "ggb": nc.dram_tensor("gath_gb", [2, Tloc, 2 * GBW], F32),
        "scan_loc": nc.dram_tensor("scan_loc", [2 * nchl, SECL, 128], BF16),
        "gb_loc": nc.dram_tensor("gb_loc", [2 * Tloc, GBW], F32),
        "og": nc.dram_tensor("og_fm", [VW, 2 * Tloc], BF16),
        "sp": nc.dram_tensor("spart", [2 * Tloc, D], F32),
        "smy": nc.dram_tensor("s_my", [Tloc, D], F32),
        "gmy": nc.dram_tensor("g_my", [Tloc, D], F32),
    }
    db = {}
    if dbg:
        for nm, ref in [("d_xn", "xn_cc"), ("d_contrib", "contrib"), ("d_cgb", "cgb"),
                        ("d_og", "og"), ("d_sp", "sp"), ("d_smy", "smy"),
                        ("d_gmy", "gmy"), ("d_rfac", "rfac")]:
            t = dr[ref]
            db[nm] = nc.dram_tensor(nm, [int(z) for z in t.shape], t.dtype,
                                    kind="ExternalOutput")
        db["d_act"] = nc.dram_tensor("d_act", [128, cfg["EDCH"] * cfg["TB"]], BF16,
                                     kind="ExternalOutput")
        db["d_cfm"] = nc.dram_tensor("d_cfm", [8, Tloc], F32, kind="ExternalOutput")
        db["d_logit"] = nc.dram_tensor("d_logit", [Tloc, 8], F32, kind="ExternalOutput")
        db["d_acc"] = nc.dram_tensor("d_acc", [128, (cfg["TB"] // 128) * cfg["D"]], F32,
                                     kind="ExternalOutput")

    P = Plan()
    es = contextlib.ExitStack()
    phase_es = []

    def close_phase():
        while phase_es:
            phase_es.pop().close()

    def newphase():
        st = contextlib.ExitStack()
        phase_es.append(st)
        return lambda nm, shp, dt: st.enter_context(nc.sbuf_tensor(nm, shp, dt))

    with es:
        semh = {s: es.enter_context(nc.semaphore(s)) for s in SEMS}
        psF = es.enter_context(nc.psum_tensor("psF", [128, 2048], F32))
        psH = es.enter_context(nc.psum_tensor("psH", [128, 2048], F32))
        sbg = lambda nm, shp, dt: es.enter_context(nc.sbuf_tensor(nm, shp, dt))

        cI = sbg("cI", [128, 128], F32)
        cIb = sbg("cIb", [128, 128], BF16)
        ctri = sbg("ctri", [128, 128], F32)
        cmM = sbg("cmM", [128, 128], F32)
        cmPT = sbg("cmPT", [128, 128], F32)
        c1r = sbg("c1r", [1, 128], F32)
        c1cf = sbg("c1cf", [128, 1], F32)
        c1cb = sbg("c1cb", [128, 1], BF16)
        nwc = sbg("nwc", [128, 1], F32)
        dtbb = sbg("dtbb", [128, NVH], F32)
        negeA = sbg("negeA", [128, NVH], F32)
        pesb = sbg("pesb", [128, E], F32)
        cfm = sbg("cfm", [8, Tloc], BF16)
        offsb = sbg("offsb", [1, 512], I32)
        Sf = sbg("Sf", [128, VW], F32)
        Sbf = sbg("Sbf", [128, VW], BF16)
        cvsb = sbg("cvsb", [128, 8], F32)
        for ci_, cv_ in enumerate([1e-5, 1e-6, -0.5, 1.0 / 128]):
            nc.const_aps.register(F32, cv_, cvsb[:, ci_:ci_ + 1])

        sb = newphase()
        small = sb("small", [1, 2 * NVH + E], F32)
        for h, tnm in [(cI, "I128f"), (cIb, "Ibf"), (ctri, "triUT"), (cmM, "maskM"),
                       (cmPT, "maskPTpos"), (c1r, "ones_r"), (c1cf, "ones_cf"),
                       (c1cb, "ones_cb"), (nwc, "nw")]:
            P.dma("sync", lambda e, h=h, t=io[tnm]: e.dma_start(out=h[:], in_=t.ap()))
        P.dma("sync", lambda e: e.dma_start(out=small[0:1, 0:NVH], in_=io["dtb"].ap()))
        P.dma("sync", lambda e: e.dma_start(out=small[0:1, NVH:2 * NVH], in_=io["Alog"].ap()))
        l3 = P.dma("sync", lambda e: e.dma_start(out=small[0:1, 2 * NVH:], in_=io["pes"].ap()))
        P.dma("sync", lambda e: e.dma_start(
            out=offsb[0:1, 0:512].rearrange("o (r c) -> o r c", r=8),
            in_=io["offs"].ap()))
        P.dma("sync", lambda e: e.dma_start(out=cvsb[:], in_=io["cvals"].ap()))
        P.op("tensor", lambda e: e.matmul(psF[:, 0:NVH], c1r[0:1, :], small[0:1, 0:NVH],
                                          start=True, stop=True), waits=P.qcov("sync"))
        P.op("tensor", lambda e: e.matmul(psF[:, 256:256 + NVH], c1r[0:1, :],
                                          small[0:1, NVH:2 * NVH], start=True, stop=True))
        t3 = P.op("tensor", lambda e: e.matmul(psF[:, 512:512 + E], c1r[0:1, :],
                                               small[0:1, 2 * NVH:], start=True, stop=True))
        a1 = P.op("scalar", lambda e: e.activation(negeA[:], psF[:, 256:256 + NVH], AF.Exp),
                  waits=[("eT", t3)])
        P.op("vector", lambda e: e.tensor_copy(dtbb[:], psF[:, 0:NVH]), waits=[("eT", t3)])
        ini_pes = P.op("vector", lambda e: e.tensor_copy(pesb[:], psF[:, 512:512 + E]))
        P.op("vector", lambda e: e.tensor_scalar(negeA[:], negeA[:], -1.0, None,
                                                 OP.mult, OP.bypass), waits=[("eA", a1)])
        P.op("vector", lambda e: e.memset(Sf[:], 0.0))
        ini_done = P.op("vector", lambda e: e.memset(Sbf[:], 0.0))
        psF_free = ("eV", ini_pes)

        xa = sb("xa", [128, D], F32)
        xnb = sb("xnb", [128, D], BF16)
        stat = sb("stat", [128, 12], F32)
        sqt = sb("sqt", [128, D], F32)
        ntile = (Tloc + 3 + 127) // 128
        pv6 = pa1 = 0
        pxst = prst = Z
        vz1 = P.op("vector", lambda e: e.memset(xnb[:], 0.0))
        vz2 = P.op("vector", lambda e: e.memset(stat[:, 11:12], 0.0))
        npad_t = Tpad - 16 - Tloc
        for r0p, nrp in [(0, 13), (16 + Tloc, npad_t)]:
            if nrp <= 0:
                continue
            pxst = P.dma("gpsimd", lambda e, r0p=r0p, nrp=nrp: e.dma_start(
                out=dr["xn_cc"].ap()[:, r0p:r0p + nrp, :].transpose([1, 0, 2]),
                in_=xnb[0:nrp, :].rearrange("p (k f) -> p k f", k=KCH)),
                waits=[("eV", vz1)])
            prst = P.dma("gpsimd", lambda e, r0p=r0p, nrp=nrp: e.dma_start(
                out=dr["rfac"].ap()[r0p:r0p + nrp, :], in_=stat[0:nrp, 11:12]),
                waits=[("eV", vz2)])
        pxst = P.qcov("gpsimd")
        prst = P.qcov("gpsimd")
        for i in range(ntile):
            r0 = i * 128
            nr = min(128, Tloc + 3 - r0)
            ld = P.dma("sync", lambda e, r0=r0, nr=nr: e.dma_start(
                out=xa[0:nr, :], in_=io["x_slice"].ap()[r0:r0 + nr, :]),
                waits=[("eV", pv6), ("eA", pa1)])
            pa1 = P.op("scalar", lambda e, nr=nr: e.activation(
                sqt[0:nr, :], xa[0:nr, :], AF.Square, accum_out=stat[0:nr, 1:2]),
                waits=[ld])
            v = P.op("vector", lambda e, nr=nr: e.reduce_sum(stat[0:nr, 0:1], xa[0:nr, :], AX.X),
                     waits=[ld, pxst])
            v = P.op("vector", lambda e, nr=nr: e.tensor_scalar(
                stat[0:nr, 2:3], stat[0:nr, 0:1], 1.0 / D, None, OP.mult, OP.bypass))
            v = P.op("vector", lambda e, nr=nr: e.tensor_tensor(
                stat[0:nr, 3:4], stat[0:nr, 2:3], stat[0:nr, 2:3], OP.mult))
            v4 = P.op("vector", lambda e, nr=nr: e.scalar_tensor_tensor(
                stat[0:nr, 4:5], stat[0:nr, 1:2], 1.0 / D, stat[0:nr, 3:4],
                OP.mult, OP.subtract), waits=[("eA", pa1)])
            a2 = P.op("scalar", lambda e, nr=nr: e.activation(
                stat[0:nr, 5:6], stat[0:nr, 4:5], AF.Sqrt, bias=1e-5), waits=[("eV", v4)])
            v = P.op("vector", lambda e, nr=nr: e.reciprocal(stat[0:nr, 6:7], stat[0:nr, 5:6]),
                     waits=[("eA", a2)])
            pv6 = P.op("vector", lambda e, nr=nr: e.tensor_scalar(
                xnb[0:nr, :], xa[0:nr, :], stat[0:nr, 2:3], stat[0:nr, 6:7],
                OP.subtract, OP.mult))
            v = P.op("vector", lambda e, nr=nr: e.tensor_tensor(
                stat[0:nr, 7:8], stat[0:nr, 4:5], stat[0:nr, 6:7], OP.mult))
            v8 = P.op("vector", lambda e, nr=nr: e.tensor_tensor(
                stat[0:nr, 8:9], stat[0:nr, 7:8], stat[0:nr, 6:7], OP.mult))
            a3 = P.op("scalar", lambda e, nr=nr: e.activation(
                stat[0:nr, 9:10], stat[0:nr, 8:9], AF.Sqrt, bias=1e-6), waits=[("eV", v8)])
            v9 = P.op("vector", lambda e, nr=nr: e.reciprocal(
                stat[0:nr, 10:11], stat[0:nr, 9:10]), waits=[("eA", a3), prst])
            pxst = P.dma("gpsimd", lambda e, r0=r0, nr=nr: e.dma_start(
                out=dr["xn_cc"].ap()[:, 13 + r0:13 + r0 + nr, :].transpose([1, 0, 2]),
                in_=xnb[0:nr, :].rearrange("p (k f) -> p k f", k=KCH)),
                waits=[("eV", pv6)])
            prst = P.dma("gpsimd", lambda e, r0=r0, nr=nr: e.dma_start(
                out=dr["rfac"].ap()[13 + r0:13 + r0 + nr, :], in_=stat[0:nr, 10:11]),
                waits=[("eV", v9)])
        if dbg:
            P.dma("gpsimd", lambda e: e.dma_start(out=db["d_xn"].ap(), in_=dr["xn_cc"].ap()),
                  waits=P.qcov("gpsimd"))
            P.dma("gpsimd", lambda e: e.dma_start(out=db["d_rfac"].ap(), in_=dr["rfac"].ap()),
                  waits=P.qcov("gpsimd"))

        if cfg["phases"] <= 0:
            P.kill()
        P.barrier()
        close_phase()
        sb = newphase()
        XNR = FPAD + Tloc
        xnfm = [sb(f"xnfm{k}", [128, XNR], BF16) for k in range(KCH)]
        wsb = [sb(f"wsb{j}", [128, KCH * 128], BF16) for j in range(2)]
        wbag = sb("wbag", [128, KCH * WBA], BF16)
        gwkb = [sb(f"gwk{j}", [128, E], F32) for j in range(2)]
        rscf = sb("rscf", [128, KCH], F32)
        cwsb = [sb(f"cwsb{j}", [128, 4], F32) for j in range(2)]
        mixf = sb("mixf", [128, Tloc + 3], BF16)
        cacc = sb("cacc", [128, Tloc], F32)
        rrow = sb("rrow", [1, 1024], F32)
        outb = sb("outb", [128, Tloc], BF16)
        bat = sb("bat", [128, WBA], F32)
        btmp = sb("btmp", [128, max(2 * NVH, 6 * E + 8)], F32)
        spsc = sb("spsc", [128, NVH], F32)
        spb = sb("spb", [128, NVH], F32)
        gbp = sb("gbp", [128, 2 * GBW], F32)
        rfc = sb("rfc", [128, 1], F32)
        ctile = sb("ctile", [128, 8], F32)

        xl = [P.dma("sync", lambda e, k=k: e.dma_start(
            out=xnfm[k][:], in_=dr["xn_cc"].ap()[k][0:XNR, :], transpose=True))
              for k in range(KCH)]
        xl_all = P.qcov("sync")
        wbld = P.dma("gpsimd", lambda e: e.dma_start(
            out=wbag[:].rearrange("p (k f) -> p k f", f=WBA)[:, :, 0:4 * NKH],
            in_=io["Wb"].ap().rearrange("(k p) f -> p k f", p=128)))
        rscld = P.dma("gpsimd", lambda e: e.dma_start(
            out=rscf[:], in_=io["rsc"].ap().rearrange("(k p) o -> p k o", p=128),
            allow_slow_non_contiguous=True))
        foldv = {}
        for k in range(KCH):
            gld = P.dma("scalar", lambda e, k=k: e.dma_start(
                out=gwkb[k % 2][:], in_=io["gw"].ap()[k * 128:(k + 1) * 128, :]),
                waits=[("eV", foldv.get(k - 2, 0))])
            foldv[k] = P.op("vector", lambda e, k=k: e.tensor_scalar(
                wbag[:, k * WBA + 4 * NKH:(k + 1) * WBA], gwkb[k % 2][:],
                rscf[:, k:k + 1], float(D) ** -0.5, OP.mult, OP.mult),
                waits=[gld, rscld, wbld])
        fold_all = foldv[KCH - 1]

        PSB = 1536
        gbst = {}
        logit_dump = {}
        prev_slot_ev = 0
        prev_ct_ev = 0
        for i in range(nchl):
            col0 = FPAD + i * 128
            mml = 0
            for k in range(KCH):
                w0 = [xl_all, ("eV", fold_all), ("eV", prev_slot_ev),
                      ("eV", prev_ct_ev)] if k == 0 else []
                mml = P.op("tensor", lambda e, k=k, col0=col0: e.matmul(
                    psH[:, PSB:PSB + WBA], xnfm[k][:, col0:col0 + 128],
                    wbag[:, k * WBA:(k + 1) * WBA], start=(k == 0), stop=(k == KCH - 1)),
                    waits=w0)
            rfl = P.dma("scalar", lambda e, i=i: e.dma_start(
                out=rfc[:], in_=dr["rfac"].ap()[FPAD + i * 128:FPAD + (i + 1) * 128, :]),
                waits=[prst, ("eA", 0 if i == 0 else rfl_used)])
            vb = P.op("vector", lambda e: e.tensor_copy(bat[:], psH[:, PSB:PSB + WBA]),
                      waits=[("eT", mml)])
            prev_slot_ev = vb
            lg0 = 4 * NKH
            vcl = P.op("vector", lambda e: e.tensor_copy(btmp[:, 0:E], bat[:, lg0:lg0 + E]),
                       waits=[logit_dump.get(i - 1, Z)])
            bsel = bat[:, 0:4 * NKH].rearrange("p (h f) -> p h f", f=4)[:, :, 0:2]
            asel = bat[:, 0:4 * NKH].rearrange("p (h f) -> p h f", f=4)[:, :, 2:4]

            def gsl(sec):
                return bass.AP(gbp, sec * VHL, [[2 * GBW, 128], [GBW, 2], [1, VHL]])
            vbet = P.op("vector", lambda e: e.tensor_copy(spsc[:, 0:NVH], bsel))
            abeta = P.op("scalar", lambda e: e.activation(gsl(0), spsc[:], AF.Sigmoid),
                         waits=[("eV", vbet), gbst.get(i - 1, Z)])
            aab = P.op("scalar", lambda e: e.activation(spb[:], spsc[:], AF.Abs),
                       waits=[("eV", vcl)])
            ae1 = P.op("scalar", lambda e: e.activation(spb[:], spb[:], AF.Exp, scale=-1.0))
            ae2 = P.op("scalar", lambda e: e.activation(spb[:], spb[:], AF.Ln, bias=1.0))
            vmx = P.op("vector", lambda e: e.tensor_scalar(
                spsc[:], spsc[:], -1.0, 0.0, OP.mult, OP.max), waits=[("eA", aab)])
            vlnb = P.op("vector", lambda e: e.scalar_tensor_tensor(
                gsl(2), spsc[:], -1.0, spb[:], OP.mult, OP.subtract),
                waits=[("eA", ae2)])
            vas = P.op("vector", lambda e: e.tensor_tensor(spsc[:], asel, dtbb[:], OP.add))
            aab2 = P.op("scalar", lambda e: e.activation(spb[:], spsc[:], AF.Abs),
                        waits=[("eV", vas)])
            ae3 = P.op("scalar", lambda e: e.activation(spb[:], spb[:], AF.Exp, scale=-1.0))
            ae4 = P.op("scalar", lambda e: e.activation(spb[:], spb[:], AF.Ln, bias=1.0))
            vmx2 = P.op("vector", lambda e: e.tensor_scalar(
                spsc[:], spsc[:], 0.0, None, OP.max, OP.bypass), waits=[("eA", aab2)])
            vsp = P.op("vector", lambda e: e.tensor_tensor(
                spsc[:], spsc[:], spb[:], OP.add), waits=[("eA", ae4)])
            vg = P.op("vector", lambda e: e.tensor_tensor(gsl(1), spsc[:], negeA[:], OP.mult))
            gbst[i] = P.dma("scalar", lambda e, i=i: e.dma_start(
                out=dr["cgb"].ap()[i * 128:(i + 1) * 128, :], in_=gbp[:]),
                waits=[("eV", vg)])
            if dbg:
                logit_dump[i] = P.dma("gpsimd", lambda e, i=i: e.dma_start(
                    out=db["d_logit"].ap()[i * 128:(i + 1) * 128, 0:E], in_=btmp[:, 0:E]),
                    waits=[("eV", vcl)])
            m1 = P.op("vector", lambda e: e.reduce_max(btmp[:, 5 * E:5 * E + 1], btmp[:, 0:E], AX.X),
                      waits=[logit_dump.get(i - 1, Z)] if dbg else ())
            e1v = P.op("vector", lambda e: e.tensor_scalar(
                btmp[:, 2 * E:3 * E], btmp[:, 0:E], btmp[:, 5 * E:5 * E + 1], None, OP.is_ge, OP.bypass))
            msk = P.op("vector", lambda e: e.scalar_tensor_tensor(
                btmp[:, 3 * E:4 * E], btmp[:, 2 * E:3 * E], NEG, btmp[:, 0:E], OP.mult, OP.add))
            m2 = P.op("vector", lambda e: e.reduce_max(btmp[:, 5 * E + 1:5 * E + 2], btmp[:, 3 * E:4 * E], AX.X))
            e2v = P.op("vector", lambda e: e.tensor_scalar(
                btmp[:, 4 * E:5 * E], btmp[:, 3 * E:4 * E], btmp[:, 5 * E + 1:5 * E + 2], None,
                OP.is_ge, OP.bypass))
            d12 = P.op("vector", lambda e: e.tensor_tensor(
                btmp[:, 5 * E + 2:5 * E + 3], btmp[:, 5 * E:5 * E + 1], btmp[:, 5 * E + 1:5 * E + 2], OP.subtract))
            w1 = P.op("scalar", lambda e: e.activation(
                btmp[:, 5 * E + 3:5 * E + 4], btmp[:, 5 * E + 2:5 * E + 3], AF.Sigmoid, scale=rfc[:, 0:1]),
                waits=[("eV", d12), rfl])
            rfl_used = w1
            w2 = P.op("vector", lambda e: e.tensor_scalar(
                btmp[:, 5 * E + 4:5 * E + 5], btmp[:, 5 * E + 3:5 * E + 4], -1.0, 1.0, OP.mult, OP.add),
                waits=[("eA", w1)])
            cb1 = P.op("vector", lambda e: e.tensor_scalar(
                btmp[:, 2 * E:3 * E], btmp[:, 2 * E:3 * E], btmp[:, 5 * E + 3:5 * E + 4], None,
                OP.mult, OP.bypass))
            cb2 = P.op("vector", lambda e: e.scalar_tensor_tensor(
                btmp[:, 4 * E:5 * E], btmp[:, 4 * E:5 * E], btmp[:, 5 * E + 4:5 * E + 5],
                btmp[:, 2 * E:3 * E], OP.mult, OP.add))
            cpes = P.op("vector", lambda e: e.tensor_tensor(
                ctile[:, 0:E], btmp[:, 4 * E:5 * E], pesb[:], OP.mult))
            ctr = P.op("tensor", lambda e: e.transpose(
                psH[0:E, PSB + 256:PSB + 384], ctile[:, 0:E], cI[:]),
                waits=[("eV", cpes), ("eV", prev_ct_ev)])
            prev_ct_ev = P.op("vector", lambda e, i=i: e.tensor_copy(
                cfm[0:E, i * 128:(i + 1) * 128], psH[0:E, PSB + 256:PSB + 384]),
                waits=[("eT", ctr)])
        if dbg:
            P.dma("gpsimd", lambda e: e.dma_start(out=db["d_cgb"].ap(), in_=dr["cgb"].ap()),
                  waits=P.qcov("scalar"))
            P.dma("gpsimd", lambda e: e.dma_start(out=db["d_cfm"].ap()[0:E, :], in_=cfm[0:E, :]),
                  waits=[("eV", prev_ct_ev)])

        TPq = tile_plan(Tloc + 3)
        TPz = tile_plan(Tloc)
        L2G = tile_plan(Tloc, 1024)
        nfc = 6 * NKH
        slot_gate = [Z, Z, Z, Z]
        mm_done = {}
        cw_used = {}
        last_store = Z
        cacc_free = ("eV", 0)
        sqb_free = ("eT", 0)
        rrow_free = ("eT", 0)
        psH_l2_free = [Z, Z]

        def secrows(kh, sec):
            rank, khl = kh // NKHL, kh % NKHL
            if sec == 0:
                return rank * SECL + khl * 128
            if sec == 1:
                return rank * SECL + KW + khl * 128
            if sec in (2, 3):
                return rank * SECL + 2 * KW + (2 * khl + sec - 2) * 128
            return rank * SECL + 2 * KW + VW + (2 * khl + sec - 4) * 128

        def convrow(kh, sec):
            return kh * 128 if sec == 0 else (KD + kh * 128 if sec == 1
                                              else 2 * KD + (2 * kh + sec - 2) * 128)

        wld = {}
        for fc in range(nfc):
            kh, sec = fc // 6, fc % 6
            col0 = kh * 6 * 128 + sec * 128
            wld[fc] = P.dma("gpsimd", lambda e, col0=col0, fc=fc: e.dma_start(
                out=wsb[fc % 2][:],
                in_=io["Wq"].ap()[:, col0:col0 + 128].rearrange("(k p) f -> p k f", p=128)),
                waits=[("eT", mm_done.get(fc - 2, 0))])
            if sec < 4:
                cwld = P.dma("scalar", lambda e, kh=kh, sec=sec, fc=fc: e.dma_start(
                    out=cwsb[fc % 2][:],
                    in_=io["convw"].ap()[convrow(kh, sec):convrow(kh, sec) + 128, :]),
                    waits=[("eV", cw_used.get(fc - 2, 0))])
            tiles = TPq if sec < 4 else TPz
            mml = 0
            last_ev = Z
            for ti, (t0_, tn) in enumerate(tiles):
                slot = ti % 4
                for k in range(KCH):
                    src_off = 13 + t0_ if sec < 4 else FPAD + t0_
                    w0 = ([wld[fc], xl_all, slot_gate[slot]]
                          if k == 0 else [])
                    mml = P.op("tensor", lambda e, k=k, slot=slot, so=src_off, tn=tn, fc=fc: e.matmul(
                        psF[:, slot * 512:slot * 512 + tn],
                        wsb[fc % 2][:, k * 128:(k + 1) * 128],
                        xnfm[k][:, so:so + tn], start=(k == 0), stop=(k == KCH - 1)),
                        waits=w0)
                if sec < 4:
                    ev = P.op("vector", lambda e, slot=slot, t0_=t0_, tn=tn: e.tensor_copy(
                        mixf[:, t0_:t0_ + tn], psF[:, slot * 512:slot * 512 + tn]),
                        waits=[("eT", mml), cacc_free if ti == 0 else Z,
                               sqb_free if ti == 0 else Z])
                    slot_gate[slot] = ("eV", ev)
                else:
                    ev = P.op("scalar", lambda e, slot=slot, t0_=t0_, tn=tn: e.activation(
                        outb[:, t0_:t0_ + tn], psF[:, slot * 512:slot * 512 + tn], AF.Silu),
                        waits=[("eT", mml), last_store if ti == 0 else Z])
                    slot_gate[slot] = ("eA", ev)
                last_ev = slot_gate[slot]
            mm_done[fc] = mml
            if sec < 4:
                c1 = P.op("vector", lambda e, fc=fc: e.tensor_scalar(
                    cacc[:], mixf[:, 0:Tloc], cwsb[fc % 2][:, 0:1], None, OP.mult, OP.bypass),
                    waits=[cwld, last_ev, cacc_free])
                for j in range(1, 4):
                    c1 = P.op("vector", lambda e, j=j, fc=fc: e.scalar_tensor_tensor(
                        cacc[:], mixf[:, j:j + Tloc], cwsb[fc % 2][:, j:j + 1], cacc[:],
                        OP.mult, OP.add))
                cw_used[fc] = c1
                if sec < 2:
                    asil = P.op("scalar", lambda e: e.activation(cacc[:], cacc[:], AF.Silu),
                                waits=[("eV", c1)])
                    asq = P.op("scalar", lambda e: e.activation(mixf[:, 0:Tloc], cacc[:], AF.Square),
                               waits=[sqb_free, ("eV", c1)])
                    qsc = float(128) ** -0.5 if sec == 0 else 1.0
                    vout = 0
                    for gi, (g0_, gn) in enumerate(L2G):
                        nsub = max(1, gn // 512)
                        subw = min(512, gn)
                        rowl = 0
                        for si in range(nsub):
                            rowl = P.op("tensor", lambda e, g0_=g0_, si=si, subw=subw: e.matmul(
                                psH[0:1, si * 512:si * 512 + subw], c1cb[:],
                                mixf[:, g0_ + si * 512:g0_ + si * 512 + subw],
                                start=True, stop=True),
                                waits=([("eA", asq), psH_l2_free[0], psH_l2_free[1], rrow_free]
                                       if si == 0 else []))
                        arow = P.op("scalar", lambda e, gn=gn: e.activation(
                            rrow[0:1, 0:gn], psH[0:1, 0:gn], AF.Sqrt, bias=1e-6),
                            waits=[("eT", rowl)])
                        vrec = P.op("vector", lambda e, gn=gn: e.reciprocal(
                            rrow[0:1, 0:gn], rrow[0:1, 0:gn]), waits=[("eA", arow)])
                        if qsc != 1.0:
                            vrec = P.op("vector", lambda e, gn=gn, qsc=qsc: e.tensor_scalar(
                                rrow[0:1, 0:gn], rrow[0:1, 0:gn], qsc, None, OP.mult, OP.bypass))
                        bcl = 0
                        for si in range(nsub):
                            bcl = P.op("tensor", lambda e, si=si, subw=subw: e.matmul(
                                psH[:, si * 512:si * 512 + subw], c1r[0:1, :],
                                rrow[0:1, si * 512:si * 512 + subw], start=True, stop=True),
                                waits=[("eV", vrec)] if si == 0 else [])
                        rrow_free = ("eT", bcl)
                        vout = P.op("vector", lambda e, g0_=g0_, gn=gn: e.tensor_tensor(
                            outb[:, g0_:g0_ + gn], cacc[:, g0_:g0_ + gn], psH[:, 0:gn], OP.mult),
                            waits=[("eT", bcl), last_store if gi == 0 else Z])
                        psH_l2_free[gi % 2] = ("eV", vout)
                    st_gate = ("eV", vout)
                    cacc_free = ("eV", vout)
                    sqb_free = ("eT", rowl)
                else:
                    dsil = P.op("scalar", lambda e: e.activation(outb[:], cacc[:], AF.Silu),
                                waits=[("eV", c1), last_store])
                    st_gate = ("eA", dsil)
                    cacc_free = ("eA", dsil)
            else:
                st_gate = last_ev
            rb = secrows(kh, sec)
            last_store = P.dma("scalar", lambda e, rb=rb: e.dma_start(
                out=dr["contrib"].ap()[:, rb:rb + 128, :].transpose([1, 0, 2]),
                in_=outb[:].rearrange("p (c t) -> p c t", c=nchl)), waits=[st_gate])
        contrib_done = last_store
        gb_done = gbst[nchl - 1]
        if dbg:
            P.dma("gpsimd", lambda e: e.dma_start(out=db["d_contrib"].ap(), in_=dr["contrib"].ap()),
                  waits=P.qcov("scalar"))

        if cfg["phases"] <= 1:
            P.kill()
        ccA = P.cc(lambda e: e.collective_compute(
            "AllGather", OP.bypass, replica_groups=[[2 * i, 2 * i + 1] for i in range(B)],
            ins=[dr["contrib"].ap().opt()], outs=[dr["gcb"].ap().opt()]),
            waits=P.qcov("scalar"))
        ccB = P.cc(lambda e: e.collective_compute(
            "AllGather", OP.bypass, replica_groups=[[2 * i, 2 * i + 1] for i in range(B)],
            ins=[dr["cgb"].ap().opt()], outs=[dr["ggb"].ap().opt()]),
            waits=P.qcov("scalar"))

        if cfg["phases"] <= 2:
            P.kill()
        P.barrier()
        close_phase()
        sb = newphase()
        xnt = [sb(f"f_xnt{k}", [128, TB], BF16) for k in range(KCH)]
        gup = [sb(f"f_gup{j}", [128, KCH * 256], BF16) for j in range(2)]
        DNH = min(256, D)
        dnb = [sb(f"f_dnb{j}", [128, EDCH * DNH], BF16) for j in range(2)]
        actb = sb("f_act", [128, EDCH * TB], BF16)
        acc = sb("f_acc", [128, (TB // 128) * D], F32)
        ggb_ = sb("f_gg", [128, 512], BF16)
        gg2 = sb("f_gg2", [128, 512], BF16)
        crw = sb("f_crw", [1, TB], F32)
        cbc = sb("f_cbc", [128, TB], BF16)
        GH = D // 2
        gtmp = sb("f_gtmp", [128, GH], F32)
        ndh = D // DNH
        TT512 = tile_plan(TB)
        gate_slot = [Z, Z, Z, Z]
        dn_slot = [Z, Z]
        gu_mm_done = {"last": 0}
        gup_free = [0, 0]
        dn_mm_done = [0, 0]
        cbl_prev = 0
        acc_free = 0
        vg2_prev = 0
        act_dump_gate = Z

        gst_last = Z
        for tb in range(ntb):
            xwait = [pxst, ("eT", gu_mm_done["last"])]
            xld = [P.dma("sync", lambda e, k=k, tb=tb: e.dma_start(
                out=xnt[k][:], in_=dr["xn_cc"].ap()[k][FPAD + tb * TB:FPAD + (tb + 1) * TB, :],
                transpose=True), waits=xwait) for k in range(KCH)]
            xld_all = P.qcov("sync")
            if cfg["phases"] <= 2.05:
                P.kill()
            for e_ in range(E):
                crl = 0
                cbl = 0
                cm_gate = cbl_prev
                for si, (s0, sn) in enumerate(TT512):
                    cm1 = P.op("tensor", lambda e, e_=e_, tb=tb, s0=s0, sn=sn: e.matmul(
                        psH[0:1, 1536:1536 + sn], cIb[0:E, e_:e_ + 1],
                        cfm[0:E, tb * TB + s0:tb * TB + s0 + sn], start=True, stop=True),
                        waits=[("eV", cm_gate)] if si == 0 else [("eV", cbl)])
                    crl = P.op("vector", lambda e, s0=s0, sn=sn: e.tensor_copy(
                        crw[0:1, s0:s0 + sn], psH[0:1, 1536:1536 + sn]), waits=[("eT", cm1)])
                    cm2 = P.op("tensor", lambda e, s0=s0, sn=sn: e.matmul(
                        psH[:, 1536:1536 + sn], c1r[0:1, :], crw[0:1, s0:s0 + sn],
                        start=True, stop=True), waits=[("eV", crl)])
                    cbl = P.op("vector", lambda e, s0=s0, sn=sn: e.tensor_copy(
                        cbc[:, s0:s0 + sn], psH[:, 1536:1536 + sn]), waits=[("eT", cm2)])
                    if cfg["phases"] <= 2.1:
                        P.kill()
                cbl_prev = cbl
                if cfg["phases"] <= 2.15:
                    P.kill()
                for fp in range(EDCH):
                    gld = P.dma("gpsimd", lambda e, e_=e_, fp=fp, tb=tb: e.dma_start(
                        out=gup[fp % 2][:],
                        in_=io["gu"].ap()[e_][:, 2 * fp * 128:2 * fp * 128 + 256]
                        .rearrange("(k p) f -> p k f", p=128)),
                        waits=[("eT", gup_free[fp % 2])])
                    mlast = 0
                    for si, (s0, sn) in enumerate(TT512):
                        slA, slB = (si % 2) * 2, (si % 2) * 2 + 1
                        for k in range(KCH):
                            w0 = ([gld, xld_all, gate_slot[slA]]
                                  if k == 0 else [])
                            P.op("tensor", lambda e, k=k, slA=slA, s0=s0, sn=sn, fp=fp: e.matmul(
                                psF[:, slA * 512:slA * 512 + sn],
                                gup[fp % 2][:, k * 256:k * 256 + 128],
                                xnt[k][:, s0:s0 + sn], start=(k == 0), stop=(k == KCH - 1)),
                                waits=w0)
                        for k in range(KCH):
                            w0 = [gate_slot[slB]] if k == 0 else []
                            mlast = P.op("tensor", lambda e, k=k, slB=slB, s0=s0, sn=sn, fp=fp: e.matmul(
                                psF[:, slB * 512:slB * 512 + sn],
                                gup[fp % 2][:, k * 256 + 128:k * 256 + 256],
                                xnt[k][:, s0:s0 + sn], start=(k == 0), stop=(k == KCH - 1)),
                                waits=w0)
                        ag = P.op("scalar", lambda e, slA=slA, sn=sn: e.activation(
                            ggb_[:, 0:sn], psF[:, slA * 512:slA * 512 + sn], AF.Gelu),
                            waits=[("eT", mlast), ("eV", vg2_prev)])
                        vg2 = P.op("vector", lambda e, s0=s0, sn=sn: e.tensor_tensor(
                            gg2[:, 0:sn], ggb_[:, 0:sn], cbc[:, s0:s0 + sn], OP.mult),
                            waits=[("eA", ag), ("eV", cbl)])
                        vg2_prev = vg2
                        vact = P.op("vector", lambda e, slB=slB, fp=fp, s0=s0, sn=sn: e.tensor_tensor(
                            actb[:, fp * TB + s0:fp * TB + s0 + sn], gg2[:, 0:sn],
                            psF[:, slB * 512:slB * 512 + sn], OP.mult),
                            waits=[act_dump_gate])
                        gate_slot[slA] = ("eA", ag)
                        gate_slot[slB] = ("eV", vact)
                    gup_free[fp % 2] = mlast
                    gu_mm_done["last"] = mlast
                    if cfg["phases"] <= 2.3:
                        P.kill()
                act_done = P.cnt["eV"]
                if cfg["phases"] <= 2.4:
                    P.kill()
                if dbg and tb == 0 and e_ == 0:
                    act_dump_gate = P.dma("gpsimd",
                                          lambda e: e.dma_start(out=db["d_act"].ap(), in_=actb[:]),
                                          waits=[("eV", act_done)])
                DW = min(512, DNH)
                for dh in range(ndh):
                    dld = P.dma("gpsimd", lambda e, e_=e_, dh=dh: e.dma_start(
                        out=dnb[dh % 2][:],
                        in_=io["dn"].ap()[e_][:, dh * DNH:(dh + 1) * DNH]
                        .rearrange("(k p) f -> p k f", p=128)),
                        waits=[("eT", dn_mm_done[dh % 2])])
                    for tc in range(TB // 128):
                        for dt in range(DNH // DW):
                            sl = 2 + (tc * (DNH // DW) + dt) % 2
                            mm2 = 0
                            for k in range(EDCH):
                                w0 = ([dld, ("eV", act_done), dn_slot[sl - 2]]
                                      if k == 0 else [])
                                mm2 = P.op("tensor", lambda e, k=k, sl=sl, tc=tc, dt=dt, dh=dh, DW=DW: e.matmul(
                                    psH[:, (sl - 2) * 512:(sl - 2) * 512 + DW],
                                    actb[:, k * TB + tc * 128:k * TB + tc * 128 + 128],
                                    dnb[dh % 2][:, k * DNH + dt * DW:k * DNH + (dt + 1) * DW],
                                    start=(k == 0), stop=(k == EDCH - 1)), waits=w0)
                            col = tc * D + dh * DNH + dt * DW
                            if e_ == 0:
                                va = P.op("vector", lambda e, col=col, sl=sl, DW=DW: e.tensor_copy(
                                    acc[:, col:col + DW], psH[:, (sl - 2) * 512:(sl - 2) * 512 + DW]),
                                    waits=[("eT", mm2), ("eA", acc_free)])
                            else:
                                va = P.op("vector", lambda e, col=col, sl=sl, DW=DW: e.tensor_tensor(
                                    acc[:, col:col + DW], acc[:, col:col + DW],
                                    psH[:, (sl - 2) * 512:(sl - 2) * 512 + DW], OP.add),
                                    waits=[("eT", mm2)])
                            dn_slot[sl - 2] = ("eV", va)
                    dn_mm_done[dh % 2] = mm2
                    if cfg["phases"] <= 2.5:
                        P.kill()
            if dbg and tb == 0:
                P.dma("gpsimd", lambda e: e.dma_start(out=db["d_acc"].ap(), in_=acc[:]),
                      waits=[dn_slot[0], dn_slot[1]])
            for tc in range(TB // 128):
                for gh in range(D // GH):
                    ags = P.op("scalar", lambda e, tc=tc, gh=gh: e.activation(
                        gtmp[:], acc[:, tc * D + gh * GH:tc * D + (gh + 1) * GH], AF.Silu),
                        waits=[dn_slot[0], dn_slot[1], gst_last])
                    acc_free = ags
                    gst_last = P.dma("scalar", lambda e, tb=tb, tc=tc, gh=gh: e.dma_start(
                        out=dr["gmy"].ap()[tb * TB + tc * 128:tb * TB + (tc + 1) * 128,
                                           gh * GH:(gh + 1) * GH],
                        in_=gtmp[:]), waits=[("eA", ags)])
        gmy_done = P.qcov("scalar")
        if dbg:
            P.dma("gpsimd", lambda e: e.dma_start(out=db["d_gmy"].ap(), in_=dr["gmy"].ap()),
                  waits=P.qcov("scalar"))

        if cfg["phases"] <= 3:
            P.kill()
        P.barrier()
        close_phase()
        sb = newphase()
        qf2 = [sb(f"s_qf{j}", [128, KW], BF16) for j in range(2)]
        kf2 = [sb(f"s_kf{j}", [128, KW], BF16) for j in range(2)]
        ktm2 = [sb(f"s_ktm{j}", [128, KW], BF16) for j in range(2)]
        vtm2 = [sb(f"s_vtm{j}", [128, VW], BF16) for j in range(2)]
        zf2 = [sb(f"s_zf{j}", [128, VW], BF16) for j in range(2)]
        gbt2 = [sb(f"s_gbt{j}", [128, GBW], F32) for j in range(2)]
        lgv = sb("s_lgv", [128, VHL], F32)
        lgbv = sb("s_lgbv", [128, VHL], F32)
        lamE = sb("s_lamE", [128, VHL], F32)
        blamn = sb("s_blamn", [128, VHL], F32)
        escv = sb("s_escv", [128, VHL], F32)
        lamCv = sb("s_lamCv", [128, VHL], F32)
        trow = sb("s_trow", [1, VHL], F32)
        Blg = sb("s_Blg", [128, VW], F32)
        Blgb = sb("s_Blgb", [128, VW], F32)
        Blgp = sb("s_Blgp", [128, VW], F32)
        t1f = sb("s_t1f", [128, VW], F32)
        t2f = sb("s_t2f", [128, VW], F32)
        eMZ = sb("s_eMZ", [128, VW], F32)
        eMTZ = sb("s_eMTZ", [128, VW], F32)
        eDin = sb("s_eDin", [128, VW], F32)
        eLam = sb("s_eLam", [128, VW], F32)
        Mb = sb("s_Mb", [128, VW], BF16)
        MTb = sb("s_MTb", [128, VW], BF16)
        Ptb = sb("s_Ptb", [128, VW], BF16)
        Qlb = sb("s_Qlb", [128, VW], BF16)
        WTb = [sb(f"s_WT{j}", [128, VW], BF16) for j in range(2)]
        Pb = [sb(f"s_Pb{j}", [128, VW], BF16) for j in range(2)]
        PTb = [sb(f"s_PTb{j}", [128, VW], BF16) for j in range(2)]
        Rb = sb("s_Rb", [128, VW], BF16)
        ub = sb("s_ub", [128, VW], BF16)
        ogz = sb("s_ogz", [128, VW], F32)
        sqz = sb("s_sqz", [128, VW], BF16)
        ogb = sb("s_ogb", [128, VW], BF16)
        kbb = sb("s_kbb", [128, VW], BF16)
        rfr = sb("s_rfr", [1, VW], F32)
        regcell = {}

        def regdma(e, row, cg, out_ap, handle, dims):
            if "r" not in regcell:
                regcell["r"] = e.alloc_register(name="roff")
            r = regcell["r"]
            e.reg_load(r, offsb[0:1, row * 64 + cg:row * 64 + cg + 1])
            return e.dma_start(out=out_ap, in_=bass.AP(handle, r, dims))

        def bcv(t):
            return bass.AP(t, 0, [[VHL, 128], [1, VHL], [0, 128]])

        def tile128(t):
            return bass.AP(t, 0, [[128, 128], [0, VHL], [1, 128]])

        def repA(pst, base):
            return bass.AP(pst, base, [[2048, 128], [128, NKHL], [0, 2], [1, 128]])

        def repK(t):
            return bass.AP(t, 0, [[KW, 128], [128, NKHL], [0, 2], [1, 128]])

        G_ = min(4, VHL)
        H_ = VHL // G_

        def ilvcol(vh):
            return 512 * (vh % G_) + 128 * (vh // G_)

        def ilv(pst):
            return bass.AP(pst, 0, [[2048, 128], [128, H_], [512, G_], [1, 128]])

        if cfg.get("sk", 99) <= -2:
            P.kill()
        rs = []
        for bb in range(2):
            rs.append(P.dma("gpsimd", lambda e, bb=bb: regdma(
                e, 5, bb, dr["scan_loc"].ap()[bb * nchl:(bb + 1) * nchl], dr["gcb"],
                [[2 * SECL * 128, nchl], [128, SECL], [1, 128]]),
                waits=[("cc", ccA)]))
            rs.append(P.dma("gpsimd", lambda e, bb=bb: regdma(
                e, 6, bb, dr["gb_loc"].ap()[bb * Tloc:(bb + 1) * Tloc], dr["ggb"],
                [[2 * GBW, Tloc], [1, GBW]]),
                waits=[("cc", ccB)]))
        rs_done = P.qcov("gpsimd")
        if cfg.get("sk", 99) <= -1:
            P.kill()
        c_qf = [Z, Z]
        c_kf = [Z, Z]
        c_ktm = [Z, Z]
        c_vtm = [Z, Z]
        c_zf = [Z, Z]
        c_gbt = [Z, Z]
        prev_Sbf = 0
        prev_STog = Z
        prev_psF_cons = ("eV", 0)
        prev_psH_cons = ("eV", 0)
        gT1, gT2 = [Z, Z], [Z, Z]
        gLq, gLk, gLz, gLgb = [Z, Z], [Z, Z], [Z, Z], [Z, Z]
        ogst = {}
        for cg in range(nchg):
            j = cg % 2
            rw = rs_done if cg < 2 else []
            gLgb[j] = P.dma("scalar", lambda e, j=j, cg=cg: e.dma_start(
                out=gbt2[j][:], in_=dr["gb_loc"].ap()[cg * 128:(cg + 1) * 128, :]),
                waits=[c_gbt[j], rw])
            gLq[j] = P.dma("scalar", lambda e, j=j, cg=cg: e.dma_start(
                out=qf2[j][:].rearrange("p (h t) -> p h t", h=NKHL),
                in_=dr["scan_loc"].ap()[cg, 0:KW, :].rearrange("(h f) t -> f h t", f=128)),
                waits=[c_qf[j], rw])
            gLk[j] = P.dma("scalar", lambda e, j=j, cg=cg: e.dma_start(
                out=kf2[j][:].rearrange("p (h t) -> p h t", h=NKHL),
                in_=dr["scan_loc"].ap()[cg, KW:2 * KW, :].rearrange("(h f) t -> f h t", f=128)),
                waits=[c_kf[j], rw])
            gLz[j] = P.dma("scalar", lambda e, j=j, cg=cg: e.dma_start(
                out=zf2[j][:].rearrange("p (h t) -> p h t", h=VHL),
                in_=dr["scan_loc"].ap()[cg, 2 * KW + VW:SECL, :].rearrange("(h f) t -> f h t", f=128)),
                waits=[c_zf[j], rw])
            if cfg.get("sk", 99) <= -0.5:
                P.kill()
            gT1[j] = P.dma("sync", lambda e, j=j, cg=cg: e.dma_start(
                out=ktm2[j][:], in_=dr["scan_loc"].ap()[cg, KW:2 * KW, :], transpose=True),
                waits=[c_ktm[j], rw])
            gT2[j] = P.dma("sync", lambda e, j=j, cg=cg: e.dma_start(
                out=vtm2[j][:], in_=dr["scan_loc"].ap()[cg, 2 * KW:2 * KW + VW, :], transpose=True),
                waits=[c_vtm[j], rw])

            if cfg.get("sk", 99) <= 0:
                P.kill()
            M1 = P.op("tensor", lambda e, j=j: e.matmul(
                psH[:, 0:VHL], ctri[:], gbt2[j][:, VHL:2 * VHL], start=True, stop=True),
                waits=[gLgb[j], prev_psH_cons])
            M2 = P.op("tensor", lambda e, j=j: e.matmul(
                psH[0:1, 256:256 + VHL], c1cf[:], gbt2[j][:, VHL:2 * VHL],
                start=True, stop=True))
            V1 = P.op("vector", lambda e: e.tensor_copy(lgv[:], psH[:, 0:VHL]),
                      waits=[("eT", M2)])
            V2 = P.op("vector", lambda e: e.tensor_copy(trow[:], psH[0:1, 256:256 + VHL]))
            M3 = P.op("tensor", lambda e: e.matmul(
                psH[:, 512:512 + VHL], c1r[0:1, :], trow[0:1, :], start=True, stop=True),
                waits=[("eV", V2)])
            V3 = P.op("vector", lambda e, j=j: e.tensor_tensor(
                lgbv[:], lgv[:], gbt2[j][:, 2 * VHL:3 * VHL], OP.add))
            A1 = P.op("scalar", lambda e: e.activation(lamCv[:], psH[:, 512:512 + VHL], AF.Exp),
                      waits=[("eT", M3)])
            V4 = P.op("vector", lambda e: e.tensor_tensor(
                t1f[:, 0:VHL], lgv[:], psH[:, 512:512 + VHL], OP.subtract),
                waits=[("eT", M3)])
            A2 = P.op("scalar", lambda e: e.activation(escv[:], t1f[:, 0:VHL], AF.Exp,
                                                       scale=-1.0),
                      waits=[("eV", V4)])
            A3 = P.op("scalar", lambda e: e.activation(lamE[:], lgv[:], AF.Exp),
                      waits=[("eV", V3)])
            V5 = P.op("vector", lambda e, j=j: e.tensor_tensor(
                blamn[:], gbt2[j][:, 0:VHL], lamE[:], OP.mult), waits=[("eA", A3)])
            V6 = P.op("vector", lambda e: e.tensor_scalar(
                blamn[:], blamn[:], -1.0, None, OP.mult, OP.bypass))

            if cfg.get("sk", 99) <= 1:
                P.kill()
            V7 = P.op("vector", lambda e: e.tensor_copy(Blg[:], bcv(lgv)))
            V8 = P.op("vector", lambda e: e.tensor_tensor(
                Blgb[:], bcv(lgbv), tile128(cmM), OP.add))
            if cfg.get("sk", 99) <= 1.2:
                P.kill()
            trl = 0
            for vh in range(VHL):
                trl = P.op("tensor", lambda e, vh=vh: e.transpose(
                    psF[:, ilvcol(vh):ilvcol(vh) + 128], Blg[:, vh * 128:(vh + 1) * 128], cI[:]),
                    waits=[("eV", V7), prev_psF_cons] if vh == 0 else [])
            if cfg.get("sk", 99) <= 1.3:
                P.kill()
            trl2 = 0
            for vh in range(VHL):
                trl2 = P.op("tensor", lambda e, vh=vh: e.transpose(
                    psH[:, ilvcol(vh):ilvcol(vh) + 128], Blgb[:, vh * 128:(vh + 1) * 128], cI[:]),
                    waits=[("eV", V8), ("eV", V4), ("eA", A1), ("eV", V1)] if vh == 0 else [])
            if cfg.get("sk", 99) <= 1.4:
                P.kill()
            V9 = P.op("vector", lambda e: e.tensor_tensor(
                t1f[:], Blgb[:], ilv(psF), OP.subtract), waits=[("eT", trl)])
            A4 = P.op("scalar", lambda e: e.activation(eMZ[:], t1f[:], AF.Exp),
                      waits=[("eV", V9), ("eA", A2)])
            if cfg.get("sk", 99) <= 1.5:
                P.kill()
            V10 = P.op("vector", lambda e: e.tensor_tensor(
                t2f[:], Blg[:], ilv(psH), OP.subtract), waits=[("eT", trl2)])
            A5 = P.op("scalar", lambda e: e.activation(eMTZ[:], t2f[:], AF.Exp, scale=-1.0),
                      waits=[("eV", V10)])
            if cfg.get("sk", 99) <= 1.6:
                P.kill()
            A6 = P.op("scalar", lambda e: e.activation(eLam[:], ilv(psF), AF.Exp),
                      waits=[("eT", trl)])
            if cfg.get("sk", 99) <= 1.7:
                P.kill()
            V11 = P.op("vector", lambda e: e.tensor_tensor(
                Blgp[:], Blg[:], tile128(cmPT), OP.add))
            if cfg.get("sk", 99) <= 1.8:
                P.kill()
            V12 = P.op("vector", lambda e: e.tensor_tensor(
                t1f[:], Blgp[:], ilv(psF), OP.subtract),
                waits=[("eA", A6)])
            if cfg.get("sk", 99) <= 1.9:
                P.kill()
            A7 = P.op("scalar", lambda e: e.activation(eDin[:], t1f[:], AF.Exp, scale=-1.0),
                      waits=[("eV", V12)])
            if cfg.get("sk", 99) <= 2:
                P.kill()
            agl = 0
            for kl in range(NKHL):
                agl = P.op("tensor", lambda e, kl=kl, j=j: e.matmul(
                    psF[:, kl * 128:(kl + 1) * 128], kf2[j][:, kl * 128:(kl + 1) * 128],
                    kf2[j][:, kl * 128:(kl + 1) * 128], start=True, stop=True),
                    waits=[("eA", A6), ("eV", V12), gLk[j], gLq[j]]
                    if kl == 0 else [])
            for kl in range(NKHL):
                agl = P.op("tensor", lambda e, kl=kl, j=j: e.matmul(
                    psF[:, KW + kl * 128:KW + (kl + 1) * 128],
                    kf2[j][:, kl * 128:(kl + 1) * 128],
                    qf2[j][:, kl * 128:(kl + 1) * 128], start=True, stop=True))
            V13 = P.op("vector", lambda e: e.tensor_tensor(
                Mb[:], eMZ[:], repA(psF, 0), OP.mult), waits=[("eT", agl), ("eA", A5)])
            V14 = P.op("vector", lambda e: e.tensor_tensor(
                MTb[:], eMTZ[:], repA(psF, 0), OP.mult))
            V15 = P.op("vector", lambda e: e.tensor_tensor(
                Ptb[:], eDin[:], repA(psF, KW), OP.mult), waits=[("eA", A7)])
            V16 = P.op("vector", lambda e, j=j: e.tensor_tensor(
                Qlb[:], repK(qf2[j]), eLam[:], OP.mult), waits=[("eA", A6)])
            V17 = P.op("vector", lambda e: e.scalar_tensor_tensor(
                WTb[0][:], MTb[:], -1.0, tile128(cI), OP.mult, OP.add))
            c_qf[j] = ("eV", V16)
            if cfg.get("sk", 99) <= 3:
                P.kill()
            cur = 0
            lastPTev = V15
            lastW = V17
            lastPn = 0
            psHfree = ("eV", V10)
            for lev in range(1, NLEV + 1):
                nxt = lev % 2
                Ps = Mb if lev == 1 else Pb[cur]
                PTs = MTb if lev == 1 else PTb[cur]
                ml = 0
                for vh in range(VHL):
                    ml = P.op("tensor", lambda e, vh=vh, Ps=Ps, PTs=PTs: e.matmul(
                        psH[:, vh * 128:(vh + 1) * 128], PTs[:, vh * 128:(vh + 1) * 128],
                        Ps[:, vh * 128:(vh + 1) * 128], start=True, stop=True),
                        waits=[psHfree, ("eV", V14 if lev == 1 else lastW)] if vh == 0 else [])
                vPn = P.op("vector", lambda e, nxt=nxt: e.tensor_copy(Pb[nxt][:], psH[:, 0:VW]),
                           waits=[("eT", ml)])
                if lev < NLEV:
                    ml2 = 0
                    for vh in range(VHL):
                        ml2 = P.op("tensor", lambda e, vh=vh, Ps=Ps, PTs=PTs: e.matmul(
                            psF[:, vh * 128:(vh + 1) * 128], Ps[:, vh * 128:(vh + 1) * 128],
                            PTs[:, vh * 128:(vh + 1) * 128], start=True, stop=True),
                            waits=[("eV", V15 if lev == 1 else lastPTev), ("eV", V16),
                                   ("eA", A6)] if vh == 0 else [])
                    lastPTev_ = P.op("vector", lambda e, nxt=nxt: e.tensor_copy(
                        PTb[nxt][:], psF[:, 0:VW]), waits=[("eT", ml2)])
                    lastPTev = lastPTev_
                wl = 0
                for vh in range(VHL):
                    wl = P.op("tensor", lambda e, vh=vh, nxt=nxt, cur=cur: e.matmul(
                        psH[:, vh * 128:(vh + 1) * 128], Pb[nxt][:, vh * 128:(vh + 1) * 128],
                        WTb[cur][:, vh * 128:(vh + 1) * 128], start=True, stop=True),
                        waits=[("eV", vPn)] if vh == 0 else [])
                lastW = P.op("vector", lambda e, nxt=nxt, cur=cur: e.tensor_tensor(
                    WTb[nxt][:], WTb[cur][:], psH[:, 0:VW], OP.add), waits=[("eT", wl)])
                psHfree = ("eV", lastW)
                cur = nxt
            WTf = WTb[cur]
            if cfg.get("sk", 99) <= 4:
                P.kill()
            ksl = 0
            for vh in range(VHL):
                kl = vh // 2
                ksl = P.op("tensor", lambda e, vh=vh, kl=kl, j=j: e.matmul(
                    psF[:, vh * 128:(vh + 1) * 128], kf2[j][:, kl * 128:(kl + 1) * 128],
                    Sbf[:, vh * 128:(vh + 1) * 128], start=True, stop=True),
                    waits=[("eV", lastPTev), ("eV", prev_Sbf), ("eV", V16)] if vh == 0 else [])
            V18 = P.op("vector", lambda e, j=j: e.tensor_tensor(
                t1f[:], vtm2[j][:], bcv(bass_col(gbt2[j], 0, VHL)) if False else
                bass.AP(gbt2[j], 0, [[GBW, 128], [1, VHL], [0, 128]]), OP.mult),
                waits=[gT2[j], ("eA", A7)])
            c_vtm[j] = ("eV", V18)
            c_gbt[j] = ("eV", V18)
            V19 = P.op("vector", lambda e: e.tensor_tensor(
                t2f[:], bcv(blamn), psF[:, 0:VW], OP.mult),
                waits=[("eT", ksl), ("eA", A7)])
            V20 = P.op("vector", lambda e: e.tensor_tensor(Rb[:], t1f[:], t2f[:], OP.add))
            ul = 0
            for vh in range(VHL):
                ul = P.op("tensor", lambda e, vh=vh, WTf=WTf: e.matmul(
                    psH[:, vh * 128:(vh + 1) * 128], WTf[:, vh * 128:(vh + 1) * 128],
                    Rb[:, vh * 128:(vh + 1) * 128], start=True, stop=True),
                    waits=[("eV", V20), psHfree] if vh == 0 else [])
            V21 = P.op("vector", lambda e: e.tensor_copy(ub[:], psH[:, 0:VW]), waits=[("eT", ul)])
            ol = 0
            for vh in range(VHL):
                P.op("tensor", lambda e, vh=vh: e.matmul(
                    psF[:, vh * 128:(vh + 1) * 128], ub[:, vh * 128:(vh + 1) * 128],
                    Ptb[:, vh * 128:(vh + 1) * 128], start=True, stop=False),
                    waits=[("eV", V21), ("eV", V19)] if vh == 0 else [])
                ol = P.op("tensor", lambda e, vh=vh: e.matmul(
                    psF[:, vh * 128:(vh + 1) * 128], Sbf[:, vh * 128:(vh + 1) * 128],
                    Qlb[:, vh * 128:(vh + 1) * 128], start=False, stop=True))
            if cfg.get("sk", 99) <= 5:
                P.kill()
            V22 = P.op("vector", lambda e, j=j: e.tensor_tensor(
                ogz[:], zf2[j][:], psF[:, 0:VW], OP.mult), waits=[("eT", ol), gLz[j]])
            c_zf[j] = ("eV", V22)
            A8 = P.op("scalar", lambda e: e.activation(sqz[:], ogz[:], AF.Square),
                      waits=[("eV", V22)])
            rl = 0
            for vh in range(VHL):
                rl = P.op("tensor", lambda e, vh=vh: e.matmul(
                    psH[0:1, vh * 128:(vh + 1) * 128], c1cb[:],
                    sqz[:, vh * 128:(vh + 1) * 128], start=True, stop=True),
                    waits=[("eA", A8), ("eV", V21)] if vh == 0 else [])
            A9 = P.op("scalar", lambda e: e.activation(
                rfr[0:1, :], psH[0:1, 0:VW], AF.Ln, scale=1.0 / 128, bias=1e-6),
                waits=[("eT", rl)])
            A10 = P.op("scalar", lambda e: e.activation(
                rfr[0:1, :], rfr[0:1, :], AF.Exp, scale=-0.5))
            bl = 0
            for vh in range(VHL):
                bl = P.op("tensor", lambda e, vh=vh: e.matmul(
                    psH[:, vh * 128:(vh + 1) * 128], c1r[0:1, :],
                    rfr[0:1, vh * 128:(vh + 1) * 128], start=True, stop=True),
                    waits=[("eA", A10)] if vh == 0 else [])
            V23a = P.op("vector", lambda e: e.tensor_tensor(
                t1f[:], ogz[:], psH[:, 0:VW], OP.mult), waits=[("eT", bl)])
            V23 = P.op("vector", lambda e: e.tensor_scalar(
                ogb[:], t1f[:], nwc[:, 0:1], None, OP.mult, OP.bypass),
                waits=[prev_STog])
            prev_STog = P.dma("scalar", lambda e, cg=cg: e.dma_start(
                out=dr["og"].ap()[:, cg * 128:(cg + 1) * 128]
                .rearrange("(vh f) t -> f vh t", vh=VHL), in_=ogb[:]
                .rearrange("p (vh t) -> p vh t", vh=VHL)), waits=[("eV", V23)])
            ogst[cg] = prev_STog
            if cfg.get("sk", 99) <= 6:
                P.kill()
            V24 = P.op("vector", lambda e, j=j: e.tensor_tensor(
                kbb[:], repK(ktm2[j]), bcv(escv), OP.mult),
                waits=[gT1[j], ("eA", A2)])
            c_ktm[j] = ("eV", V24)
            c_kf[j] = ("eT", ksl)
            sul = 0
            for vh in range(VHL):
                sul = P.op("tensor", lambda e, vh=vh: e.matmul(
                    psF[:, vh * 128:(vh + 1) * 128], kbb[:, vh * 128:(vh + 1) * 128],
                    ub[:, vh * 128:(vh + 1) * 128], start=True, stop=True),
                    waits=[("eV", V24), ("eV", V22)] if vh == 0 else [])
            V25 = P.op("vector", lambda e: e.tensor_tensor(
                t2f[:], Sf[:], bcv(lamCv), OP.mult), waits=[("eA", A1)])
            V26 = P.op("vector", lambda e: e.tensor_tensor(
                Sf[:], t2f[:], psF[:, 0:VW], OP.add), waits=[("eT", sul)])
            prev_Sbf = P.op("vector", lambda e: e.tensor_copy(Sbf[:], Sf[:]))
            prev_psF_cons = ("eV", V26)
            prev_psH_cons = ("eT", bl) if False else ("eV", V23)
        og_done = P.qcov("scalar")
        if dbg:
            P.dma("gpsimd", lambda e: e.dma_start(out=db["d_og"].ap(), in_=dr["og"].ap()),
                  waits=P.qcov("scalar"))

        if cfg["phases"] <= 4:
            P.kill()
        P.barrier()
        close_phase()
        sb = newphase()
        wout_sb = sb("w_wout", [128, VHL * D], BF16)
        ogt = [sb(f"w_ogt{j}", [128, VW], BF16) for j in range(2)]
        ssb = [sb(f"w_ssb{j}", [128, D], F32) for j in range(2)]
        wol = P.dma("gpsimd", lambda e: e.dma_start(
            out=wout_sb[:].rearrange("p (vh d) -> p vh d", vh=VHL),
            in_=io["Wout_my"].ap().rearrange("(vh f) d -> f vh d", f=128)))
        ps_of = [psF, psH]
        wev = [Z, Z]
        sst = [Z, Z]
        wst_last = Z
        for tc in range(2 * nchl):
            jj = tc % 2
            ogl = P.dma("sync", lambda e, tc=tc: e.dma_start(
                out=ogt[tc % 2][:].rearrange("p (vh t) -> p vh t", vh=VHL),
                in_=dr["og"].ap()[:, tc * 128:(tc + 1) * 128]
                .rearrange("(vh f) t -> f vh t", f=128)),
                waits=[og_done, wev[jj]])
            mml = 0
            pst = ps_of[jj]
            for dt in range(D // 512):
                for vh in range(VHL):
                    mml = P.op("tensor", lambda e, dt=dt, vh=vh, tc=tc, pst=pst: e.matmul(
                        pst[:, dt * 512:(dt + 1) * 512],
                        ogt[tc % 2][:, vh * 128:(vh + 1) * 128],
                        wout_sb[:, vh * D + dt * 512:vh * D + dt * 512 + 512],
                        start=(vh == 0), stop=(vh == VHL - 1)),
                        waits=[ogl, wol, wev[jj]] if (dt == 0 and vh == 0) else [])
            vev = P.op("vector", lambda e, jj=jj, pst=pst: e.tensor_copy(ssb[jj][:], pst[:, 0:D]),
                       waits=[("eT", mml), sst[jj]])
            wev[jj] = ("eV", vev)
            wst_last = P.dma("scalar", lambda e, tc=tc, jj=jj: e.dma_start(
                out=dr["sp"].ap()[tc * 128:(tc + 1) * 128, :], in_=ssb[jj][:]),
                waits=[("eV", vev)])
            sst[jj] = wst_last
        if dbg:
            P.dma("gpsimd", lambda e: e.dma_start(out=db["d_sp"].ap(), in_=dr["sp"].ap()),
                  waits=P.qcov("scalar"))
        ccC = P.cc(lambda e: e.collective_compute(
            "ReduceScatter", OP.add, replica_groups=[[2 * i, 2 * i + 1] for i in range(B)],
            ins=[dr["sp"].ap().opt()], outs=[dr["smy"].ap().opt()]),
            waits=P.qcov("scalar"))
        if dbg:
            P.dma("gpsimd", lambda e: e.dma_start(out=db["d_smy"].ap(), in_=dr["smy"].ap()),
                  waits=[("cc", ccC)])

        if cfg["phases"] <= 5:
            P.kill()
        P.barrier()
        close_phase()
        sb = newphase()
        fx = [sb(f"fi_x{j}", [128, D], F32) for j in range(2)]
        fs = [sb(f"fi_s{j}", [128, D], F32) for j in range(2)]
        fg = [sb(f"fi_g{j}", [128, D], F32) for j in range(2)]
        fo = [sb(f"fi_o{j}", [128, D], F32) for j in range(2)]
        fev = [Z, Z]
        fst = [Z, Z]
        for tc in range(Tloc // 128):
            jj = tc % 2
            lx = P.dma("sync", lambda e, tc=tc, jj=jj: e.dma_start(
                out=fx[jj][:], in_=io["x_slice"].ap()[3 + tc * 128:3 + (tc + 1) * 128, :]),
                waits=[fev[jj]])
            ls = P.dma("sync", lambda e, tc=tc, jj=jj: e.dma_start(
                out=fs[jj][:], in_=dr["smy"].ap()[tc * 128:(tc + 1) * 128, :]),
                waits=[("cc", ccC)])
            lg_ = P.dma("scalar", lambda e, tc=tc, jj=jj: e.dma_start(
                out=fg[jj][:], in_=dr["gmy"].ap()[tc * 128:(tc + 1) * 128, :]),
                waits=[gmy_done])
            v1_ = P.op("vector", lambda e, jj=jj: e.tensor_tensor(
                fo[jj][:], fs[jj][:], fg[jj][:], OP.mult),
                waits=[ls, lg_, fst[jj]])
            v2_ = P.op("vector", lambda e, jj=jj: e.tensor_tensor(
                fo[jj][:], fo[jj][:], fx[jj][:], OP.add), waits=[lx])
            fev[jj] = ("eV", v2_)
            fst[jj] = P.dma("scalar", lambda e, tc=tc, jj=jj: e.dma_start(
                out=io["out_my"].ap()[tc * 128:(tc + 1) * 128, :], in_=fo[jj][:]),
                waits=[("eV", v2_)])
        if getattr(P, "dead", False):
            P.dead = False
            P.dma("gpsimd", lambda e: e.dma_start(
                out=io["out_my"].ap(), in_=io["x_slice"].ap()[3:, :]))
        P.barrier()

        if cfg.get("dbgcnt"):
            print("PLAN cnt:", {k: v for k, v in P.cnt.items() if v},
                  "steps:", {e: len(s) for e, s in P.steps.items()})
        P.emit(nc, semh)
        close_phase()
    return nc


def bass_col(t, a, b):
    return t[:, a:b]


_CACHE = {}


def kernel(**inputs):
    cfg = mkcfg(mini=False)
    key = "full"
    if key not in _CACHE:
        _CACHE[key] = build(cfg)
    nc = _CACHE[key]
    maps = prep_inputs(inputs, cfg)
    res = run_bass_kernel_spmd(nc, maps, core_ids=list(range(cfg["ncores"])))
    B, Tloc, D = cfg["B"], cfg["Tloc"], cfg["D"]
    out = np.empty((B, 2 * Tloc, D), np.float32)
    for c in range(cfg["ncores"]):
        b, p = c // 2, c % 2
        out[b, p * Tloc:(p + 1) * Tloc] = np.asarray(res.results[c]["out_my"]).reshape(Tloc, D)
    return out



# version 04
# speedup vs baseline: 1.0152x; 1.0152x over previous
import contextlib
import numpy as np
import ml_dtypes
import concourse.bass as bass
import concourse.mybir as mybir
from concourse.bass_utils import run_bass_kernel_spmd

F32 = mybir.dt.float32
BF16 = mybir.dt.bfloat16
I32 = mybir.dt.int32
AF = mybir.ActivationFunctionType
OP = mybir.AluOpType
AX = mybir.AxisListType
BFNP = ml_dtypes.bfloat16
NEG = -1e30
FPAD = 16


def mkcfg(mini=False):
    c = {}
    if mini:
        c.update(B=1, Tloc=256, D=512, NKH=4, E=4, ED=256, dbg=True)
    else:
        c.update(B=4, Tloc=4096, D=2048, NKH=16, E=8, ED=2048, dbg=False)
    c["C"] = 128
    c["NKHL"] = c["NKH"] // 2
    c["VHL"] = c["NKHL"] * 2
    c["NVH"] = c["NKH"] * 2
    c["KD"] = c["NKH"] * 128
    c["VD"] = c["NVH"] * 128
    c["KCH"] = c["D"] // 128
    c["EDCH"] = c["ED"] // 128
    c["nchl"] = c["Tloc"] // c["C"]
    c["nchg"] = 2 * c["nchl"]
    c["Tpad"] = ((c["Tloc"] + FPAD + 127) // 128) * 128
    c["ncores"] = 2 * c["B"]
    c["SECL"] = c["KD"] + c["VD"]
    c["KW"] = c["NKHL"] * 128
    c["VW"] = c["VHL"] * 128
    c["TB"] = min(1024, c["Tloc"])
    c["ntb"] = c["Tloc"] // c["TB"]
    c["phases"] = 9
    return c


def _consts():
    tt = np.arange(128)[:, None]
    ss = np.arange(128)[None, :]
    return dict(
        I128f=np.eye(128, dtype=np.float32),
        Ibf=np.eye(128, dtype=np.float32).astype(BFNP),
        triUT=np.triu(np.ones((128, 128), np.float32)),
        maskM=np.where(ss < tt, 0.0, NEG).astype(np.float32),
        maskPTpos=np.where(ss >= tt, 0.0, -NEG).astype(np.float32),
        ones_r=np.ones((1, 128), np.float32),
        cvals=np.tile(np.array([[1e-5, 1e-6, -0.5, 1.0 / 128, 0.0, 0.0, 0.0, 0.0]], np.float32), (128, 1)),
        ones_cf=np.ones((128, 1), np.float32),
        ones_cb=np.ones((128, 1), np.float32).astype(BFNP),
    )


def prep_inputs(inputs, cfg):
    B, Tloc, D, NKH, E, ED = (cfg[k] for k in ("B", "Tloc", "D", "NKH", "E", "ED"))
    x = np.ascontiguousarray(inputs["L0_attn_hidden_states"], dtype=np.float32)
    gu = inputs["L0_ffn_gate_up"]
    gu_i = np.ascontiguousarray(
        gu.reshape(E, D, 2, ED // 128, 128).transpose(0, 1, 3, 2, 4)
        .reshape(E, D, 2 * ED)).astype(np.float32)
    cn = _consts()
    SECL, nchl, nchg = cfg["SECL"], cfg["nchl"], cfg["nchg"]
    KW, VW = cfg["KW"], cfg["VW"]
    Wo = np.ascontiguousarray(inputs["L0_attn_W_out"], dtype=np.float32)
    maps = []
    for c in range(cfg["ncores"]):
        b, p = c // 2, c % 2
        t0 = p * Tloc
        xs = np.zeros((Tloc + 3, D), np.float32)
        lo = max(0, t0 - 3)
        xs[3 - (t0 - lo):] = x[b, lo:t0 + Tloc]
        tab = np.zeros((8, 64), np.int64)
        for cg in range(nchg):
            blk, cl = cg // nchl, cg % nchl
            base = (blk * nchl + cl) * (2 * SECL * 128) + p * SECL * 128
            tab[0, cg] = base
            tab[1, cg] = (blk * Tloc + cl * 128) * (6 * cfg["VHL"]) + p * 3 * cfg["VHL"]
            tab[2, cg] = base + KW * 128
            tab[3, cg] = base + KW * 128
            tab[4, cg] = base + (2 * KW + VW) * 128
        for bb in range(2):
            tab[5, bb] = (bb * nchl * 2 + p) * SECL * 128
            tab[6, bb] = bb * Tloc * (6 * cfg["VHL"]) + p * 3 * cfg["VHL"]
        d = dict(
            x_slice=xs,
            Wq=np.ascontiguousarray(inputs["L0_attn_W_qkvz"], dtype=np.float32),
            Wb=np.ascontiguousarray(inputs["L0_attn_W_ba"], dtype=np.float32),
            convw=np.ascontiguousarray(inputs["L0_attn_conv_w"], dtype=np.float32),
            dtb=inputs["L0_attn_dt_bias"].reshape(1, -1).astype(np.float32),
            Alog=inputs["L0_attn_A_log"].reshape(1, -1).astype(np.float32),
            nw=inputs["L0_attn_norm_weight"].reshape(-1, 1).astype(np.float32),
            Wout_my=np.ascontiguousarray(Wo[p * (cfg["VD"] // 2):(p + 1) * (cfg["VD"] // 2)]),
            rsc=inputs["L0_ffn_router_scale"].reshape(-1, 1).astype(np.float32),
            gw=np.ascontiguousarray(inputs["L0_ffn_gate_w"], dtype=np.float32),
            gu=gu_i,
            dn=np.ascontiguousarray(inputs["L0_ffn_down"], dtype=np.float32),
            pes=inputs["L0_ffn_per_expert_scale"].reshape(1, -1).astype(np.float32),
            offs=tab.astype(np.int32),
        )
        d.update(cn)
        maps.append(d)
    return maps


ENGS = ("sync", "gpsimd", "tensor", "vector", "scalar")
QSEM = {"sync": "qS", "gpsimd": "qG", "scalar": "qC"}
ESEM = {"vector": "eV", "scalar": "eA", "tensor": "eT", "gpsimd": "eP"}
NQ = 4
SEMS = (["%s%d" % (q, i) for q in ("qS", "qG", "qC") for i in range(NQ)]
        + ["eV", "eA", "eT", "eP", "cc"])
Z = ("eV", 0)


class Plan:
    def __init__(self):
        self.steps = {e: [] for e in ENGS}
        self.cnt = {s: 0 for s in SEMS}
        self.qn = {}

    def kill(self):
        self.dead = True

    def _add(self, eng, sem, k, emit, waits):
        if getattr(self, "dead", False):
            return self.cnt[sem]
        flat = []
        for w in waits:
            if isinstance(w, list):
                flat.extend(w)
            else:
                flat.append(w)
        self.cnt[sem] += k
        self.steps[eng].append((emit, [w for w in flat if w[1] > 0], sem, k))
        return self.cnt[sem]

    def qcov(self, eng):
        return [("%s%d" % (QSEM[eng], i), self.cnt["%s%d" % (QSEM[eng], i)])
                for i in range(NQ)]

    def dma(self, eng, emit, waits=()):
        if getattr(self, "dead", False):
            return Z
        qn = self.qn.get(eng, 0)
        self.qn[eng] = qn + 1
        sem = "%s%d" % (QSEM[eng], qn % NQ)
        w = [(sem, self.cnt[sem])] + list(waits)
        return (sem, self._add(eng, sem, 16, emit, w))

    def op(self, eng, emit, waits=(), chain=True):
        sem = ESEM[eng]
        w = list(waits)
        if chain and eng in ("vector", "scalar") and self.cnt[sem] > 0:
            w.append((sem, self.cnt[sem]))
        return self._add(eng, sem, 1, emit, w)

    def cc(self, emit, waits=()):
        return self._add("gpsimd", "cc", 1, emit, waits)

    def barrier(self, exclude=()):
        tgt = [(s, v) for s, v in self.cnt.items() if v > 0 and s not in exclude]
        for e in ENGS:
            self.steps[e].append((None, tgt, None, 0))

    def emit(self, nc, semh):
        def runner(engname):
            def run(e):
                waited = {}
                for emitf, waits, sem, k in self.steps[engname]:
                    for snm, val in waits:
                        if waited.get(snm, 0) < val:
                            e.wait_ge(semh[snm], val)
                            waited[snm] = val
                    if emitf is not None:
                        emitf(e).then_inc(semh[sem], k)
            return run
        with nc.Block() as blk:
            blk.sync(runner("sync"))
            blk.gpsimd(runner("gpsimd"))
            blk.tensor(runner("tensor"))
            blk.vector(runner("vector"))
            blk.scalar(runner("scalar"))


def tile_plan(total, width=512):
    if total <= width:
        return [(0, total)]
    starts = [i * width for i in range(total // width)]
    if total % width:
        starts.append(total - width)
    return [(s, width) for s in starts]


def build(cfg):
    B, Tloc, D, NKH, E, ED, C = (cfg[k] for k in ("B", "Tloc", "D", "NKH", "E", "ED", "C"))
    KCH, EDCH, VHL, NKHL = cfg["KCH"], cfg["EDCH"], cfg["VHL"], cfg["NKHL"]
    nchl, nchg, Tpad, SECL = cfg["nchl"], cfg["nchg"], cfg["Tpad"], cfg["SECL"]
    NVH, KD, VD, KW, VW = cfg["NVH"], cfg["KD"], cfg["VD"], cfg["KW"], cfg["VW"]
    TB, ntb = cfg["TB"], cfg["ntb"]
    dbg = cfg["dbg"]
    WBA = 4 * NKH + E
    GBW = 3 * VHL
    NLEV = 6

    nc = bass.Bass("TRN2", target_bir_lowering=False)
    io = {}
    for nm, shp, dt in [
        ("x_slice", [Tloc + 3, D], F32), ("Wq", [D, 6 * NKH * 128], F32),
        ("Wb", [D, 4 * NKH], F32), ("convw", [4 * NKH * 128, 4], F32),
        ("dtb", [1, NVH], F32), ("Alog", [1, NVH], F32), ("nw", [128, 1], F32),
        ("Wout_my", [VD // 2, D], F32), ("rsc", [D, 1], F32), ("gw", [D, E], F32),
        ("gu", [E, D, 2 * ED], F32), ("dn", [E, ED, D], F32), ("pes", [1, E], F32),
        ("offs", [8, 64], I32), ("I128f", [128, 128], F32), ("Ibf", [128, 128], BF16),
        ("triUT", [128, 128], F32), ("maskM", [128, 128], F32),
        ("maskPTpos", [128, 128], F32), ("ones_r", [1, 128], F32),
        ("ones_cf", [128, 1], F32), ("ones_cb", [128, 1], BF16),
        ("cvals", [128, 8], F32),
    ]:
        io[nm] = nc.dram_tensor(nm, shp, dt, kind="ExternalInput")
    io["out_my"] = nc.dram_tensor("out_my", [Tloc, D], F32, kind="ExternalOutput")
    dr = {
        "xn_cc": nc.dram_tensor("xn_cc", [KCH, Tpad, 128], BF16),
        "rfac": nc.dram_tensor("rfac_rows", [Tpad, 1], F32),
        "contrib": nc.dram_tensor("contrib_cb", [nchl, 2 * SECL, 128], BF16),
        "gcb": nc.dram_tensor("gath_cb", [2, nchl, 2 * SECL, 128], BF16),
        "cgb": nc.dram_tensor("contrib_gb", [Tloc, 2 * GBW], F32),
        "ggb": nc.dram_tensor("gath_gb", [2, Tloc, 2 * GBW], F32),
        "scan_loc": nc.dram_tensor("scan_loc", [2 * nchl, SECL, 128], BF16),
        "gb_loc": nc.dram_tensor("gb_loc", [2 * Tloc, GBW], F32),
        "og": nc.dram_tensor("og_fm", [VW, 2 * Tloc], BF16),
        "sp": nc.dram_tensor("spart", [2 * Tloc, D], F32),
        "smy": nc.dram_tensor("s_my", [Tloc, D], F32),
        "gmy": nc.dram_tensor("g_my", [Tloc, D], F32),
    }
    db = {}
    if dbg:
        for nm, ref in [("d_xn", "xn_cc"), ("d_contrib", "contrib"), ("d_cgb", "cgb"),
                        ("d_og", "og"), ("d_sp", "sp"), ("d_smy", "smy"),
                        ("d_gmy", "gmy"), ("d_rfac", "rfac")]:
            t = dr[ref]
            db[nm] = nc.dram_tensor(nm, [int(z) for z in t.shape], t.dtype,
                                    kind="ExternalOutput")
        db["d_act"] = nc.dram_tensor("d_act", [128, cfg["EDCH"] * cfg["TB"]], BF16,
                                     kind="ExternalOutput")
        db["d_cfm"] = nc.dram_tensor("d_cfm", [8, Tloc], F32, kind="ExternalOutput")
        db["d_logit"] = nc.dram_tensor("d_logit", [Tloc, 8], F32, kind="ExternalOutput")
        db["d_acc"] = nc.dram_tensor("d_acc", [128, (cfg["TB"] // 128) * cfg["D"]], F32,
                                     kind="ExternalOutput")

    P = Plan()
    es = contextlib.ExitStack()
    phase_es = []

    def close_phase():
        while phase_es:
            phase_es.pop().close()

    def newphase():
        st = contextlib.ExitStack()
        phase_es.append(st)
        return lambda nm, shp, dt: st.enter_context(nc.sbuf_tensor(nm, shp, dt))

    with es:
        semh = {s: es.enter_context(nc.semaphore(s)) for s in SEMS}
        psF = es.enter_context(nc.psum_tensor("psF", [128, 2048], F32))
        psH = es.enter_context(nc.psum_tensor("psH", [128, 2048], F32))
        sbg = lambda nm, shp, dt: es.enter_context(nc.sbuf_tensor(nm, shp, dt))

        cI = sbg("cI", [128, 128], F32)
        cIb = sbg("cIb", [128, 128], BF16)
        ctri = sbg("ctri", [128, 128], F32)
        cmM = sbg("cmM", [128, 128], F32)
        cmPT = sbg("cmPT", [128, 128], F32)
        c1r = sbg("c1r", [1, 128], F32)
        c1cf = sbg("c1cf", [128, 1], F32)
        c1cb = sbg("c1cb", [128, 1], BF16)
        nwc = sbg("nwc", [128, 1], F32)
        dtbb = sbg("dtbb", [128, NVH], F32)
        negeA = sbg("negeA", [128, NVH], F32)
        pesb = sbg("pesb", [128, E], F32)
        cfm = sbg("cfm", [8, Tloc], BF16)
        offsb = sbg("offsb", [1, 512], I32)
        Sf = sbg("Sf", [128, VW], F32)
        Sbf = sbg("Sbf", [128, VW], BF16)
        cvsb = sbg("cvsb", [128, 8], F32)
        for ci_, cv_ in enumerate([1e-5, 1e-6, -0.5, 1.0 / 128]):
            nc.const_aps.register(F32, cv_, cvsb[:, ci_:ci_ + 1])

        sb = newphase()
        small = sb("small", [1, 2 * NVH + E], F32)
        for h, tnm in [(cI, "I128f"), (cIb, "Ibf"), (ctri, "triUT"), (cmM, "maskM"),
                       (cmPT, "maskPTpos"), (c1r, "ones_r"), (c1cf, "ones_cf"),
                       (c1cb, "ones_cb"), (nwc, "nw")]:
            P.dma("sync", lambda e, h=h, t=io[tnm]: e.dma_start(out=h[:], in_=t.ap()))
        P.dma("sync", lambda e: e.dma_start(out=small[0:1, 0:NVH], in_=io["dtb"].ap()))
        P.dma("sync", lambda e: e.dma_start(out=small[0:1, NVH:2 * NVH], in_=io["Alog"].ap()))
        l3 = P.dma("sync", lambda e: e.dma_start(out=small[0:1, 2 * NVH:], in_=io["pes"].ap()))
        P.dma("sync", lambda e: e.dma_start(
            out=offsb[0:1, 0:512].rearrange("o (r c) -> o r c", r=8),
            in_=io["offs"].ap()))
        P.dma("sync", lambda e: e.dma_start(out=cvsb[:], in_=io["cvals"].ap()))
        P.op("tensor", lambda e: e.matmul(psF[:, 0:NVH], c1r[0:1, :], small[0:1, 0:NVH],
                                          start=True, stop=True), waits=P.qcov("sync"))
        P.op("tensor", lambda e: e.matmul(psF[:, 256:256 + NVH], c1r[0:1, :],
                                          small[0:1, NVH:2 * NVH], start=True, stop=True))
        t3 = P.op("tensor", lambda e: e.matmul(psF[:, 512:512 + E], c1r[0:1, :],
                                               small[0:1, 2 * NVH:], start=True, stop=True))
        a1 = P.op("scalar", lambda e: e.activation(negeA[:], psF[:, 256:256 + NVH], AF.Exp),
                  waits=[("eT", t3)])
        P.op("vector", lambda e: e.tensor_copy(dtbb[:], psF[:, 0:NVH]), waits=[("eT", t3)])
        ini_pes = P.op("vector", lambda e: e.tensor_copy(pesb[:], psF[:, 512:512 + E]))
        P.op("vector", lambda e: e.tensor_scalar(negeA[:], negeA[:], -1.0, None,
                                                 OP.mult, OP.bypass), waits=[("eA", a1)])
        P.op("vector", lambda e: e.memset(Sf[:], 0.0))
        ini_done = P.op("vector", lambda e: e.memset(Sbf[:], 0.0))
        psF_free = ("eV", ini_pes)

        xa = sb("xa", [128, D], F32)
        xnb = sb("xnb", [128, D], BF16)
        stat = sb("stat", [128, 12], F32)
        sqt = sb("sqt", [128, D], F32)
        ntile = (Tloc + 3 + 127) // 128
        pv6 = pa1 = 0
        pxst = prst = Z
        vz1 = P.op("vector", lambda e: e.memset(xnb[:], 0.0))
        vz2 = P.op("vector", lambda e: e.memset(stat[:, 11:12], 0.0))
        npad_t = Tpad - 16 - Tloc
        for r0p, nrp in [(0, 13), (16 + Tloc, npad_t)]:
            if nrp <= 0:
                continue
            pxst = P.dma("gpsimd", lambda e, r0p=r0p, nrp=nrp: e.dma_start(
                out=dr["xn_cc"].ap()[:, r0p:r0p + nrp, :].transpose([1, 0, 2]),
                in_=xnb[0:nrp, :].rearrange("p (k f) -> p k f", k=KCH)),
                waits=[("eV", vz1)])
            prst = P.dma("gpsimd", lambda e, r0p=r0p, nrp=nrp: e.dma_start(
                out=dr["rfac"].ap()[r0p:r0p + nrp, :], in_=stat[0:nrp, 11:12]),
                waits=[("eV", vz2)])
        pxst = P.qcov("gpsimd")
        prst = P.qcov("gpsimd")
        for i in range(ntile):
            r0 = i * 128
            nr = min(128, Tloc + 3 - r0)
            ld = P.dma("sync", lambda e, r0=r0, nr=nr: e.dma_start(
                out=xa[0:nr, :], in_=io["x_slice"].ap()[r0:r0 + nr, :]),
                waits=[("eV", pv6), ("eA", pa1)])
            pa1 = P.op("scalar", lambda e, nr=nr: e.activation(
                sqt[0:nr, :], xa[0:nr, :], AF.Square, accum_out=stat[0:nr, 1:2]),
                waits=[ld])
            v = P.op("vector", lambda e, nr=nr: e.reduce_sum(stat[0:nr, 0:1], xa[0:nr, :], AX.X),
                     waits=[ld, pxst])
            v = P.op("vector", lambda e, nr=nr: e.tensor_scalar(
                stat[0:nr, 2:3], stat[0:nr, 0:1], 1.0 / D, None, OP.mult, OP.bypass))
            v = P.op("vector", lambda e, nr=nr: e.tensor_tensor(
                stat[0:nr, 3:4], stat[0:nr, 2:3], stat[0:nr, 2:3], OP.mult))
            v4 = P.op("vector", lambda e, nr=nr: e.scalar_tensor_tensor(
                stat[0:nr, 4:5], stat[0:nr, 1:2], 1.0 / D, stat[0:nr, 3:4],
                OP.mult, OP.subtract), waits=[("eA", pa1)])
            a2 = P.op("scalar", lambda e, nr=nr: e.activation(
                stat[0:nr, 5:6], stat[0:nr, 4:5], AF.Sqrt, bias=1e-5), waits=[("eV", v4)])
            v = P.op("vector", lambda e, nr=nr: e.reciprocal(stat[0:nr, 6:7], stat[0:nr, 5:6]),
                     waits=[("eA", a2)])
            pv6 = P.op("vector", lambda e, nr=nr: e.tensor_scalar(
                xnb[0:nr, :], xa[0:nr, :], stat[0:nr, 2:3], stat[0:nr, 6:7],
                OP.subtract, OP.mult))
            v = P.op("vector", lambda e, nr=nr: e.tensor_tensor(
                stat[0:nr, 7:8], stat[0:nr, 4:5], stat[0:nr, 6:7], OP.mult))
            v8 = P.op("vector", lambda e, nr=nr: e.tensor_tensor(
                stat[0:nr, 8:9], stat[0:nr, 7:8], stat[0:nr, 6:7], OP.mult))
            a3 = P.op("scalar", lambda e, nr=nr: e.activation(
                stat[0:nr, 9:10], stat[0:nr, 8:9], AF.Sqrt, bias=1e-6), waits=[("eV", v8)])
            v9 = P.op("vector", lambda e, nr=nr: e.reciprocal(
                stat[0:nr, 10:11], stat[0:nr, 9:10]), waits=[("eA", a3), prst])
            pxst = P.dma("gpsimd", lambda e, r0=r0, nr=nr: e.dma_start(
                out=dr["xn_cc"].ap()[:, 13 + r0:13 + r0 + nr, :].transpose([1, 0, 2]),
                in_=xnb[0:nr, :].rearrange("p (k f) -> p k f", k=KCH)),
                waits=[("eV", pv6)])
            prst = P.dma("gpsimd", lambda e, r0=r0, nr=nr: e.dma_start(
                out=dr["rfac"].ap()[13 + r0:13 + r0 + nr, :], in_=stat[0:nr, 10:11]),
                waits=[("eV", v9)])
        if dbg:
            P.dma("gpsimd", lambda e: e.dma_start(out=db["d_xn"].ap(), in_=dr["xn_cc"].ap()),
                  waits=P.qcov("gpsimd"))
            P.dma("gpsimd", lambda e: e.dma_start(out=db["d_rfac"].ap(), in_=dr["rfac"].ap()),
                  waits=P.qcov("gpsimd"))

        if cfg["phases"] <= 0:
            P.kill()
        P.barrier()
        close_phase()
        sb = newphase()
        XNR = FPAD + Tloc
        xnfm = [sb(f"xnfm{k}", [128, XNR], BF16) for k in range(KCH)]
        wsb = [sb(f"wsb{j}", [128, KCH * 128], BF16) for j in range(2)]
        wbag = sb("wbag", [128, KCH * WBA], BF16)
        gwkb = [sb(f"gwk{j}", [128, E], F32) for j in range(2)]
        rscf = sb("rscf", [128, KCH], F32)
        cwsb = [sb(f"cwsb{j}", [128, 4], F32) for j in range(2)]
        mixf = sb("mixf", [128, Tloc + 3], BF16)
        cacc = sb("cacc", [128, Tloc], F32)
        rrow = sb("rrow", [1, 1024], F32)
        outb = sb("outb", [128, Tloc], BF16)
        bat = sb("bat", [128, WBA], F32)
        btmp = sb("btmp", [128, max(2 * NVH, 6 * E + 8)], F32)
        spsc = sb("spsc", [128, NVH], F32)
        spb = sb("spb", [128, NVH], F32)
        gbp = sb("gbp", [128, 2 * GBW], F32)
        rfc = sb("rfc", [128, 1], F32)
        ctile = sb("ctile", [128, 8], F32)

        xl = [P.dma("sync", lambda e, k=k: e.dma_start(
            out=xnfm[k][:], in_=dr["xn_cc"].ap()[k][0:XNR, :], transpose=True))
              for k in range(KCH)]
        xl_all = P.qcov("sync")
        wbld = P.dma("gpsimd", lambda e: e.dma_start(
            out=wbag[:].rearrange("p (k f) -> p k f", f=WBA)[:, :, 0:4 * NKH],
            in_=io["Wb"].ap().rearrange("(k p) f -> p k f", p=128)))
        rscld = P.dma("gpsimd", lambda e: e.dma_start(
            out=rscf[:], in_=io["rsc"].ap().rearrange("(k p) o -> p k o", p=128),
            allow_slow_non_contiguous=True))
        foldv = {}
        for k in range(KCH):
            gld = P.dma("scalar", lambda e, k=k: e.dma_start(
                out=gwkb[k % 2][:], in_=io["gw"].ap()[k * 128:(k + 1) * 128, :]),
                waits=[("eV", foldv.get(k - 2, 0))])
            foldv[k] = P.op("vector", lambda e, k=k: e.tensor_scalar(
                wbag[:, k * WBA + 4 * NKH:(k + 1) * WBA], gwkb[k % 2][:],
                rscf[:, k:k + 1], float(D) ** -0.5, OP.mult, OP.mult),
                waits=[gld, rscld, wbld])
        fold_all = foldv[KCH - 1]

        PSB = 1536
        gbst = {}
        logit_dump = {}
        prev_slot_ev = 0
        prev_ct_ev = 0
        for i in range(nchl):
            col0 = FPAD + i * 128
            mml = 0
            for k in range(KCH):
                w0 = [xl_all, ("eV", fold_all), ("eV", prev_slot_ev),
                      ("eV", prev_ct_ev)] if k == 0 else []
                mml = P.op("tensor", lambda e, k=k, col0=col0: e.matmul(
                    psH[:, PSB:PSB + WBA], xnfm[k][:, col0:col0 + 128],
                    wbag[:, k * WBA:(k + 1) * WBA], start=(k == 0), stop=(k == KCH - 1)),
                    waits=w0)
            rfl = P.dma("scalar", lambda e, i=i: e.dma_start(
                out=rfc[:], in_=dr["rfac"].ap()[FPAD + i * 128:FPAD + (i + 1) * 128, :]),
                waits=[prst, ("eA", 0 if i == 0 else rfl_used)])
            vb = P.op("vector", lambda e: e.tensor_copy(bat[:], psH[:, PSB:PSB + WBA]),
                      waits=[("eT", mml)])
            prev_slot_ev = vb
            lg0 = 4 * NKH
            vcl = P.op("vector", lambda e: e.tensor_copy(btmp[:, 0:E], bat[:, lg0:lg0 + E]),
                       waits=[logit_dump.get(i - 1, Z)])
            bsel = bat[:, 0:4 * NKH].rearrange("p (h f) -> p h f", f=4)[:, :, 0:2]
            asel = bat[:, 0:4 * NKH].rearrange("p (h f) -> p h f", f=4)[:, :, 2:4]

            def gsl(sec):
                return bass.AP(gbp, sec * VHL, [[2 * GBW, 128], [GBW, 2], [1, VHL]])
            vbet = P.op("vector", lambda e: e.tensor_copy(spsc[:, 0:NVH], bsel))
            abeta = P.op("scalar", lambda e: e.activation(gsl(0), spsc[:], AF.Sigmoid),
                         waits=[("eV", vbet), gbst.get(i - 1, Z)])
            aab = P.op("scalar", lambda e: e.activation(spb[:], spsc[:], AF.Abs),
                       waits=[("eV", vcl)])
            ae1 = P.op("scalar", lambda e: e.activation(spb[:], spb[:], AF.Exp, scale=-1.0))
            ae2 = P.op("scalar", lambda e: e.activation(spb[:], spb[:], AF.Ln, bias=1.0))
            vmx = P.op("vector", lambda e: e.tensor_scalar(
                spsc[:], spsc[:], -1.0, 0.0, OP.mult, OP.max), waits=[("eA", aab)])
            vlnb = P.op("vector", lambda e: e.scalar_tensor_tensor(
                gsl(2), spsc[:], -1.0, spb[:], OP.mult, OP.subtract),
                waits=[("eA", ae2)])
            vas = P.op("vector", lambda e: e.tensor_tensor(spsc[:], asel, dtbb[:], OP.add))
            aab2 = P.op("scalar", lambda e: e.activation(spb[:], spsc[:], AF.Abs),
                        waits=[("eV", vas)])
            ae3 = P.op("scalar", lambda e: e.activation(spb[:], spb[:], AF.Exp, scale=-1.0))
            ae4 = P.op("scalar", lambda e: e.activation(spb[:], spb[:], AF.Ln, bias=1.0))
            vmx2 = P.op("vector", lambda e: e.tensor_scalar(
                spsc[:], spsc[:], 0.0, None, OP.max, OP.bypass), waits=[("eA", aab2)])
            vsp = P.op("vector", lambda e: e.tensor_tensor(
                spsc[:], spsc[:], spb[:], OP.add), waits=[("eA", ae4)])
            vg = P.op("vector", lambda e: e.tensor_tensor(gsl(1), spsc[:], negeA[:], OP.mult))
            gbst[i] = P.dma("scalar", lambda e, i=i: e.dma_start(
                out=dr["cgb"].ap()[i * 128:(i + 1) * 128, :], in_=gbp[:]),
                waits=[("eV", vg)])
            if dbg:
                logit_dump[i] = P.dma("gpsimd", lambda e, i=i: e.dma_start(
                    out=db["d_logit"].ap()[i * 128:(i + 1) * 128, 0:E], in_=btmp[:, 0:E]),
                    waits=[("eV", vcl)])
            m1 = P.op("vector", lambda e: e.reduce_max(btmp[:, 5 * E:5 * E + 1], btmp[:, 0:E], AX.X),
                      waits=[logit_dump.get(i - 1, Z)] if dbg else ())
            e1v = P.op("vector", lambda e: e.tensor_scalar(
                btmp[:, 2 * E:3 * E], btmp[:, 0:E], btmp[:, 5 * E:5 * E + 1], None, OP.is_ge, OP.bypass))
            msk = P.op("vector", lambda e: e.scalar_tensor_tensor(
                btmp[:, 3 * E:4 * E], btmp[:, 2 * E:3 * E], NEG, btmp[:, 0:E], OP.mult, OP.add))
            m2 = P.op("vector", lambda e: e.reduce_max(btmp[:, 5 * E + 1:5 * E + 2], btmp[:, 3 * E:4 * E], AX.X))
            e2v = P.op("vector", lambda e: e.tensor_scalar(
                btmp[:, 4 * E:5 * E], btmp[:, 3 * E:4 * E], btmp[:, 5 * E + 1:5 * E + 2], None,
                OP.is_ge, OP.bypass))
            d12 = P.op("vector", lambda e: e.tensor_tensor(
                btmp[:, 5 * E + 2:5 * E + 3], btmp[:, 5 * E:5 * E + 1], btmp[:, 5 * E + 1:5 * E + 2], OP.subtract))
            w1 = P.op("scalar", lambda e: e.activation(
                btmp[:, 5 * E + 3:5 * E + 4], btmp[:, 5 * E + 2:5 * E + 3], AF.Sigmoid, scale=rfc[:, 0:1]),
                waits=[("eV", d12), rfl])
            rfl_used = w1
            w2 = P.op("vector", lambda e: e.tensor_scalar(
                btmp[:, 5 * E + 4:5 * E + 5], btmp[:, 5 * E + 3:5 * E + 4], -1.0, 1.0, OP.mult, OP.add),
                waits=[("eA", w1)])
            cb1 = P.op("vector", lambda e: e.tensor_scalar(
                btmp[:, 2 * E:3 * E], btmp[:, 2 * E:3 * E], btmp[:, 5 * E + 3:5 * E + 4], None,
                OP.mult, OP.bypass))
            cb2 = P.op("vector", lambda e: e.scalar_tensor_tensor(
                btmp[:, 4 * E:5 * E], btmp[:, 4 * E:5 * E], btmp[:, 5 * E + 4:5 * E + 5],
                btmp[:, 2 * E:3 * E], OP.mult, OP.add))
            cpes = P.op("vector", lambda e: e.tensor_tensor(
                ctile[:, 0:E], btmp[:, 4 * E:5 * E], pesb[:], OP.mult))
            ctr = P.op("tensor", lambda e: e.transpose(
                psH[0:E, PSB + 256:PSB + 384], ctile[:, 0:E], cI[:]),
                waits=[("eV", cpes), ("eV", prev_ct_ev)])
            prev_ct_ev = P.op("vector", lambda e, i=i: e.tensor_copy(
                cfm[0:E, i * 128:(i + 1) * 128], psH[0:E, PSB + 256:PSB + 384]),
                waits=[("eT", ctr)])
        if dbg:
            P.dma("gpsimd", lambda e: e.dma_start(out=db["d_cgb"].ap(), in_=dr["cgb"].ap()),
                  waits=P.qcov("scalar"))
            P.dma("gpsimd", lambda e: e.dma_start(out=db["d_cfm"].ap()[0:E, :], in_=cfm[0:E, :]),
                  waits=[("eV", prev_ct_ev)])

        TPq = tile_plan(Tloc + 3)
        TPz = tile_plan(Tloc)
        L2G = tile_plan(Tloc, 1024)
        nfc = 6 * NKH
        slot_gate = [Z, Z, Z, Z]
        mm_done = {}
        cw_used = {}
        last_store = Z
        cacc_free = ("eV", 0)
        sqb_free = ("eT", 0)
        rrow_free = ("eT", 0)
        psH_l2_free = [Z, Z]

        def secrows(kh, sec):
            rank, khl = kh // NKHL, kh % NKHL
            if sec == 0:
                return rank * SECL + khl * 128
            if sec == 1:
                return rank * SECL + KW + khl * 128
            if sec in (2, 3):
                return rank * SECL + 2 * KW + (2 * khl + sec - 2) * 128
            return rank * SECL + 2 * KW + VW + (2 * khl + sec - 4) * 128

        def convrow(kh, sec):
            return kh * 128 if sec == 0 else (KD + kh * 128 if sec == 1
                                              else 2 * KD + (2 * kh + sec - 2) * 128)

        wld = {}
        for fc in range(nfc):
            kh, sec = fc // 6, fc % 6
            col0 = kh * 6 * 128 + sec * 128
            wld[fc] = P.dma("gpsimd", lambda e, col0=col0, fc=fc: e.dma_start(
                out=wsb[fc % 2][:],
                in_=io["Wq"].ap()[:, col0:col0 + 128].rearrange("(k p) f -> p k f", p=128)),
                waits=[("eT", mm_done.get(fc - 2, 0))])
            if sec < 4:
                cwld = P.dma("scalar", lambda e, kh=kh, sec=sec, fc=fc: e.dma_start(
                    out=cwsb[fc % 2][:],
                    in_=io["convw"].ap()[convrow(kh, sec):convrow(kh, sec) + 128, :]),
                    waits=[("eV", cw_used.get(fc - 2, 0))])
            tiles = TPq if sec < 4 else TPz
            mml = 0
            last_ev = Z
            for ti, (t0_, tn) in enumerate(tiles):
                slot = ti % 4
                for k in range(KCH):
                    src_off = 13 + t0_ if sec < 4 else FPAD + t0_
                    w0 = ([wld[fc], xl_all, slot_gate[slot]]
                          if k == 0 else [])
                    mml = P.op("tensor", lambda e, k=k, slot=slot, so=src_off, tn=tn, fc=fc: e.matmul(
                        psF[:, slot * 512:slot * 512 + tn],
                        wsb[fc % 2][:, k * 128:(k + 1) * 128],
                        xnfm[k][:, so:so + tn], start=(k == 0), stop=(k == KCH - 1)),
                        waits=w0)
                if sec < 4:
                    ev = P.op("vector", lambda e, slot=slot, t0_=t0_, tn=tn: e.tensor_copy(
                        mixf[:, t0_:t0_ + tn], psF[:, slot * 512:slot * 512 + tn]),
                        waits=[("eT", mml), cacc_free if ti == 0 else Z,
                               sqb_free if ti == 0 else Z])
                    slot_gate[slot] = ("eV", ev)
                else:
                    ev = P.op("scalar", lambda e, slot=slot, t0_=t0_, tn=tn: e.activation(
                        outb[:, t0_:t0_ + tn], psF[:, slot * 512:slot * 512 + tn], AF.Silu),
                        waits=[("eT", mml), last_store if ti == 0 else Z])
                    slot_gate[slot] = ("eA", ev)
                last_ev = slot_gate[slot]
            mm_done[fc] = mml
            if sec < 4:
                c1 = P.op("vector", lambda e, fc=fc: e.tensor_scalar(
                    cacc[:], mixf[:, 0:Tloc], cwsb[fc % 2][:, 0:1], None, OP.mult, OP.bypass),
                    waits=[cwld, last_ev, cacc_free])
                for j in range(1, 4):
                    c1 = P.op("vector", lambda e, j=j, fc=fc: e.scalar_tensor_tensor(
                        cacc[:], mixf[:, j:j + Tloc], cwsb[fc % 2][:, j:j + 1], cacc[:],
                        OP.mult, OP.add))
                cw_used[fc] = c1
                if sec < 2:
                    asil = P.op("scalar", lambda e: e.activation(cacc[:], cacc[:], AF.Silu),
                                waits=[("eV", c1)])
                    asq = P.op("scalar", lambda e: e.activation(mixf[:, 0:Tloc], cacc[:], AF.Square),
                               waits=[sqb_free, ("eV", c1)])
                    qsc = float(128) ** -0.5 if sec == 0 else 1.0
                    vout = 0
                    for gi, (g0_, gn) in enumerate(L2G):
                        nsub = max(1, gn // 512)
                        subw = min(512, gn)
                        rowl = 0
                        for si in range(nsub):
                            rowl = P.op("tensor", lambda e, g0_=g0_, si=si, subw=subw: e.matmul(
                                psH[0:1, si * 512:si * 512 + subw], c1cb[:],
                                mixf[:, g0_ + si * 512:g0_ + si * 512 + subw],
                                start=True, stop=True),
                                waits=([("eA", asq), psH_l2_free[0], psH_l2_free[1], rrow_free]
                                       if si == 0 else []))
                        arow = P.op("scalar", lambda e, gn=gn: e.activation(
                            rrow[0:1, 0:gn], psH[0:1, 0:gn], AF.Sqrt, bias=1e-6),
                            waits=[("eT", rowl)])
                        vrec = P.op("vector", lambda e, gn=gn: e.reciprocal(
                            rrow[0:1, 0:gn], rrow[0:1, 0:gn]), waits=[("eA", arow)])
                        if qsc != 1.0:
                            vrec = P.op("vector", lambda e, gn=gn, qsc=qsc: e.tensor_scalar(
                                rrow[0:1, 0:gn], rrow[0:1, 0:gn], qsc, None, OP.mult, OP.bypass))
                        bcl = 0
                        for si in range(nsub):
                            bcl = P.op("tensor", lambda e, si=si, subw=subw: e.matmul(
                                psH[:, si * 512:si * 512 + subw], c1r[0:1, :],
                                rrow[0:1, si * 512:si * 512 + subw], start=True, stop=True),
                                waits=[("eV", vrec)] if si == 0 else [])
                        rrow_free = ("eT", bcl)
                        vout = P.op("vector", lambda e, g0_=g0_, gn=gn: e.tensor_tensor(
                            outb[:, g0_:g0_ + gn], cacc[:, g0_:g0_ + gn], psH[:, 0:gn], OP.mult),
                            waits=[("eT", bcl), last_store if gi == 0 else Z])
                        psH_l2_free[gi % 2] = ("eV", vout)
                    st_gate = ("eV", vout)
                    cacc_free = ("eV", vout)
                    sqb_free = ("eT", rowl)
                else:
                    dsil = P.op("scalar", lambda e: e.activation(outb[:], cacc[:], AF.Silu),
                                waits=[("eV", c1), last_store])
                    st_gate = ("eA", dsil)
                    cacc_free = ("eA", dsil)
            else:
                st_gate = last_ev
            rb = secrows(kh, sec)
            last_store = P.dma("scalar", lambda e, rb=rb: e.dma_start(
                out=dr["contrib"].ap()[:, rb:rb + 128, :].transpose([1, 0, 2]),
                in_=outb[:].rearrange("p (c t) -> p c t", c=nchl)), waits=[st_gate])
        contrib_done = last_store
        gb_done = gbst[nchl - 1]
        if dbg:
            P.dma("gpsimd", lambda e: e.dma_start(out=db["d_contrib"].ap(), in_=dr["contrib"].ap()),
                  waits=P.qcov("scalar"))

        if cfg["phases"] <= 1:
            P.kill()
        ccA = P.cc(lambda e: e.collective_compute(
            "AllGather", OP.bypass, replica_groups=[[2 * i, 2 * i + 1] for i in range(B)],
            ins=[dr["contrib"].ap().opt()], outs=[dr["gcb"].ap().opt()]),
            waits=P.qcov("scalar"))
        ccB = P.cc(lambda e: e.collective_compute(
            "AllGather", OP.bypass, replica_groups=[[2 * i, 2 * i + 1] for i in range(B)],
            ins=[dr["cgb"].ap().opt()], outs=[dr["ggb"].ap().opt()]),
            waits=P.qcov("scalar"))

        if cfg["phases"] <= 2:
            P.kill()
        P.barrier(exclude=("cc",))
        close_phase()
        sb = newphase()
        xnt = [sb(f"f_xnt{k}", [128, TB], BF16) for k in range(KCH)]
        gup = [sb(f"f_gup{j}", [128, KCH * 256], BF16) for j in range(2)]
        DNH = min(512, D)
        EDH = max(1, EDCH // 2)
        nhf = EDCH // EDH
        dnb = [sb(f"f_dnb{j}", [128, EDH * DNH], BF16) for j in range(2)]
        actb = sb("f_act", [128, EDCH * TB], BF16)
        acc = sb("f_acc", [128, (TB // 128) * D], F32)
        ggb_ = sb("f_gg", [128, 512], BF16)
        gg2 = sb("f_gg2", [128, 512], BF16)
        crw = sb("f_crw", [1, TB], F32)
        cbc = sb("f_cbc", [128, TB], BF16)
        GH = D // 2
        gtmp = sb("f_gtmp", [128, GH], F32)
        ndh = D // DNH
        TT512 = tile_plan(TB)
        gate_slot = [Z, Z, Z, Z]
        dn_slot = [Z, Z]
        gu_mm_done = {"last": 0}
        gup_free = [0, 0]
        dn_mm_done = [0, 0]
        cbl_prev = 0
        acc_free = 0
        vg2_prev = 0
        act_dump_gate = Z

        gst_last = Z
        for tb in range(ntb):
            xwait = [pxst, ("eT", gu_mm_done["last"])]
            xld = [P.dma("sync", lambda e, k=k, tb=tb: e.dma_start(
                out=xnt[k][:], in_=dr["xn_cc"].ap()[k][FPAD + tb * TB:FPAD + (tb + 1) * TB, :],
                transpose=True), waits=xwait) for k in range(KCH)]
            xld_all = P.qcov("sync")
            if cfg["phases"] <= 2.05:
                P.kill()
            for e_ in range(E):
                crl = 0
                cbl = 0
                cm_gate = cbl_prev
                for si, (s0, sn) in enumerate(TT512):
                    cm1 = P.op("tensor", lambda e, e_=e_, tb=tb, s0=s0, sn=sn: e.matmul(
                        psH[0:1, 1536:1536 + sn], cIb[0:E, e_:e_ + 1],
                        cfm[0:E, tb * TB + s0:tb * TB + s0 + sn], start=True, stop=True),
                        waits=[("eV", cm_gate)] if si == 0 else [("eV", cbl)])
                    crl = P.op("vector", lambda e, s0=s0, sn=sn: e.tensor_copy(
                        crw[0:1, s0:s0 + sn], psH[0:1, 1536:1536 + sn]), waits=[("eT", cm1)])
                    cm2 = P.op("tensor", lambda e, s0=s0, sn=sn: e.matmul(
                        psH[:, 1536:1536 + sn], c1r[0:1, :], crw[0:1, s0:s0 + sn],
                        start=True, stop=True), waits=[("eV", crl)])
                    cbl = P.op("vector", lambda e, s0=s0, sn=sn: e.tensor_copy(
                        cbc[:, s0:s0 + sn], psH[:, 1536:1536 + sn]), waits=[("eT", cm2)])
                    if cfg["phases"] <= 2.1:
                        P.kill()
                cbl_prev = cbl
                if cfg["phases"] <= 2.15:
                    P.kill()
                for fp in range(EDCH):
                    gld = P.dma("gpsimd", lambda e, e_=e_, fp=fp, tb=tb: e.dma_start(
                        out=gup[fp % 2][:],
                        in_=io["gu"].ap()[e_][:, 2 * fp * 128:2 * fp * 128 + 256]
                        .rearrange("(k p) f -> p k f", p=128)),
                        waits=[("eT", gup_free[fp % 2])])
                    mlast = 0
                    for si, (s0, sn) in enumerate(TT512):
                        slA, slB = (si % 2) * 2, (si % 2) * 2 + 1
                        for k in range(KCH):
                            w0 = ([gld, xld_all, gate_slot[slA]]
                                  if k == 0 else [])
                            P.op("tensor", lambda e, k=k, slA=slA, s0=s0, sn=sn, fp=fp: e.matmul(
                                psF[:, slA * 512:slA * 512 + sn],
                                gup[fp % 2][:, k * 256:k * 256 + 128],
                                xnt[k][:, s0:s0 + sn], start=(k == 0), stop=(k == KCH - 1)),
                                waits=w0)
                        for k in range(KCH):
                            w0 = [gate_slot[slB]] if k == 0 else []
                            mlast = P.op("tensor", lambda e, k=k, slB=slB, s0=s0, sn=sn, fp=fp: e.matmul(
                                psF[:, slB * 512:slB * 512 + sn],
                                gup[fp % 2][:, k * 256 + 128:k * 256 + 256],
                                xnt[k][:, s0:s0 + sn], start=(k == 0), stop=(k == KCH - 1)),
                                waits=w0)
                        ag = P.op("scalar", lambda e, slA=slA, sn=sn: e.activation(
                            ggb_[:, 0:sn], psF[:, slA * 512:slA * 512 + sn], AF.Gelu),
                            waits=[("eT", mlast), ("eV", vg2_prev)])
                        vg2 = P.op("vector", lambda e, s0=s0, sn=sn: e.tensor_tensor(
                            gg2[:, 0:sn], ggb_[:, 0:sn], cbc[:, s0:s0 + sn], OP.mult),
                            waits=[("eA", ag), ("eV", cbl)])
                        vg2_prev = vg2
                        vact = P.op("vector", lambda e, slB=slB, fp=fp, s0=s0, sn=sn: e.tensor_tensor(
                            actb[:, fp * TB + s0:fp * TB + s0 + sn], gg2[:, 0:sn],
                            psF[:, slB * 512:slB * 512 + sn], OP.mult),
                            waits=[act_dump_gate])
                        gate_slot[slA] = ("eA", ag)
                        gate_slot[slB] = ("eV", vact)
                    gup_free[fp % 2] = mlast
                    gu_mm_done["last"] = mlast
                    if cfg["phases"] <= 2.3:
                        P.kill()
                act_done = P.cnt["eV"]
                if cfg["phases"] <= 2.4:
                    P.kill()
                if dbg and tb == 0 and e_ == 0:
                    act_dump_gate = P.dma("gpsimd",
                                          lambda e: e.dma_start(out=db["d_act"].ap(), in_=actb[:]),
                                          waits=[("eV", act_done)])
                for dh in range(ndh):
                    for hf in range(nhf):
                        bi = (dh * nhf + hf) % 2
                        dld = P.dma("gpsimd", lambda e, e_=e_, dh=dh, hf=hf, bi=bi: e.dma_start(
                            out=dnb[bi][:],
                            in_=io["dn"].ap()[e_][hf * EDH * 128:(hf + 1) * EDH * 128,
                                                  dh * DNH:(dh + 1) * DNH]
                            .rearrange("(k p) f -> p k f", p=128)),
                            waits=[("eT", dn_mm_done[bi])])
                        for tc in range(TB // 128):
                            sl = 2 + tc % 2
                            mm2 = 0
                            for k in range(EDH):
                                w0 = ([dld, ("eV", act_done), dn_slot[sl - 2]]
                                      if k == 0 else [])
                                mm2 = P.op("tensor", lambda e, k=k, sl=sl, tc=tc, hf=hf, bi=bi: e.matmul(
                                    psH[:, (sl - 2) * 512:(sl - 2) * 512 + DNH],
                                    actb[:, (hf * EDH + k) * TB + tc * 128:(hf * EDH + k) * TB + tc * 128 + 128],
                                    dnb[bi][:, k * DNH:(k + 1) * DNH],
                                    start=(k == 0), stop=(k == EDH - 1)), waits=w0)
                            col = tc * D + dh * DNH
                            if e_ == 0 and hf == 0:
                                va = P.op("vector", lambda e, col=col, sl=sl: e.tensor_copy(
                                    acc[:, col:col + DNH], psH[:, (sl - 2) * 512:(sl - 2) * 512 + DNH]),
                                    waits=[("eT", mm2), ("eA", acc_free)])
                            else:
                                va = P.op("vector", lambda e, col=col, sl=sl: e.tensor_tensor(
                                    acc[:, col:col + DNH], acc[:, col:col + DNH],
                                    psH[:, (sl - 2) * 512:(sl - 2) * 512 + DNH], OP.add),
                                    waits=[("eT", mm2)])
                            dn_slot[sl - 2] = ("eV", va)
                        dn_mm_done[bi] = mm2
                    if cfg["phases"] <= 2.5:
                        P.kill()
            if dbg and tb == 0:
                P.dma("gpsimd", lambda e: e.dma_start(out=db["d_acc"].ap(), in_=acc[:]),
                      waits=[dn_slot[0], dn_slot[1]])
            for tc in range(TB // 128):
                for gh in range(D // GH):
                    ags = P.op("scalar", lambda e, tc=tc, gh=gh: e.activation(
                        gtmp[:], acc[:, tc * D + gh * GH:tc * D + (gh + 1) * GH], AF.Silu),
                        waits=[dn_slot[0], dn_slot[1], gst_last])
                    acc_free = ags
                    gst_last = P.dma("scalar", lambda e, tb=tb, tc=tc, gh=gh: e.dma_start(
                        out=dr["gmy"].ap()[tb * TB + tc * 128:tb * TB + (tc + 1) * 128,
                                           gh * GH:(gh + 1) * GH],
                        in_=gtmp[:]), waits=[("eA", ags)])
        gmy_done = P.qcov("scalar")
        if dbg:
            P.dma("gpsimd", lambda e: e.dma_start(out=db["d_gmy"].ap(), in_=dr["gmy"].ap()),
                  waits=P.qcov("scalar"))

        if cfg["phases"] <= 3:
            P.kill()
        P.barrier()
        close_phase()
        sb = newphase()
        qf2 = [sb(f"s_qf{j}", [128, KW], BF16) for j in range(2)]
        kf2 = [sb(f"s_kf{j}", [128, KW], BF16) for j in range(2)]
        ktm2 = [sb(f"s_ktm{j}", [128, KW], BF16) for j in range(2)]
        vtm2 = [sb(f"s_vtm{j}", [128, VW], BF16) for j in range(2)]
        zf2 = [sb(f"s_zf{j}", [128, VW], BF16) for j in range(2)]
        gbt2 = [sb(f"s_gbt{j}", [128, GBW], F32) for j in range(2)]
        lgv = sb("s_lgv", [128, VHL], F32)
        lgbv = sb("s_lgbv", [128, VHL], F32)
        lamE = sb("s_lamE", [128, VHL], F32)
        blamn = sb("s_blamn", [128, VHL], F32)
        escv = sb("s_escv", [128, VHL], F32)
        lamCv = sb("s_lamCv", [128, VHL], F32)
        trow = sb("s_trow", [1, VHL], F32)
        Blg = sb("s_Blg", [128, VW], F32)
        Blgb = sb("s_Blgb", [128, VW], F32)
        Blgp = sb("s_Blgp", [128, VW], F32)
        t1f = sb("s_t1f", [128, VW], F32)
        t2f = sb("s_t2f", [128, VW], F32)
        eMZ = sb("s_eMZ", [128, VW], F32)
        eMTZ = sb("s_eMTZ", [128, VW], F32)
        eDin = sb("s_eDin", [128, VW], F32)
        eLam = sb("s_eLam", [128, VW], F32)
        Mb = sb("s_Mb", [128, VW], BF16)
        MTb = sb("s_MTb", [128, VW], BF16)
        Ptb = sb("s_Ptb", [128, VW], BF16)
        Qlb = sb("s_Qlb", [128, VW], BF16)
        WTb = [sb(f"s_WT{j}", [128, VW], BF16) for j in range(2)]
        Pb = [sb(f"s_Pb{j}", [128, VW], BF16) for j in range(2)]
        PTb = [sb(f"s_PTb{j}", [128, VW], BF16) for j in range(2)]
        Rb = sb("s_Rb", [128, VW], BF16)
        ub = sb("s_ub", [128, VW], BF16)
        ogz = sb("s_ogz", [128, VW], F32)
        sqz = sb("s_sqz", [128, VW], BF16)
        ogb = sb("s_ogb", [128, VW], BF16)
        kbb = sb("s_kbb", [128, VW], BF16)
        rfr = sb("s_rfr", [1, VW], F32)
        regcell = {}

        def regdma(e, row, cg, out_ap, handle, dims):
            if "r" not in regcell:
                regcell["r"] = e.alloc_register(name="roff")
            r = regcell["r"]
            e.reg_load(r, offsb[0:1, row * 64 + cg:row * 64 + cg + 1])
            return e.dma_start(out=out_ap, in_=bass.AP(handle, r, dims))

        def bcv(t):
            return bass.AP(t, 0, [[VHL, 128], [1, VHL], [0, 128]])

        def tile128(t):
            return bass.AP(t, 0, [[128, 128], [0, VHL], [1, 128]])

        def repA(pst, base):
            return bass.AP(pst, base, [[2048, 128], [128, NKHL], [0, 2], [1, 128]])

        def repK(t):
            return bass.AP(t, 0, [[KW, 128], [128, NKHL], [0, 2], [1, 128]])

        G_ = min(4, VHL)
        H_ = VHL // G_

        def ilvcol(vh):
            return 512 * (vh % G_) + 128 * (vh // G_)

        def ilv(pst):
            return bass.AP(pst, 0, [[2048, 128], [128, H_], [512, G_], [1, 128]])

        if cfg.get("sk", 99) <= -2:
            P.kill()
        rs = []
        for bb in range(2):
            rs.append(P.dma("gpsimd", lambda e, bb=bb: regdma(
                e, 5, bb, dr["scan_loc"].ap()[bb * nchl:(bb + 1) * nchl], dr["gcb"],
                [[2 * SECL * 128, nchl], [128, SECL], [1, 128]]),
                waits=[("cc", ccA)]))
            rs.append(P.dma("gpsimd", lambda e, bb=bb: regdma(
                e, 6, bb, dr["gb_loc"].ap()[bb * Tloc:(bb + 1) * Tloc], dr["ggb"],
                [[2 * GBW, Tloc], [1, GBW]]),
                waits=[("cc", ccB)]))
        rs_done = P.qcov("gpsimd")
        if cfg.get("sk", 99) <= -1:
            P.kill()
        c_qf = [Z, Z]
        c_kf = [Z, Z]
        c_ktm = [Z, Z]
        c_vtm = [Z, Z]
        c_zf = [Z, Z]
        c_gbt = [Z, Z]
        prev_Sbf = 0
        prev_STog = Z
        prev_psF_cons = ("eV", 0)
        prev_psH_cons = ("eV", 0)
        gT1, gT2 = [Z, Z], [Z, Z]
        gLq, gLk, gLz, gLgb = [Z, Z], [Z, Z], [Z, Z], [Z, Z]
        ogst = {}
        for cg in range(nchg):
            j = cg % 2
            rw = rs_done if cg < 2 else []
            gLgb[j] = P.dma("scalar", lambda e, j=j, cg=cg: e.dma_start(
                out=gbt2[j][:], in_=dr["gb_loc"].ap()[cg * 128:(cg + 1) * 128, :]),
                waits=[c_gbt[j], rw])
            gLq[j] = P.dma("scalar", lambda e, j=j, cg=cg: e.dma_start(
                out=qf2[j][:].rearrange("p (h t) -> p h t", h=NKHL),
                in_=dr["scan_loc"].ap()[cg, 0:KW, :].rearrange("(h f) t -> f h t", f=128)),
                waits=[c_qf[j], rw])
            gLk[j] = P.dma("scalar", lambda e, j=j, cg=cg: e.dma_start(
                out=kf2[j][:].rearrange("p (h t) -> p h t", h=NKHL),
                in_=dr["scan_loc"].ap()[cg, KW:2 * KW, :].rearrange("(h f) t -> f h t", f=128)),
                waits=[c_kf[j], rw])
            gLz[j] = P.dma("scalar", lambda e, j=j, cg=cg: e.dma_start(
                out=zf2[j][:].rearrange("p (h t) -> p h t", h=VHL),
                in_=dr["scan_loc"].ap()[cg, 2 * KW + VW:SECL, :].rearrange("(h f) t -> f h t", f=128)),
                waits=[c_zf[j], rw])
            if cfg.get("sk", 99) <= -0.5:
                P.kill()
            gT1[j] = P.dma("sync", lambda e, j=j, cg=cg: e.dma_start(
                out=ktm2[j][:], in_=dr["scan_loc"].ap()[cg, KW:2 * KW, :], transpose=True),
                waits=[c_ktm[j], rw])
            gT2[j] = P.dma("sync", lambda e, j=j, cg=cg: e.dma_start(
                out=vtm2[j][:], in_=dr["scan_loc"].ap()[cg, 2 * KW:2 * KW + VW, :], transpose=True),
                waits=[c_vtm[j], rw])

            if cfg.get("sk", 99) <= 0:
                P.kill()
            M1 = P.op("tensor", lambda e, j=j: e.matmul(
                psH[:, 0:VHL], ctri[:], gbt2[j][:, VHL:2 * VHL], start=True, stop=True),
                waits=[gLgb[j], prev_psH_cons])
            M2 = P.op("tensor", lambda e, j=j: e.matmul(
                psH[0:1, 256:256 + VHL], c1cf[:], gbt2[j][:, VHL:2 * VHL],
                start=True, stop=True))
            V1 = P.op("vector", lambda e: e.tensor_copy(lgv[:], psH[:, 0:VHL]),
                      waits=[("eT", M2)])
            V2 = P.op("vector", lambda e: e.tensor_copy(trow[:], psH[0:1, 256:256 + VHL]))
            M3 = P.op("tensor", lambda e: e.matmul(
                psH[:, 512:512 + VHL], c1r[0:1, :], trow[0:1, :], start=True, stop=True),
                waits=[("eV", V2)])
            V3 = P.op("vector", lambda e, j=j: e.tensor_tensor(
                lgbv[:], lgv[:], gbt2[j][:, 2 * VHL:3 * VHL], OP.add))
            A1 = P.op("scalar", lambda e: e.activation(lamCv[:], psH[:, 512:512 + VHL], AF.Exp),
                      waits=[("eT", M3)])
            V4 = P.op("vector", lambda e: e.tensor_tensor(
                t1f[:, 0:VHL], lgv[:], psH[:, 512:512 + VHL], OP.subtract),
                waits=[("eT", M3)])
            A2 = P.op("scalar", lambda e: e.activation(escv[:], t1f[:, 0:VHL], AF.Exp,
                                                       scale=-1.0),
                      waits=[("eV", V4)])
            A3 = P.op("scalar", lambda e: e.activation(lamE[:], lgv[:], AF.Exp),
                      waits=[("eV", V3)])
            V5 = P.op("vector", lambda e, j=j: e.tensor_tensor(
                blamn[:], gbt2[j][:, 0:VHL], lamE[:], OP.mult), waits=[("eA", A3)])
            V6 = P.op("vector", lambda e: e.tensor_scalar(
                blamn[:], blamn[:], -1.0, None, OP.mult, OP.bypass))

            if cfg.get("sk", 99) <= 1:
                P.kill()
            V7 = P.op("vector", lambda e: e.tensor_copy(Blg[:], bcv(lgv)))
            V8 = P.op("vector", lambda e: e.tensor_tensor(
                Blgb[:], bcv(lgbv), tile128(cmM), OP.add))
            if cfg.get("sk", 99) <= 1.2:
                P.kill()
            trl = 0
            for vh in range(VHL):
                trl = P.op("tensor", lambda e, vh=vh: e.transpose(
                    psF[:, ilvcol(vh):ilvcol(vh) + 128], Blg[:, vh * 128:(vh + 1) * 128], cI[:]),
                    waits=[("eV", V7), prev_psF_cons] if vh == 0 else [])
            if cfg.get("sk", 99) <= 1.3:
                P.kill()
            trl2 = 0
            for vh in range(VHL):
                trl2 = P.op("tensor", lambda e, vh=vh: e.transpose(
                    psH[:, ilvcol(vh):ilvcol(vh) + 128], Blgb[:, vh * 128:(vh + 1) * 128], cI[:]),
                    waits=[("eV", V8), ("eV", V4), ("eA", A1), ("eV", V1)] if vh == 0 else [])
            if cfg.get("sk", 99) <= 1.4:
                P.kill()
            V9 = P.op("vector", lambda e: e.tensor_tensor(
                t1f[:], Blgb[:], ilv(psF), OP.subtract), waits=[("eT", trl)])
            A4 = P.op("scalar", lambda e: e.activation(eMZ[:], t1f[:], AF.Exp),
                      waits=[("eV", V9), ("eA", A2)])
            if cfg.get("sk", 99) <= 1.5:
                P.kill()
            V10 = P.op("vector", lambda e: e.tensor_tensor(
                t2f[:], Blg[:], ilv(psH), OP.subtract), waits=[("eT", trl2)])
            A5 = P.op("scalar", lambda e: e.activation(eMTZ[:], t2f[:], AF.Exp, scale=-1.0),
                      waits=[("eV", V10)])
            if cfg.get("sk", 99) <= 1.6:
                P.kill()
            A6 = P.op("scalar", lambda e: e.activation(eLam[:], ilv(psF), AF.Exp),
                      waits=[("eT", trl)])
            if cfg.get("sk", 99) <= 1.7:
                P.kill()
            V11 = P.op("vector", lambda e: e.tensor_tensor(
                Blgp[:], Blg[:], tile128(cmPT), OP.add))
            if cfg.get("sk", 99) <= 1.8:
                P.kill()
            V12 = P.op("vector", lambda e: e.tensor_tensor(
                t1f[:], Blgp[:], ilv(psF), OP.subtract),
                waits=[("eA", A6)])
            if cfg.get("sk", 99) <= 1.9:
                P.kill()
            A7 = P.op("scalar", lambda e: e.activation(eDin[:], t1f[:], AF.Exp, scale=-1.0),
                      waits=[("eV", V12)])
            if cfg.get("sk", 99) <= 2:
                P.kill()
            agl = 0
            for kl in range(NKHL):
                agl = P.op("tensor", lambda e, kl=kl, j=j: e.matmul(
                    psF[:, kl * 128:(kl + 1) * 128], kf2[j][:, kl * 128:(kl + 1) * 128],
                    kf2[j][:, kl * 128:(kl + 1) * 128], start=True, stop=True),
                    waits=[("eA", A6), ("eV", V12), gLk[j], gLq[j]]
                    if kl == 0 else [])
            for kl in range(NKHL):
                agl = P.op("tensor", lambda e, kl=kl, j=j: e.matmul(
                    psF[:, KW + kl * 128:KW + (kl + 1) * 128],
                    kf2[j][:, kl * 128:(kl + 1) * 128],
                    qf2[j][:, kl * 128:(kl + 1) * 128], start=True, stop=True))
            V13 = P.op("vector", lambda e: e.tensor_tensor(
                Mb[:], eMZ[:], repA(psF, 0), OP.mult), waits=[("eT", agl), ("eA", A5)])
            V14 = P.op("vector", lambda e: e.tensor_tensor(
                MTb[:], eMTZ[:], repA(psF, 0), OP.mult))
            V15 = P.op("vector", lambda e: e.tensor_tensor(
                Ptb[:], eDin[:], repA(psF, KW), OP.mult), waits=[("eA", A7)])
            V16 = P.op("vector", lambda e, j=j: e.tensor_tensor(
                Qlb[:], repK(qf2[j]), eLam[:], OP.mult), waits=[("eA", A6)])
            V17 = P.op("vector", lambda e: e.scalar_tensor_tensor(
                WTb[0][:], MTb[:], -1.0, tile128(cI), OP.mult, OP.add))
            c_qf[j] = ("eV", V16)
            if cfg.get("sk", 99) <= 3:
                P.kill()
            cur = 0
            lastPTev = V15
            lastW = V17
            lastPn = 0
            psHfree = ("eV", V10)
            for lev in range(1, NLEV + 1):
                nxt = lev % 2
                Ps = Mb if lev == 1 else Pb[cur]
                PTs = MTb if lev == 1 else PTb[cur]
                ml = 0
                for vh in range(VHL):
                    ml = P.op("tensor", lambda e, vh=vh, Ps=Ps, PTs=PTs: e.matmul(
                        psH[:, vh * 128:(vh + 1) * 128], PTs[:, vh * 128:(vh + 1) * 128],
                        Ps[:, vh * 128:(vh + 1) * 128], start=True, stop=True),
                        waits=[psHfree, ("eV", V14 if lev == 1 else lastW)] if vh == 0 else [])
                vPn = P.op("vector", lambda e, nxt=nxt: e.tensor_copy(Pb[nxt][:], psH[:, 0:VW]),
                           waits=[("eT", ml)])
                if lev < NLEV:
                    ml2 = 0
                    for vh in range(VHL):
                        ml2 = P.op("tensor", lambda e, vh=vh, Ps=Ps, PTs=PTs: e.matmul(
                            psF[:, vh * 128:(vh + 1) * 128], Ps[:, vh * 128:(vh + 1) * 128],
                            PTs[:, vh * 128:(vh + 1) * 128], start=True, stop=True),
                            waits=[("eV", V15 if lev == 1 else lastPTev), ("eV", V16),
                                   ("eA", A6)] if vh == 0 else [])
                    lastPTev_ = P.op("vector", lambda e, nxt=nxt: e.tensor_copy(
                        PTb[nxt][:], psF[:, 0:VW]), waits=[("eT", ml2)])
                    lastPTev = lastPTev_
                wl = 0
                for vh in range(VHL):
                    wl = P.op("tensor", lambda e, vh=vh, nxt=nxt, cur=cur: e.matmul(
                        psH[:, vh * 128:(vh + 1) * 128], Pb[nxt][:, vh * 128:(vh + 1) * 128],
                        WTb[cur][:, vh * 128:(vh + 1) * 128], start=True, stop=True),
                        waits=[("eV", vPn)] if vh == 0 else [])
                lastW = P.op("vector", lambda e, nxt=nxt, cur=cur: e.tensor_tensor(
                    WTb[nxt][:], WTb[cur][:], psH[:, 0:VW], OP.add), waits=[("eT", wl)])
                psHfree = ("eV", lastW)
                cur = nxt
            WTf = WTb[cur]
            if cfg.get("sk", 99) <= 4:
                P.kill()
            ksl = 0
            for vh in range(VHL):
                kl = vh // 2
                ksl = P.op("tensor", lambda e, vh=vh, kl=kl, j=j: e.matmul(
                    psF[:, vh * 128:(vh + 1) * 128], kf2[j][:, kl * 128:(kl + 1) * 128],
                    Sbf[:, vh * 128:(vh + 1) * 128], start=True, stop=True),
                    waits=[("eV", lastPTev), ("eV", prev_Sbf), ("eV", V16)] if vh == 0 else [])
            V18 = P.op("vector", lambda e, j=j: e.tensor_tensor(
                t1f[:], vtm2[j][:], bcv(bass_col(gbt2[j], 0, VHL)) if False else
                bass.AP(gbt2[j], 0, [[GBW, 128], [1, VHL], [0, 128]]), OP.mult),
                waits=[gT2[j], ("eA", A7)])
            c_vtm[j] = ("eV", V18)
            c_gbt[j] = ("eV", V18)
            V19 = P.op("vector", lambda e: e.tensor_tensor(
                t2f[:], bcv(blamn), psF[:, 0:VW], OP.mult),
                waits=[("eT", ksl), ("eA", A7)])
            V20 = P.op("vector", lambda e: e.tensor_tensor(Rb[:], t1f[:], t2f[:], OP.add))
            ul = 0
            for vh in range(VHL):
                ul = P.op("tensor", lambda e, vh=vh, WTf=WTf: e.matmul(
                    psH[:, vh * 128:(vh + 1) * 128], WTf[:, vh * 128:(vh + 1) * 128],
                    Rb[:, vh * 128:(vh + 1) * 128], start=True, stop=True),
                    waits=[("eV", V20), psHfree] if vh == 0 else [])
            V21 = P.op("vector", lambda e: e.tensor_copy(ub[:], psH[:, 0:VW]), waits=[("eT", ul)])
            ol = 0
            for vh in range(VHL):
                P.op("tensor", lambda e, vh=vh: e.matmul(
                    psF[:, vh * 128:(vh + 1) * 128], ub[:, vh * 128:(vh + 1) * 128],
                    Ptb[:, vh * 128:(vh + 1) * 128], start=True, stop=False),
                    waits=[("eV", V21), ("eV", V19)] if vh == 0 else [])
                ol = P.op("tensor", lambda e, vh=vh: e.matmul(
                    psF[:, vh * 128:(vh + 1) * 128], Sbf[:, vh * 128:(vh + 1) * 128],
                    Qlb[:, vh * 128:(vh + 1) * 128], start=False, stop=True))
            if cfg.get("sk", 99) <= 5:
                P.kill()
            V22 = P.op("vector", lambda e, j=j: e.tensor_tensor(
                ogz[:], zf2[j][:], psF[:, 0:VW], OP.mult), waits=[("eT", ol), gLz[j]])
            c_zf[j] = ("eV", V22)
            A8 = P.op("scalar", lambda e: e.activation(sqz[:], ogz[:], AF.Square),
                      waits=[("eV", V22)])
            rl = 0
            for vh in range(VHL):
                rl = P.op("tensor", lambda e, vh=vh: e.matmul(
                    psH[0:1, vh * 128:(vh + 1) * 128], c1cb[:],
                    sqz[:, vh * 128:(vh + 1) * 128], start=True, stop=True),
                    waits=[("eA", A8), ("eV", V21)] if vh == 0 else [])
            A9 = P.op("scalar", lambda e: e.activation(
                rfr[0:1, :], psH[0:1, 0:VW], AF.Ln, scale=1.0 / 128, bias=1e-6),
                waits=[("eT", rl)])
            A10 = P.op("scalar", lambda e: e.activation(
                rfr[0:1, :], rfr[0:1, :], AF.Exp, scale=-0.5))
            bl = 0
            for vh in range(VHL):
                bl = P.op("tensor", lambda e, vh=vh: e.matmul(
                    psH[:, vh * 128:(vh + 1) * 128], c1r[0:1, :],
                    rfr[0:1, vh * 128:(vh + 1) * 128], start=True, stop=True),
                    waits=[("eA", A10)] if vh == 0 else [])
            V23a = P.op("vector", lambda e: e.tensor_tensor(
                t1f[:], ogz[:], psH[:, 0:VW], OP.mult), waits=[("eT", bl)])
            V23 = P.op("vector", lambda e: e.tensor_scalar(
                ogb[:], t1f[:], nwc[:, 0:1], None, OP.mult, OP.bypass),
                waits=[prev_STog])
            prev_STog = P.dma("scalar", lambda e, cg=cg: e.dma_start(
                out=dr["og"].ap()[:, cg * 128:(cg + 1) * 128]
                .rearrange("(vh f) t -> f vh t", vh=VHL), in_=ogb[:]
                .rearrange("p (vh t) -> p vh t", vh=VHL)), waits=[("eV", V23)])
            ogst[cg] = prev_STog
            if cfg.get("sk", 99) <= 6:
                P.kill()
            V24 = P.op("vector", lambda e, j=j: e.tensor_tensor(
                kbb[:], repK(ktm2[j]), bcv(escv), OP.mult),
                waits=[gT1[j], ("eA", A2)])
            c_ktm[j] = ("eV", V24)
            c_kf[j] = ("eT", ksl)
            sul = 0
            for vh in range(VHL):
                sul = P.op("tensor", lambda e, vh=vh: e.matmul(
                    psF[:, vh * 128:(vh + 1) * 128], kbb[:, vh * 128:(vh + 1) * 128],
                    ub[:, vh * 128:(vh + 1) * 128], start=True, stop=True),
                    waits=[("eV", V24), ("eV", V22)] if vh == 0 else [])
            V25 = P.op("vector", lambda e: e.tensor_tensor(
                t2f[:], Sf[:], bcv(lamCv), OP.mult), waits=[("eA", A1)])
            V26 = P.op("vector", lambda e: e.tensor_tensor(
                Sf[:], t2f[:], psF[:, 0:VW], OP.add), waits=[("eT", sul)])
            prev_Sbf = P.op("vector", lambda e: e.tensor_copy(Sbf[:], Sf[:]))
            prev_psF_cons = ("eV", V26)
            prev_psH_cons = ("eT", bl) if False else ("eV", V23)
        og_done = P.qcov("scalar")
        if dbg:
            P.dma("gpsimd", lambda e: e.dma_start(out=db["d_og"].ap(), in_=dr["og"].ap()),
                  waits=P.qcov("scalar"))

        if cfg["phases"] <= 4:
            P.kill()
        P.barrier()
        close_phase()
        sb = newphase()
        wout_sb = sb("w_wout", [128, VHL * D], BF16)
        ogt = [sb(f"w_ogt{j}", [128, VW], BF16) for j in range(2)]
        ssb = [sb(f"w_ssb{j}", [128, D], F32) for j in range(2)]
        wol = P.dma("gpsimd", lambda e: e.dma_start(
            out=wout_sb[:].rearrange("p (vh d) -> p vh d", vh=VHL),
            in_=io["Wout_my"].ap().rearrange("(vh f) d -> f vh d", f=128)))
        ps_of = [psF, psH]
        wev = [Z, Z]
        sst = [Z, Z]
        wst_last = Z
        for tc in range(2 * nchl):
            jj = tc % 2
            ogl = P.dma("sync", lambda e, tc=tc: e.dma_start(
                out=ogt[tc % 2][:].rearrange("p (vh t) -> p vh t", vh=VHL),
                in_=dr["og"].ap()[:, tc * 128:(tc + 1) * 128]
                .rearrange("(vh f) t -> f vh t", f=128)),
                waits=[og_done, wev[jj]])
            mml = 0
            pst = ps_of[jj]
            for dt in range(D // 512):
                for vh in range(VHL):
                    mml = P.op("tensor", lambda e, dt=dt, vh=vh, tc=tc, pst=pst: e.matmul(
                        pst[:, dt * 512:(dt + 1) * 512],
                        ogt[tc % 2][:, vh * 128:(vh + 1) * 128],
                        wout_sb[:, vh * D + dt * 512:vh * D + dt * 512 + 512],
                        start=(vh == 0), stop=(vh == VHL - 1)),
                        waits=[ogl, wol, wev[jj]] if (dt == 0 and vh == 0) else [])
            vev = P.op("vector", lambda e, jj=jj, pst=pst: e.tensor_copy(ssb[jj][:], pst[:, 0:D]),
                       waits=[("eT", mml), sst[jj]])
            wev[jj] = ("eV", vev)
            wst_last = P.dma("scalar", lambda e, tc=tc, jj=jj: e.dma_start(
                out=dr["sp"].ap()[tc * 128:(tc + 1) * 128, :], in_=ssb[jj][:]),
                waits=[("eV", vev)])
            sst[jj] = wst_last
        if dbg:
            P.dma("gpsimd", lambda e: e.dma_start(out=db["d_sp"].ap(), in_=dr["sp"].ap()),
                  waits=P.qcov("scalar"))
        ccC = P.cc(lambda e: e.collective_compute(
            "ReduceScatter", OP.add, replica_groups=[[2 * i, 2 * i + 1] for i in range(B)],
            ins=[dr["sp"].ap().opt()], outs=[dr["smy"].ap().opt()]),
            waits=P.qcov("scalar"))
        if dbg:
            P.dma("gpsimd", lambda e: e.dma_start(out=db["d_smy"].ap(), in_=dr["smy"].ap()),
                  waits=[("cc", ccC)])

        if cfg["phases"] <= 5:
            P.kill()
        P.barrier()
        close_phase()
        sb = newphase()
        fx = [sb(f"fi_x{j}", [128, D], F32) for j in range(2)]
        fs = [sb(f"fi_s{j}", [128, D], F32) for j in range(2)]
        fg = [sb(f"fi_g{j}", [128, D], F32) for j in range(2)]
        fo = [sb(f"fi_o{j}", [128, D], F32) for j in range(2)]
        fev = [Z, Z]
        fst = [Z, Z]
        for tc in range(Tloc // 128):
            jj = tc % 2
            lx = P.dma("sync", lambda e, tc=tc, jj=jj: e.dma_start(
                out=fx[jj][:], in_=io["x_slice"].ap()[3 + tc * 128:3 + (tc + 1) * 128, :]),
                waits=[fev[jj]])
            ls = P.dma("sync", lambda e, tc=tc, jj=jj: e.dma_start(
                out=fs[jj][:], in_=dr["smy"].ap()[tc * 128:(tc + 1) * 128, :]),
                waits=[("cc", ccC)])
            lg_ = P.dma("scalar", lambda e, tc=tc, jj=jj: e.dma_start(
                out=fg[jj][:], in_=dr["gmy"].ap()[tc * 128:(tc + 1) * 128, :]),
                waits=[gmy_done])
            v1_ = P.op("vector", lambda e, jj=jj: e.tensor_tensor(
                fo[jj][:], fs[jj][:], fg[jj][:], OP.mult),
                waits=[ls, lg_, fst[jj]])
            v2_ = P.op("vector", lambda e, jj=jj: e.tensor_tensor(
                fo[jj][:], fo[jj][:], fx[jj][:], OP.add), waits=[lx])
            fev[jj] = ("eV", v2_)
            fst[jj] = P.dma("scalar", lambda e, tc=tc, jj=jj: e.dma_start(
                out=io["out_my"].ap()[tc * 128:(tc + 1) * 128, :], in_=fo[jj][:]),
                waits=[("eV", v2_)])
        if getattr(P, "dead", False):
            P.dead = False
            P.dma("gpsimd", lambda e: e.dma_start(
                out=io["out_my"].ap(), in_=io["x_slice"].ap()[3:, :]))
        P.barrier()

        if cfg.get("dbgcnt"):
            print("PLAN cnt:", {k: v for k, v in P.cnt.items() if v},
                  "steps:", {e: len(s) for e, s in P.steps.items()})
        P.emit(nc, semh)
        close_phase()
    return nc


def bass_col(t, a, b):
    return t[:, a:b]


_CACHE = {}


def kernel(**inputs):
    cfg = mkcfg(mini=False)
    key = "full"
    if key not in _CACHE:
        _CACHE[key] = build(cfg)
    nc = _CACHE[key]
    maps = prep_inputs(inputs, cfg)
    res = run_bass_kernel_spmd(nc, maps, core_ids=list(range(cfg["ncores"])))
    B, Tloc, D = cfg["B"], cfg["Tloc"], cfg["D"]
    out = np.empty((B, 2 * Tloc, D), np.float32)
    for c in range(cfg["ncores"]):
        b, p = c // 2, c % 2
        out[b, p * Tloc:(p + 1) * Tloc] = np.asarray(res.results[c]["out_my"]).reshape(Tloc, D)
    return out



# version 05 (final)
# speedup vs baseline: 1.0287x; 1.0133x over previous
import contextlib
import numpy as np
import ml_dtypes
import concourse.bass as bass
import concourse.mybir as mybir
from concourse.bass_utils import run_bass_kernel_spmd

F32 = mybir.dt.float32
BF16 = mybir.dt.bfloat16
I32 = mybir.dt.int32
AF = mybir.ActivationFunctionType
OP = mybir.AluOpType
AX = mybir.AxisListType
BFNP = ml_dtypes.bfloat16
NEG = -1e30
FPAD = 16


def mkcfg(mini=False):
    c = {}
    if mini:
        c.update(B=1, Tloc=256, D=512, NKH=4, E=4, ED=256, dbg=True)
    else:
        c.update(B=4, Tloc=4096, D=2048, NKH=16, E=8, ED=2048, dbg=False)
    c["C"] = 128
    c["NKHL"] = c["NKH"] // 2
    c["VHL"] = c["NKHL"] * 2
    c["NVH"] = c["NKH"] * 2
    c["KD"] = c["NKH"] * 128
    c["VD"] = c["NVH"] * 128
    c["KCH"] = c["D"] // 128
    c["EDCH"] = c["ED"] // 128
    c["nchl"] = c["Tloc"] // c["C"]
    c["nchg"] = 2 * c["nchl"]
    c["Tpad"] = ((c["Tloc"] + FPAD + 127) // 128) * 128
    c["ncores"] = 2 * c["B"]
    c["SECL"] = c["KD"] + c["VD"]
    c["KW"] = c["NKHL"] * 128
    c["VW"] = c["VHL"] * 128
    c["TB"] = min(1024, c["Tloc"])
    c["ntb"] = c["Tloc"] // c["TB"]
    c["phases"] = 9
    return c


def _consts():
    tt = np.arange(128)[:, None]
    ss = np.arange(128)[None, :]
    return dict(
        I128f=np.eye(128, dtype=np.float32),
        Ibf=np.eye(128, dtype=np.float32).astype(BFNP),
        triUT=np.triu(np.ones((128, 128), np.float32)),
        maskM=np.where(ss < tt, 0.0, NEG).astype(np.float32),
        maskPTpos=np.where(ss >= tt, 0.0, -NEG).astype(np.float32),
        ones_r=np.ones((1, 128), np.float32),
        cvals=np.tile(np.array([[1e-5, 1e-6, -0.5, 1.0 / 128, 0.0, 0.0, 0.0, 0.0]], np.float32), (128, 1)),
        ones_cf=np.ones((128, 1), np.float32),
        ones_cb=np.ones((128, 1), np.float32).astype(BFNP),
    )


def prep_inputs(inputs, cfg):
    B, Tloc, D, NKH, E, ED = (cfg[k] for k in ("B", "Tloc", "D", "NKH", "E", "ED"))
    x = np.ascontiguousarray(inputs["L0_attn_hidden_states"], dtype=np.float32)
    gu = inputs["L0_ffn_gate_up"]
    gu_i = np.ascontiguousarray(
        gu.reshape(E, D, 2, ED // 128, 128).transpose(0, 1, 3, 2, 4)
        .reshape(E, D, 2 * ED)).astype(np.float32)
    cn = _consts()
    SECL, nchl, nchg = cfg["SECL"], cfg["nchl"], cfg["nchg"]
    KW, VW = cfg["KW"], cfg["VW"]
    Wo = np.ascontiguousarray(inputs["L0_attn_W_out"], dtype=np.float32)
    maps = []
    for c in range(cfg["ncores"]):
        b, p = c // 2, c % 2
        t0 = p * Tloc
        xs = np.zeros((Tloc + 3, D), np.float32)
        lo = max(0, t0 - 3)
        xs[3 - (t0 - lo):] = x[b, lo:t0 + Tloc]
        tab = np.zeros((8, 64), np.int64)
        for cg in range(nchg):
            blk, cl = cg // nchl, cg % nchl
            base = (blk * nchl + cl) * (2 * SECL * 128) + p * SECL * 128
            tab[0, cg] = base
            tab[1, cg] = (blk * Tloc + cl * 128) * (6 * cfg["VHL"]) + p * 3 * cfg["VHL"]
            tab[2, cg] = base + KW * 128
            tab[3, cg] = base + KW * 128
            tab[4, cg] = base + (2 * KW + VW) * 128
        for bb in range(2):
            tab[5, bb] = (bb * nchl * 2 + p) * SECL * 128
            tab[6, bb] = bb * Tloc * (6 * cfg["VHL"]) + p * 3 * cfg["VHL"]
        d = dict(
            x_slice=xs,
            Wq=np.ascontiguousarray(inputs["L0_attn_W_qkvz"], dtype=np.float32),
            Wb=np.ascontiguousarray(inputs["L0_attn_W_ba"], dtype=np.float32),
            convw=np.ascontiguousarray(inputs["L0_attn_conv_w"], dtype=np.float32),
            dtb=inputs["L0_attn_dt_bias"].reshape(1, -1).astype(np.float32),
            Alog=inputs["L0_attn_A_log"].reshape(1, -1).astype(np.float32),
            nw=inputs["L0_attn_norm_weight"].reshape(-1, 1).astype(np.float32),
            Wout_my=np.ascontiguousarray(Wo[p * (cfg["VD"] // 2):(p + 1) * (cfg["VD"] // 2)]),
            rsc=inputs["L0_ffn_router_scale"].reshape(-1, 1).astype(np.float32),
            gw=np.ascontiguousarray(inputs["L0_ffn_gate_w"], dtype=np.float32),
            gu=gu_i,
            dn=np.ascontiguousarray(inputs["L0_ffn_down"], dtype=np.float32),
            pes=inputs["L0_ffn_per_expert_scale"].reshape(1, -1).astype(np.float32),
            offs=tab.astype(np.int32),
        )
        d.update(cn)
        maps.append(d)
    return maps


ENGS = ("sync", "gpsimd", "tensor", "vector", "scalar")
QSEM = {"sync": "qS", "gpsimd": "qG", "scalar": "qC"}
ESEM = {"vector": "eV", "scalar": "eA", "tensor": "eT", "gpsimd": "eP"}
NQ = 4
SEMS = (["%s%d" % (q, i) for q in ("qS", "qG", "qC") for i in range(NQ)]
        + ["eV", "eA", "eT", "eP", "cc"])
Z = ("eV", 0)


class Plan:
    def __init__(self):
        self.steps = {e: [] for e in ENGS}
        self.cnt = {s: 0 for s in SEMS}
        self.qn = {}

    def kill(self):
        self.dead = True

    def _add(self, eng, sem, k, emit, waits):
        if getattr(self, "dead", False):
            return self.cnt[sem]
        flat = []
        for w in waits:
            if isinstance(w, list):
                flat.extend(w)
            else:
                flat.append(w)
        self.cnt[sem] += k
        self.steps[eng].append((emit, [w for w in flat if w[1] > 0], sem, k))
        return self.cnt[sem]

    def qcov(self, eng):
        return [("%s%d" % (QSEM[eng], i), self.cnt["%s%d" % (QSEM[eng], i)])
                for i in range(NQ)]

    def dma(self, eng, emit, waits=()):
        if getattr(self, "dead", False):
            return Z
        qn = self.qn.get(eng, 0)
        self.qn[eng] = qn + 1
        sem = "%s%d" % (QSEM[eng], qn % NQ)
        w = [(sem, self.cnt[sem])] + list(waits)
        return (sem, self._add(eng, sem, 16, emit, w))

    def op(self, eng, emit, waits=(), chain=True):
        sem = ESEM[eng]
        w = list(waits)
        if chain and eng in ("vector", "scalar") and self.cnt[sem] > 0:
            w.append((sem, self.cnt[sem]))
        return self._add(eng, sem, 1, emit, w)

    def cc(self, emit, waits=()):
        return self._add("gpsimd", "cc", 1, emit, waits)

    def barrier(self, exclude=()):
        tgt = [(s, v) for s, v in self.cnt.items() if v > 0 and s not in exclude]
        for e in ENGS:
            self.steps[e].append((None, tgt, None, 0))

    def emit(self, nc, semh):
        def runner(engname):
            def run(e):
                waited = {}
                for emitf, waits, sem, k in self.steps[engname]:
                    for snm, val in waits:
                        if waited.get(snm, 0) < val:
                            e.wait_ge(semh[snm], val)
                            waited[snm] = val
                    if emitf is not None:
                        emitf(e).then_inc(semh[sem], k)
            return run
        with nc.Block() as blk:
            blk.sync(runner("sync"))
            blk.gpsimd(runner("gpsimd"))
            blk.tensor(runner("tensor"))
            blk.vector(runner("vector"))
            blk.scalar(runner("scalar"))


def tile_plan(total, width=512):
    if total <= width:
        return [(0, total)]
    starts = [i * width for i in range(total // width)]
    if total % width:
        starts.append(total - width)
    return [(s, width) for s in starts]


def build(cfg):
    B, Tloc, D, NKH, E, ED, C = (cfg[k] for k in ("B", "Tloc", "D", "NKH", "E", "ED", "C"))
    KCH, EDCH, VHL, NKHL = cfg["KCH"], cfg["EDCH"], cfg["VHL"], cfg["NKHL"]
    nchl, nchg, Tpad, SECL = cfg["nchl"], cfg["nchg"], cfg["Tpad"], cfg["SECL"]
    NVH, KD, VD, KW, VW = cfg["NVH"], cfg["KD"], cfg["VD"], cfg["KW"], cfg["VW"]
    TB, ntb = cfg["TB"], cfg["ntb"]
    dbg = cfg["dbg"]
    WBA = 4 * NKH + E
    GBW = 3 * VHL
    NLEV = 6

    nc = bass.Bass("TRN2", target_bir_lowering=False)
    io = {}
    for nm, shp, dt in [
        ("x_slice", [Tloc + 3, D], F32), ("Wq", [D, 6 * NKH * 128], F32),
        ("Wb", [D, 4 * NKH], F32), ("convw", [4 * NKH * 128, 4], F32),
        ("dtb", [1, NVH], F32), ("Alog", [1, NVH], F32), ("nw", [128, 1], F32),
        ("Wout_my", [VD // 2, D], F32), ("rsc", [D, 1], F32), ("gw", [D, E], F32),
        ("gu", [E, D, 2 * ED], F32), ("dn", [E, ED, D], F32), ("pes", [1, E], F32),
        ("offs", [8, 64], I32), ("I128f", [128, 128], F32), ("Ibf", [128, 128], BF16),
        ("triUT", [128, 128], F32), ("maskM", [128, 128], F32),
        ("maskPTpos", [128, 128], F32), ("ones_r", [1, 128], F32),
        ("ones_cf", [128, 1], F32), ("ones_cb", [128, 1], BF16),
        ("cvals", [128, 8], F32),
    ]:
        io[nm] = nc.dram_tensor(nm, shp, dt, kind="ExternalInput")
    io["out_my"] = nc.dram_tensor("out_my", [Tloc, D], F32, kind="ExternalOutput")
    dr = {
        "xn_cc": nc.dram_tensor("xn_cc", [KCH, Tpad, 128], BF16),
        "rfac": nc.dram_tensor("rfac_rows", [Tpad, 1], F32),
        "contrib": nc.dram_tensor("contrib_cb", [nchl, 2 * SECL, 128], BF16),
        "gcb": nc.dram_tensor("gath_cb", [2, nchl, 2 * SECL, 128], BF16),
        "cgb": nc.dram_tensor("contrib_gb", [Tloc, 2 * GBW], F32),
        "ggb": nc.dram_tensor("gath_gb", [2, Tloc, 2 * GBW], F32),
        "scan_loc": nc.dram_tensor("scan_loc", [2 * nchl, SECL, 128], BF16),
        "gb_loc": nc.dram_tensor("gb_loc", [2 * Tloc, GBW], F32),
        "og": nc.dram_tensor("og_fm", [VW, 2 * Tloc], BF16),
        "sp": nc.dram_tensor("spart", [2 * Tloc, D], F32),
        "smy": nc.dram_tensor("s_my", [Tloc, D], F32),
        "gmy": nc.dram_tensor("g_my", [Tloc, D], F32),
    }
    db = {}
    if dbg:
        for nm, ref in [("d_xn", "xn_cc"), ("d_contrib", "contrib"), ("d_cgb", "cgb"),
                        ("d_og", "og"), ("d_sp", "sp"), ("d_smy", "smy"),
                        ("d_gmy", "gmy"), ("d_rfac", "rfac")]:
            t = dr[ref]
            db[nm] = nc.dram_tensor(nm, [int(z) for z in t.shape], t.dtype,
                                    kind="ExternalOutput")
        db["d_act"] = nc.dram_tensor("d_act", [128, cfg["EDCH"] * cfg["TB"]], BF16,
                                     kind="ExternalOutput")
        db["d_cfm"] = nc.dram_tensor("d_cfm", [8, Tloc], F32, kind="ExternalOutput")
        db["d_logit"] = nc.dram_tensor("d_logit", [Tloc, 8], F32, kind="ExternalOutput")
        db["d_acc"] = nc.dram_tensor("d_acc", [128, (cfg["TB"] // 128) * cfg["D"]], F32,
                                     kind="ExternalOutput")

    P = Plan()
    es = contextlib.ExitStack()
    phase_es = []

    def close_phase():
        while phase_es:
            phase_es.pop().close()

    def newphase():
        st = contextlib.ExitStack()
        phase_es.append(st)
        return lambda nm, shp, dt: st.enter_context(nc.sbuf_tensor(nm, shp, dt))

    with es:
        semh = {s: es.enter_context(nc.semaphore(s)) for s in SEMS}
        psF = es.enter_context(nc.psum_tensor("psF", [128, 2048], F32))
        psH = es.enter_context(nc.psum_tensor("psH", [128, 2048], F32))
        sbg = lambda nm, shp, dt: es.enter_context(nc.sbuf_tensor(nm, shp, dt))

        cI = sbg("cI", [128, 128], F32)
        cIb = sbg("cIb", [128, 128], BF16)
        ctri = sbg("ctri", [128, 128], F32)
        cmM = sbg("cmM", [128, 128], F32)
        cmPT = sbg("cmPT", [128, 128], F32)
        c1r = sbg("c1r", [1, 128], F32)
        c1cf = sbg("c1cf", [128, 1], F32)
        c1cb = sbg("c1cb", [128, 1], BF16)
        nwc = sbg("nwc", [128, 1], F32)
        dtbb = sbg("dtbb", [128, NVH], F32)
        negeA = sbg("negeA", [128, NVH], F32)
        pesb = sbg("pesb", [128, E], F32)
        cfm = sbg("cfm", [8, Tloc], BF16)
        offsb = sbg("offsb", [1, 512], I32)
        Sf = sbg("Sf", [128, VW], F32)
        Sbf = sbg("Sbf", [128, VW], BF16)
        cvsb = sbg("cvsb", [128, 8], F32)
        for ci_, cv_ in enumerate([1e-5, 1e-6, -0.5, 1.0 / 128]):
            nc.const_aps.register(F32, cv_, cvsb[:, ci_:ci_ + 1])

        sb = newphase()
        small = sb("small", [1, 2 * NVH + E], F32)
        for h, tnm in [(cI, "I128f"), (cIb, "Ibf"), (ctri, "triUT"), (cmM, "maskM"),
                       (cmPT, "maskPTpos"), (c1r, "ones_r"), (c1cf, "ones_cf"),
                       (c1cb, "ones_cb"), (nwc, "nw")]:
            P.dma("sync", lambda e, h=h, t=io[tnm]: e.dma_start(out=h[:], in_=t.ap()))
        P.dma("sync", lambda e: e.dma_start(out=small[0:1, 0:NVH], in_=io["dtb"].ap()))
        P.dma("sync", lambda e: e.dma_start(out=small[0:1, NVH:2 * NVH], in_=io["Alog"].ap()))
        l3 = P.dma("sync", lambda e: e.dma_start(out=small[0:1, 2 * NVH:], in_=io["pes"].ap()))
        P.dma("sync", lambda e: e.dma_start(
            out=offsb[0:1, 0:512].rearrange("o (r c) -> o r c", r=8),
            in_=io["offs"].ap()))
        P.dma("sync", lambda e: e.dma_start(out=cvsb[:], in_=io["cvals"].ap()))
        P.op("tensor", lambda e: e.matmul(psF[:, 0:NVH], c1r[0:1, :], small[0:1, 0:NVH],
                                          start=True, stop=True), waits=P.qcov("sync"))
        P.op("tensor", lambda e: e.matmul(psF[:, 256:256 + NVH], c1r[0:1, :],
                                          small[0:1, NVH:2 * NVH], start=True, stop=True))
        t3 = P.op("tensor", lambda e: e.matmul(psF[:, 512:512 + E], c1r[0:1, :],
                                               small[0:1, 2 * NVH:], start=True, stop=True))
        a1 = P.op("scalar", lambda e: e.activation(negeA[:], psF[:, 256:256 + NVH], AF.Exp),
                  waits=[("eT", t3)])
        P.op("vector", lambda e: e.tensor_copy(dtbb[:], psF[:, 0:NVH]), waits=[("eT", t3)])
        ini_pes = P.op("vector", lambda e: e.tensor_copy(pesb[:], psF[:, 512:512 + E]))
        P.op("vector", lambda e: e.tensor_scalar(negeA[:], negeA[:], -1.0, None,
                                                 OP.mult, OP.bypass), waits=[("eA", a1)])
        P.op("vector", lambda e: e.memset(Sf[:], 0.0))
        ini_done = P.op("vector", lambda e: e.memset(Sbf[:], 0.0))
        psF_free = ("eV", ini_pes)

        xa = sb("xa", [128, D], F32)
        xnb = sb("xnb", [128, D], BF16)
        stat = sb("stat", [128, 12], F32)
        sqt = sb("sqt", [128, D], F32)
        ntile = (Tloc + 3 + 127) // 128
        pv6 = pa1 = 0
        pxst = prst = Z
        vz1 = P.op("vector", lambda e: e.memset(xnb[:], 0.0))
        vz2 = P.op("vector", lambda e: e.memset(stat[:, 11:12], 0.0))
        npad_t = Tpad - 16 - Tloc
        for r0p, nrp in [(0, 13), (16 + Tloc, npad_t)]:
            if nrp <= 0:
                continue
            pxst = P.dma("gpsimd", lambda e, r0p=r0p, nrp=nrp: e.dma_start(
                out=dr["xn_cc"].ap()[:, r0p:r0p + nrp, :].transpose([1, 0, 2]),
                in_=xnb[0:nrp, :].rearrange("p (k f) -> p k f", k=KCH)),
                waits=[("eV", vz1)])
            prst = P.dma("gpsimd", lambda e, r0p=r0p, nrp=nrp: e.dma_start(
                out=dr["rfac"].ap()[r0p:r0p + nrp, :], in_=stat[0:nrp, 11:12]),
                waits=[("eV", vz2)])
        pxst = P.qcov("gpsimd")
        prst = P.qcov("gpsimd")
        for i in range(ntile):
            r0 = i * 128
            nr = min(128, Tloc + 3 - r0)
            ld = P.dma("sync", lambda e, r0=r0, nr=nr: e.dma_start(
                out=xa[0:nr, :], in_=io["x_slice"].ap()[r0:r0 + nr, :]),
                waits=[("eV", pv6), ("eA", pa1)])
            pa1 = P.op("scalar", lambda e, nr=nr: e.activation(
                sqt[0:nr, :], xa[0:nr, :], AF.Square, accum_out=stat[0:nr, 1:2]),
                waits=[ld])
            v = P.op("vector", lambda e, nr=nr: e.reduce_sum(stat[0:nr, 0:1], xa[0:nr, :], AX.X),
                     waits=[ld, pxst])
            v = P.op("vector", lambda e, nr=nr: e.tensor_scalar(
                stat[0:nr, 2:3], stat[0:nr, 0:1], 1.0 / D, None, OP.mult, OP.bypass))
            v = P.op("vector", lambda e, nr=nr: e.tensor_tensor(
                stat[0:nr, 3:4], stat[0:nr, 2:3], stat[0:nr, 2:3], OP.mult))
            v4 = P.op("vector", lambda e, nr=nr: e.scalar_tensor_tensor(
                stat[0:nr, 4:5], stat[0:nr, 1:2], 1.0 / D, stat[0:nr, 3:4],
                OP.mult, OP.subtract), waits=[("eA", pa1)])
            a2 = P.op("scalar", lambda e, nr=nr: e.activation(
                stat[0:nr, 5:6], stat[0:nr, 4:5], AF.Sqrt, bias=1e-5), waits=[("eV", v4)])
            v = P.op("vector", lambda e, nr=nr: e.reciprocal(stat[0:nr, 6:7], stat[0:nr, 5:6]),
                     waits=[("eA", a2)])
            pv6 = P.op("vector", lambda e, nr=nr: e.tensor_scalar(
                xnb[0:nr, :], xa[0:nr, :], stat[0:nr, 2:3], stat[0:nr, 6:7],
                OP.subtract, OP.mult))
            v = P.op("vector", lambda e, nr=nr: e.tensor_tensor(
                stat[0:nr, 7:8], stat[0:nr, 4:5], stat[0:nr, 6:7], OP.mult))
            v8 = P.op("vector", lambda e, nr=nr: e.tensor_tensor(
                stat[0:nr, 8:9], stat[0:nr, 7:8], stat[0:nr, 6:7], OP.mult))
            a3 = P.op("scalar", lambda e, nr=nr: e.activation(
                stat[0:nr, 9:10], stat[0:nr, 8:9], AF.Sqrt, bias=1e-6), waits=[("eV", v8)])
            v9 = P.op("vector", lambda e, nr=nr: e.reciprocal(
                stat[0:nr, 10:11], stat[0:nr, 9:10]), waits=[("eA", a3), prst])
            pxst = P.dma("gpsimd", lambda e, r0=r0, nr=nr: e.dma_start(
                out=dr["xn_cc"].ap()[:, 13 + r0:13 + r0 + nr, :].transpose([1, 0, 2]),
                in_=xnb[0:nr, :].rearrange("p (k f) -> p k f", k=KCH)),
                waits=[("eV", pv6)])
            prst = P.dma("gpsimd", lambda e, r0=r0, nr=nr: e.dma_start(
                out=dr["rfac"].ap()[13 + r0:13 + r0 + nr, :], in_=stat[0:nr, 10:11]),
                waits=[("eV", v9)])
        if dbg:
            P.dma("gpsimd", lambda e: e.dma_start(out=db["d_xn"].ap(), in_=dr["xn_cc"].ap()),
                  waits=P.qcov("gpsimd"))
            P.dma("gpsimd", lambda e: e.dma_start(out=db["d_rfac"].ap(), in_=dr["rfac"].ap()),
                  waits=P.qcov("gpsimd"))

        if cfg["phases"] <= 0:
            P.kill()
        P.barrier()
        close_phase()
        sb = newphase()
        XNR = FPAD + Tloc
        xnfm = [sb(f"xnfm{k}", [128, XNR], BF16) for k in range(KCH)]
        wsb = [sb(f"wsb{j}", [128, KCH * 128], BF16) for j in range(2)]
        wbag = sb("wbag", [128, KCH * WBA], BF16)
        gwkb = [sb(f"gwk{j}", [128, E], F32) for j in range(2)]
        rscf = sb("rscf", [128, KCH], F32)
        cwsb = [sb(f"cwsb{j}", [128, 4], F32) for j in range(2)]
        mixf = sb("mixf", [128, Tloc + 3], BF16)
        cacc = sb("cacc", [128, Tloc], F32)
        rrow = sb("rrow", [1, 1024], F32)
        outb = sb("outb", [128, Tloc], BF16)
        bat = sb("bat", [128, WBA], F32)
        btmp = sb("btmp", [128, max(2 * NVH, 6 * E + 8)], F32)
        spsc = sb("spsc", [128, NVH], F32)
        spb = sb("spb", [128, NVH], F32)
        gbp = sb("gbp", [128, 2 * GBW], F32)
        rfc = sb("rfc", [128, 1], F32)
        ctile = sb("ctile", [128, 8], F32)

        xl = [P.dma("sync", lambda e, k=k: e.dma_start(
            out=xnfm[k][:], in_=dr["xn_cc"].ap()[k][0:XNR, :], transpose=True))
              for k in range(KCH)]
        xl_all = P.qcov("sync")
        wbld = P.dma("gpsimd", lambda e: e.dma_start(
            out=wbag[:].rearrange("p (k f) -> p k f", f=WBA)[:, :, 0:4 * NKH],
            in_=io["Wb"].ap().rearrange("(k p) f -> p k f", p=128)))
        rscld = P.dma("gpsimd", lambda e: e.dma_start(
            out=rscf[:], in_=io["rsc"].ap().rearrange("(k p) o -> p k o", p=128),
            allow_slow_non_contiguous=True))
        foldv = {}
        for k in range(KCH):
            gld = P.dma("scalar", lambda e, k=k: e.dma_start(
                out=gwkb[k % 2][:], in_=io["gw"].ap()[k * 128:(k + 1) * 128, :]),
                waits=[("eV", foldv.get(k - 2, 0))])
            foldv[k] = P.op("vector", lambda e, k=k: e.tensor_scalar(
                wbag[:, k * WBA + 4 * NKH:(k + 1) * WBA], gwkb[k % 2][:],
                rscf[:, k:k + 1], float(D) ** -0.5, OP.mult, OP.mult),
                waits=[gld, rscld, wbld])
        fold_all = foldv[KCH - 1]

        PSB = 1536
        gbst = {}
        logit_dump = {}
        prev_slot_ev = 0
        prev_ct_ev = 0
        for i in range(nchl):
            col0 = FPAD + i * 128
            mml = 0
            for k in range(KCH):
                w0 = [xl_all, ("eV", fold_all), ("eV", prev_slot_ev),
                      ("eV", prev_ct_ev)] if k == 0 else []
                mml = P.op("tensor", lambda e, k=k, col0=col0: e.matmul(
                    psH[:, PSB:PSB + WBA], xnfm[k][:, col0:col0 + 128],
                    wbag[:, k * WBA:(k + 1) * WBA], start=(k == 0), stop=(k == KCH - 1)),
                    waits=w0)
            rfl = P.dma("scalar", lambda e, i=i: e.dma_start(
                out=rfc[:], in_=dr["rfac"].ap()[FPAD + i * 128:FPAD + (i + 1) * 128, :]),
                waits=[prst, ("eA", 0 if i == 0 else rfl_used)])
            vb = P.op("vector", lambda e: e.tensor_copy(bat[:], psH[:, PSB:PSB + WBA]),
                      waits=[("eT", mml)])
            prev_slot_ev = vb
            lg0 = 4 * NKH
            vcl = P.op("vector", lambda e: e.tensor_copy(btmp[:, 0:E], bat[:, lg0:lg0 + E]),
                       waits=[logit_dump.get(i - 1, Z)])
            bsel = bat[:, 0:4 * NKH].rearrange("p (h f) -> p h f", f=4)[:, :, 0:2]
            asel = bat[:, 0:4 * NKH].rearrange("p (h f) -> p h f", f=4)[:, :, 2:4]

            def gsl(sec):
                return bass.AP(gbp, sec * VHL, [[2 * GBW, 128], [GBW, 2], [1, VHL]])
            vbet = P.op("vector", lambda e: e.tensor_copy(spsc[:, 0:NVH], bsel))
            abeta = P.op("scalar", lambda e: e.activation(gsl(0), spsc[:], AF.Sigmoid),
                         waits=[("eV", vbet), gbst.get(i - 1, Z)])
            aab = P.op("scalar", lambda e: e.activation(spb[:], spsc[:], AF.Abs),
                       waits=[("eV", vcl)])
            ae1 = P.op("scalar", lambda e: e.activation(spb[:], spb[:], AF.Exp, scale=-1.0))
            ae2 = P.op("scalar", lambda e: e.activation(spb[:], spb[:], AF.Ln, bias=1.0))
            vmx = P.op("vector", lambda e: e.tensor_scalar(
                spsc[:], spsc[:], -1.0, 0.0, OP.mult, OP.max), waits=[("eA", aab)])
            vlnb = P.op("vector", lambda e: e.scalar_tensor_tensor(
                gsl(2), spsc[:], -1.0, spb[:], OP.mult, OP.subtract),
                waits=[("eA", ae2)])
            vas = P.op("vector", lambda e: e.tensor_tensor(spsc[:], asel, dtbb[:], OP.add))
            aab2 = P.op("scalar", lambda e: e.activation(spb[:], spsc[:], AF.Abs),
                        waits=[("eV", vas)])
            ae3 = P.op("scalar", lambda e: e.activation(spb[:], spb[:], AF.Exp, scale=-1.0))
            ae4 = P.op("scalar", lambda e: e.activation(spb[:], spb[:], AF.Ln, bias=1.0))
            vmx2 = P.op("vector", lambda e: e.tensor_scalar(
                spsc[:], spsc[:], 0.0, None, OP.max, OP.bypass), waits=[("eA", aab2)])
            vsp = P.op("vector", lambda e: e.tensor_tensor(
                spsc[:], spsc[:], spb[:], OP.add), waits=[("eA", ae4)])
            vg = P.op("vector", lambda e: e.tensor_tensor(gsl(1), spsc[:], negeA[:], OP.mult))
            gbst[i] = P.dma("scalar", lambda e, i=i: e.dma_start(
                out=dr["cgb"].ap()[i * 128:(i + 1) * 128, :], in_=gbp[:]),
                waits=[("eV", vg)])
            if dbg:
                logit_dump[i] = P.dma("gpsimd", lambda e, i=i: e.dma_start(
                    out=db["d_logit"].ap()[i * 128:(i + 1) * 128, 0:E], in_=btmp[:, 0:E]),
                    waits=[("eV", vcl)])
            m1 = P.op("vector", lambda e: e.reduce_max(btmp[:, 5 * E:5 * E + 1], btmp[:, 0:E], AX.X),
                      waits=[logit_dump.get(i - 1, Z)] if dbg else ())
            e1v = P.op("vector", lambda e: e.tensor_scalar(
                btmp[:, 2 * E:3 * E], btmp[:, 0:E], btmp[:, 5 * E:5 * E + 1], None, OP.is_ge, OP.bypass))
            msk = P.op("vector", lambda e: e.scalar_tensor_tensor(
                btmp[:, 3 * E:4 * E], btmp[:, 2 * E:3 * E], NEG, btmp[:, 0:E], OP.mult, OP.add))
            m2 = P.op("vector", lambda e: e.reduce_max(btmp[:, 5 * E + 1:5 * E + 2], btmp[:, 3 * E:4 * E], AX.X))
            e2v = P.op("vector", lambda e: e.tensor_scalar(
                btmp[:, 4 * E:5 * E], btmp[:, 3 * E:4 * E], btmp[:, 5 * E + 1:5 * E + 2], None,
                OP.is_ge, OP.bypass))
            d12 = P.op("vector", lambda e: e.tensor_tensor(
                btmp[:, 5 * E + 2:5 * E + 3], btmp[:, 5 * E:5 * E + 1], btmp[:, 5 * E + 1:5 * E + 2], OP.subtract))
            w1 = P.op("scalar", lambda e: e.activation(
                btmp[:, 5 * E + 3:5 * E + 4], btmp[:, 5 * E + 2:5 * E + 3], AF.Sigmoid, scale=rfc[:, 0:1]),
                waits=[("eV", d12), rfl])
            rfl_used = w1
            w2 = P.op("vector", lambda e: e.tensor_scalar(
                btmp[:, 5 * E + 4:5 * E + 5], btmp[:, 5 * E + 3:5 * E + 4], -1.0, 1.0, OP.mult, OP.add),
                waits=[("eA", w1)])
            cb1 = P.op("vector", lambda e: e.tensor_scalar(
                btmp[:, 2 * E:3 * E], btmp[:, 2 * E:3 * E], btmp[:, 5 * E + 3:5 * E + 4], None,
                OP.mult, OP.bypass))
            cb2 = P.op("vector", lambda e: e.scalar_tensor_tensor(
                btmp[:, 4 * E:5 * E], btmp[:, 4 * E:5 * E], btmp[:, 5 * E + 4:5 * E + 5],
                btmp[:, 2 * E:3 * E], OP.mult, OP.add))
            cpes = P.op("vector", lambda e: e.tensor_tensor(
                ctile[:, 0:E], btmp[:, 4 * E:5 * E], pesb[:], OP.mult))
            ctr = P.op("tensor", lambda e: e.transpose(
                psH[0:E, PSB + 256:PSB + 384], ctile[:, 0:E], cI[:]),
                waits=[("eV", cpes), ("eV", prev_ct_ev)])
            prev_ct_ev = P.op("vector", lambda e, i=i: e.tensor_copy(
                cfm[0:E, i * 128:(i + 1) * 128], psH[0:E, PSB + 256:PSB + 384]),
                waits=[("eT", ctr)])
        if dbg:
            P.dma("gpsimd", lambda e: e.dma_start(out=db["d_cgb"].ap(), in_=dr["cgb"].ap()),
                  waits=P.qcov("scalar"))
            P.dma("gpsimd", lambda e: e.dma_start(out=db["d_cfm"].ap()[0:E, :], in_=cfm[0:E, :]),
                  waits=[("eV", prev_ct_ev)])

        TPq = tile_plan(Tloc + 3)
        TPz = tile_plan(Tloc)
        L2G = tile_plan(Tloc, 1024)
        nfc = 6 * NKH
        slot_gate = [Z, Z, Z, Z]
        mm_done = {}
        cw_used = {}
        last_store = Z
        cacc_free = ("eV", 0)
        sqb_free = ("eT", 0)
        rrow_free = ("eT", 0)
        psH_l2_free = [Z, Z]

        def secrows(kh, sec):
            rank, khl = kh // NKHL, kh % NKHL
            if sec == 0:
                return rank * SECL + khl * 128
            if sec == 1:
                return rank * SECL + KW + khl * 128
            if sec in (2, 3):
                return rank * SECL + 2 * KW + (2 * khl + sec - 2) * 128
            return rank * SECL + 2 * KW + VW + (2 * khl + sec - 4) * 128

        def convrow(kh, sec):
            return kh * 128 if sec == 0 else (KD + kh * 128 if sec == 1
                                              else 2 * KD + (2 * kh + sec - 2) * 128)

        wld = {}
        for fc in range(nfc):
            kh, sec = fc // 6, fc % 6
            col0 = kh * 6 * 128 + sec * 128
            wld[fc] = P.dma("gpsimd", lambda e, col0=col0, fc=fc: e.dma_start(
                out=wsb[fc % 2][:],
                in_=io["Wq"].ap()[:, col0:col0 + 128].rearrange("(k p) f -> p k f", p=128)),
                waits=[("eT", mm_done.get(fc - 2, 0))])
            if sec < 4:
                cwld = P.dma("scalar", lambda e, kh=kh, sec=sec, fc=fc: e.dma_start(
                    out=cwsb[fc % 2][:],
                    in_=io["convw"].ap()[convrow(kh, sec):convrow(kh, sec) + 128, :]),
                    waits=[("eV", cw_used.get(fc - 2, 0))])
            tiles = TPq if sec < 4 else TPz
            mml = 0
            last_ev = Z
            for ti, (t0_, tn) in enumerate(tiles):
                slot = ti % 4
                for k in range(KCH):
                    src_off = 13 + t0_ if sec < 4 else FPAD + t0_
                    w0 = ([wld[fc], xl_all, slot_gate[slot]]
                          if k == 0 else [])
                    mml = P.op("tensor", lambda e, k=k, slot=slot, so=src_off, tn=tn, fc=fc: e.matmul(
                        psF[:, slot * 512:slot * 512 + tn],
                        wsb[fc % 2][:, k * 128:(k + 1) * 128],
                        xnfm[k][:, so:so + tn], start=(k == 0), stop=(k == KCH - 1)),
                        waits=w0)
                if sec < 4:
                    ev = P.op("vector", lambda e, slot=slot, t0_=t0_, tn=tn: e.tensor_copy(
                        mixf[:, t0_:t0_ + tn], psF[:, slot * 512:slot * 512 + tn]),
                        waits=[("eT", mml), cacc_free if ti == 0 else Z,
                               sqb_free if ti == 0 else Z])
                    slot_gate[slot] = ("eV", ev)
                else:
                    ev = P.op("scalar", lambda e, slot=slot, t0_=t0_, tn=tn: e.activation(
                        outb[:, t0_:t0_ + tn], psF[:, slot * 512:slot * 512 + tn], AF.Silu),
                        waits=[("eT", mml), last_store if ti == 0 else Z])
                    slot_gate[slot] = ("eA", ev)
                last_ev = slot_gate[slot]
            mm_done[fc] = mml
            if sec < 4:
                c1 = P.op("vector", lambda e, fc=fc: e.tensor_scalar(
                    cacc[:], mixf[:, 0:Tloc], cwsb[fc % 2][:, 0:1], None, OP.mult, OP.bypass),
                    waits=[cwld, last_ev, cacc_free])
                for j in range(1, 4):
                    c1 = P.op("vector", lambda e, j=j, fc=fc: e.scalar_tensor_tensor(
                        cacc[:], mixf[:, j:j + Tloc], cwsb[fc % 2][:, j:j + 1], cacc[:],
                        OP.mult, OP.add))
                cw_used[fc] = c1
                if sec < 2:
                    asil = P.op("scalar", lambda e: e.activation(cacc[:], cacc[:], AF.Silu),
                                waits=[("eV", c1)])
                    asq = P.op("scalar", lambda e: e.activation(mixf[:, 0:Tloc], cacc[:], AF.Square),
                               waits=[sqb_free, ("eV", c1)])
                    qsc = float(128) ** -0.5 if sec == 0 else 1.0
                    vout = 0
                    for gi, (g0_, gn) in enumerate(L2G):
                        nsub = max(1, gn // 512)
                        subw = min(512, gn)
                        rowl = 0
                        for si in range(nsub):
                            rowl = P.op("tensor", lambda e, g0_=g0_, si=si, subw=subw: e.matmul(
                                psH[0:1, si * 512:si * 512 + subw], c1cb[:],
                                mixf[:, g0_ + si * 512:g0_ + si * 512 + subw],
                                start=True, stop=True),
                                waits=([("eA", asq), psH_l2_free[0], psH_l2_free[1], rrow_free]
                                       if si == 0 else []))
                        arow = P.op("scalar", lambda e, gn=gn: e.activation(
                            rrow[0:1, 0:gn], psH[0:1, 0:gn], AF.Sqrt, bias=1e-6),
                            waits=[("eT", rowl)])
                        vrec = P.op("vector", lambda e, gn=gn: e.reciprocal(
                            rrow[0:1, 0:gn], rrow[0:1, 0:gn]), waits=[("eA", arow)])
                        if qsc != 1.0:
                            vrec = P.op("vector", lambda e, gn=gn, qsc=qsc: e.tensor_scalar(
                                rrow[0:1, 0:gn], rrow[0:1, 0:gn], qsc, None, OP.mult, OP.bypass))
                        bcl = 0
                        for si in range(nsub):
                            bcl = P.op("tensor", lambda e, si=si, subw=subw: e.matmul(
                                psH[:, si * 512:si * 512 + subw], c1r[0:1, :],
                                rrow[0:1, si * 512:si * 512 + subw], start=True, stop=True),
                                waits=[("eV", vrec)] if si == 0 else [])
                        rrow_free = ("eT", bcl)
                        vout = P.op("vector", lambda e, g0_=g0_, gn=gn: e.tensor_tensor(
                            outb[:, g0_:g0_ + gn], cacc[:, g0_:g0_ + gn], psH[:, 0:gn], OP.mult),
                            waits=[("eT", bcl), last_store if gi == 0 else Z])
                        psH_l2_free[gi % 2] = ("eV", vout)
                    st_gate = ("eV", vout)
                    cacc_free = ("eV", vout)
                    sqb_free = ("eT", rowl)
                else:
                    dsil = P.op("scalar", lambda e: e.activation(outb[:], cacc[:], AF.Silu),
                                waits=[("eV", c1), last_store])
                    st_gate = ("eA", dsil)
                    cacc_free = ("eA", dsil)
            else:
                st_gate = last_ev
            rb = secrows(kh, sec)
            last_store = P.dma("scalar", lambda e, rb=rb: e.dma_start(
                out=dr["contrib"].ap()[:, rb:rb + 128, :].transpose([1, 0, 2]),
                in_=outb[:].rearrange("p (c t) -> p c t", c=nchl)), waits=[st_gate])
        contrib_done = last_store
        gb_done = gbst[nchl - 1]
        if dbg:
            P.dma("gpsimd", lambda e: e.dma_start(out=db["d_contrib"].ap(), in_=dr["contrib"].ap()),
                  waits=P.qcov("scalar"))

        if cfg["phases"] <= 1:
            P.kill()
        ccA = P.cc(lambda e: e.collective_compute(
            "AllGather", OP.bypass, replica_groups=[[2 * i, 2 * i + 1] for i in range(B)],
            ins=[dr["contrib"].ap().opt()], outs=[dr["gcb"].ap().opt()]),
            waits=P.qcov("scalar"))
        ccB = P.cc(lambda e: e.collective_compute(
            "AllGather", OP.bypass, replica_groups=[[2 * i, 2 * i + 1] for i in range(B)],
            ins=[dr["cgb"].ap().opt()], outs=[dr["ggb"].ap().opt()]),
            waits=P.qcov("scalar"))

        if cfg["phases"] <= 2:
            P.kill()
        P.barrier(exclude=("cc",))
        close_phase()
        sb = newphase()
        xnt = [sb(f"f_xnt{k}", [128, TB], BF16) for k in range(KCH)]
        gup = [sb(f"f_gup{j}", [128, KCH * 256], BF16) for j in range(2)]
        DNH = min(512, D)
        EDH = max(1, EDCH // 2)
        nhf = EDCH // EDH
        dnb = [sb(f"f_dnb{j}", [128, EDH * DNH], BF16) for j in range(2)]
        actb = sb("f_act", [128, EDCH * TB], BF16)
        acc = sb("f_acc", [128, (TB // 128) * D], F32)
        ggb_ = sb("f_gg", [128, 512], BF16)
        gg2 = sb("f_gg2", [128, 512], BF16)
        crw = sb("f_crw", [1, TB], F32)
        cbc = sb("f_cbc", [128, TB], BF16)
        GH = D // 2
        gtmp = sb("f_gtmp", [128, GH], F32)
        ndh = D // DNH
        TT512 = tile_plan(TB)
        gate_slot = [Z, Z, Z, Z]
        dn_slot = [Z, Z]
        gu_mm_done = {"last": 0}
        gup_free = [0, 0]
        dn_mm_done = [0, 0]
        cbl_prev = 0
        acc_free = 0
        vg2_prev = 0
        act_dump_gate = Z

        gst_last = Z
        for tb in range(ntb):
            xwait = [pxst, ("eT", gu_mm_done["last"])]
            xld = [P.dma("sync", lambda e, k=k, tb=tb: e.dma_start(
                out=xnt[k][:], in_=dr["xn_cc"].ap()[k][FPAD + tb * TB:FPAD + (tb + 1) * TB, :],
                transpose=True), waits=xwait) for k in range(KCH)]
            xld_all = P.qcov("sync")
            if cfg["phases"] <= 2.05:
                P.kill()
            for e_ in range(E):
                crl = 0
                cbl = 0
                cm_gate = cbl_prev
                for si, (s0, sn) in enumerate(TT512):
                    cm1 = P.op("tensor", lambda e, e_=e_, tb=tb, s0=s0, sn=sn: e.matmul(
                        psH[0:1, 1536:1536 + sn], cIb[0:E, e_:e_ + 1],
                        cfm[0:E, tb * TB + s0:tb * TB + s0 + sn], start=True, stop=True),
                        waits=[("eV", cm_gate)] if si == 0 else [("eV", cbl)])
                    crl = P.op("vector", lambda e, s0=s0, sn=sn: e.tensor_copy(
                        crw[0:1, s0:s0 + sn], psH[0:1, 1536:1536 + sn]), waits=[("eT", cm1)])
                    cm2 = P.op("tensor", lambda e, s0=s0, sn=sn: e.matmul(
                        psH[:, 1536:1536 + sn], c1r[0:1, :], crw[0:1, s0:s0 + sn],
                        start=True, stop=True), waits=[("eV", crl)])
                    cbl = P.op("vector", lambda e, s0=s0, sn=sn: e.tensor_copy(
                        cbc[:, s0:s0 + sn], psH[:, 1536:1536 + sn]), waits=[("eT", cm2)])
                    if cfg["phases"] <= 2.1:
                        P.kill()
                cbl_prev = cbl
                if cfg["phases"] <= 2.15:
                    P.kill()
                for fp in range(EDCH):
                    gld = P.dma("gpsimd", lambda e, e_=e_, fp=fp, tb=tb: e.dma_start(
                        out=gup[fp % 2][:],
                        in_=io["gu"].ap()[e_][:, 2 * fp * 128:2 * fp * 128 + 256]
                        .rearrange("(k p) f -> p k f", p=128)),
                        waits=[("eT", gup_free[fp % 2])])
                    mlast = 0
                    for si, (s0, sn) in enumerate(TT512):
                        slA, slB = (si % 2) * 2, (si % 2) * 2 + 1
                        for k in range(KCH):
                            w0 = ([gld, xld_all, gate_slot[slA]]
                                  if k == 0 else [])
                            P.op("tensor", lambda e, k=k, slA=slA, s0=s0, sn=sn, fp=fp: e.matmul(
                                psF[:, slA * 512:slA * 512 + sn],
                                gup[fp % 2][:, k * 256:k * 256 + 128],
                                xnt[k][:, s0:s0 + sn], start=(k == 0), stop=(k == KCH - 1)),
                                waits=w0)
                        for k in range(KCH):
                            w0 = [gate_slot[slB]] if k == 0 else []
                            mlast = P.op("tensor", lambda e, k=k, slB=slB, s0=s0, sn=sn, fp=fp: e.matmul(
                                psF[:, slB * 512:slB * 512 + sn],
                                gup[fp % 2][:, k * 256 + 128:k * 256 + 256],
                                xnt[k][:, s0:s0 + sn], start=(k == 0), stop=(k == KCH - 1)),
                                waits=w0)
                        ag = P.op("scalar", lambda e, slA=slA, sn=sn: e.activation(
                            ggb_[:, 0:sn], psF[:, slA * 512:slA * 512 + sn], AF.Gelu),
                            waits=[("eT", mlast), ("eV", vg2_prev)])
                        vg2 = P.op("vector", lambda e, s0=s0, sn=sn: e.tensor_tensor(
                            gg2[:, 0:sn], ggb_[:, 0:sn], cbc[:, s0:s0 + sn], OP.mult),
                            waits=[("eA", ag), ("eV", cbl)])
                        vg2_prev = vg2
                        vact = P.op("vector", lambda e, slB=slB, fp=fp, s0=s0, sn=sn: e.tensor_tensor(
                            actb[:, fp * TB + s0:fp * TB + s0 + sn], gg2[:, 0:sn],
                            psF[:, slB * 512:slB * 512 + sn], OP.mult),
                            waits=[act_dump_gate])
                        gate_slot[slA] = ("eA", ag)
                        gate_slot[slB] = ("eV", vact)
                    gup_free[fp % 2] = mlast
                    gu_mm_done["last"] = mlast
                    if cfg["phases"] <= 2.3:
                        P.kill()
                act_done = P.cnt["eV"]
                if cfg["phases"] <= 2.4:
                    P.kill()
                if dbg and tb == 0 and e_ == 0:
                    act_dump_gate = P.dma("gpsimd",
                                          lambda e: e.dma_start(out=db["d_act"].ap(), in_=actb[:]),
                                          waits=[("eV", act_done)])
                for dh in range(ndh):
                    for hf in range(nhf):
                        bi = (dh * nhf + hf) % 2
                        dld = P.dma("gpsimd", lambda e, e_=e_, dh=dh, hf=hf, bi=bi: e.dma_start(
                            out=dnb[bi][:],
                            in_=io["dn"].ap()[e_][hf * EDH * 128:(hf + 1) * EDH * 128,
                                                  dh * DNH:(dh + 1) * DNH]
                            .rearrange("(k p) f -> p k f", p=128)),
                            waits=[("eT", dn_mm_done[bi])])
                        for tc in range(TB // 128):
                            sl = 2 + tc % 2
                            mm2 = 0
                            for k in range(EDH):
                                w0 = ([dld, ("eV", act_done), dn_slot[sl - 2]]
                                      if k == 0 else [])
                                mm2 = P.op("tensor", lambda e, k=k, sl=sl, tc=tc, hf=hf, bi=bi: e.matmul(
                                    psH[:, (sl - 2) * 512:(sl - 2) * 512 + DNH],
                                    actb[:, (hf * EDH + k) * TB + tc * 128:(hf * EDH + k) * TB + tc * 128 + 128],
                                    dnb[bi][:, k * DNH:(k + 1) * DNH],
                                    start=(k == 0), stop=(k == EDH - 1)), waits=w0)
                            col = tc * D + dh * DNH
                            if e_ == 0 and hf == 0:
                                va = P.op("vector", lambda e, col=col, sl=sl: e.tensor_copy(
                                    acc[:, col:col + DNH], psH[:, (sl - 2) * 512:(sl - 2) * 512 + DNH]),
                                    waits=[("eT", mm2), ("eA", acc_free)])
                            else:
                                va = P.op("vector", lambda e, col=col, sl=sl: e.tensor_tensor(
                                    acc[:, col:col + DNH], acc[:, col:col + DNH],
                                    psH[:, (sl - 2) * 512:(sl - 2) * 512 + DNH], OP.add),
                                    waits=[("eT", mm2)])
                            dn_slot[sl - 2] = ("eV", va)
                        dn_mm_done[bi] = mm2
                    if cfg["phases"] <= 2.5:
                        P.kill()
            if dbg and tb == 0:
                P.dma("gpsimd", lambda e: e.dma_start(out=db["d_acc"].ap(), in_=acc[:]),
                      waits=[dn_slot[0], dn_slot[1]])
            for tc in range(TB // 128):
                for gh in range(D // GH):
                    ags = P.op("scalar", lambda e, tc=tc, gh=gh: e.activation(
                        gtmp[:], acc[:, tc * D + gh * GH:tc * D + (gh + 1) * GH], AF.Silu),
                        waits=[dn_slot[0], dn_slot[1], gst_last])
                    acc_free = ags
                    gst_last = P.dma("scalar", lambda e, tb=tb, tc=tc, gh=gh: e.dma_start(
                        out=dr["gmy"].ap()[tb * TB + tc * 128:tb * TB + (tc + 1) * 128,
                                           gh * GH:(gh + 1) * GH],
                        in_=gtmp[:]), waits=[("eA", ags)])
        gmy_done = P.qcov("scalar")
        if dbg:
            P.dma("gpsimd", lambda e: e.dma_start(out=db["d_gmy"].ap(), in_=dr["gmy"].ap()),
                  waits=P.qcov("scalar"))

        if cfg["phases"] <= 3:
            P.kill()
        P.barrier()
        close_phase()
        sb = newphase()
        qf2 = [sb(f"s_qf{j}", [128, KW], BF16) for j in range(2)]
        kf2 = [sb(f"s_kf{j}", [128, KW], BF16) for j in range(2)]
        ktm2 = [sb(f"s_ktm{j}", [128, KW], BF16) for j in range(2)]
        vtm2 = [sb(f"s_vtm{j}", [128, VW], BF16) for j in range(2)]
        zf2 = [sb(f"s_zf{j}", [128, VW], BF16) for j in range(2)]
        gbt2 = [sb(f"s_gbt{j}", [128, GBW], F32) for j in range(2)]
        lgv = sb("s_lgv", [128, VHL], F32)
        lgbv = sb("s_lgbv", [128, VHL], F32)
        lamE = sb("s_lamE", [128, VHL], F32)
        blamn = sb("s_blamn", [128, VHL], F32)
        escv = sb("s_escv", [128, VHL], F32)
        lamCv = sb("s_lamCv", [128, VHL], F32)
        trow = sb("s_trow", [1, VHL], F32)
        Blg = sb("s_Blg", [128, VW], F32)
        Blgb = sb("s_Blgb", [128, VW], F32)
        Blgp = sb("s_Blgp", [128, VW], F32)
        t1f = sb("s_t1f", [128, VW], F32)
        t2f = sb("s_t2f", [128, VW], F32)
        eMZ = sb("s_eMZ", [128, VW], F32)
        eMTZ = sb("s_eMTZ", [128, VW], F32)
        eDin = sb("s_eDin", [128, VW], F32)
        eLam = sb("s_eLam", [128, VW], F32)
        Mb = sb("s_Mb", [128, VW], BF16)
        MTb = sb("s_MTb", [128, VW], BF16)
        Ptb = sb("s_Ptb", [128, VW], BF16)
        Qlb = sb("s_Qlb", [128, VW], BF16)
        WTb = [sb(f"s_WT{j}", [128, VW], BF16) for j in range(2)]
        Pb = [sb(f"s_Pb{j}", [128, VW], BF16) for j in range(2)]
        PTb = [sb(f"s_PTb{j}", [128, VW], BF16) for j in range(2)]
        Rb = sb("s_Rb", [128, VW], BF16)
        ub = sb("s_ub", [128, VW], BF16)
        ogz = sb("s_ogz", [128, VW], F32)
        sqz = sb("s_sqz", [128, VW], BF16)
        ogb = sb("s_ogb", [128, VW], BF16)
        kbb = sb("s_kbb", [128, VW], BF16)
        rfr = sb("s_rfr", [1, VW], F32)
        regcell = {}

        def regdma(e, row, cg, out_ap, handle, dims):
            if "r" not in regcell:
                regcell["r"] = e.alloc_register(name="roff")
            r = regcell["r"]
            e.reg_load(r, offsb[0:1, row * 64 + cg:row * 64 + cg + 1])
            return e.dma_start(out=out_ap, in_=bass.AP(handle, r, dims))

        def bcv(t):
            return bass.AP(t, 0, [[VHL, 128], [1, VHL], [0, 128]])

        def tile128(t):
            return bass.AP(t, 0, [[128, 128], [0, VHL], [1, 128]])

        def repA(pst, base):
            return bass.AP(pst, base, [[2048, 128], [128, NKHL], [0, 2], [1, 128]])

        def repK(t):
            return bass.AP(t, 0, [[KW, 128], [128, NKHL], [0, 2], [1, 128]])

        G_ = min(4, VHL)
        H_ = VHL // G_

        def ilvcol(vh):
            return 512 * (vh % G_) + 128 * (vh // G_)

        def ilv(pst):
            return bass.AP(pst, 0, [[2048, 128], [128, H_], [512, G_], [1, 128]])

        if cfg.get("sk", 99) <= -2:
            P.kill()
        rs = []
        for bb in range(2):
            rs.append(P.dma("gpsimd", lambda e, bb=bb: regdma(
                e, 5, bb, dr["scan_loc"].ap()[bb * nchl:(bb + 1) * nchl], dr["gcb"],
                [[2 * SECL * 128, nchl], [128, SECL], [1, 128]]),
                waits=[("cc", ccA)]))
            rs.append(P.dma("gpsimd", lambda e, bb=bb: regdma(
                e, 6, bb, dr["gb_loc"].ap()[bb * Tloc:(bb + 1) * Tloc], dr["ggb"],
                [[2 * GBW, Tloc], [1, GBW]]),
                waits=[("cc", ccB)]))
        rs_done = P.qcov("gpsimd")
        if cfg.get("sk", 99) <= -1:
            P.kill()
        c_qf = [Z, Z]
        c_kf = [Z, Z]
        c_ktm = [Z, Z]
        c_vtm = [Z, Z]
        c_zf = [Z, Z]
        c_gbt = [Z, Z]
        prev_Sbf = 0
        prev_STog = Z
        prev_psF_cons = ("eV", 0)
        prev_psH_cons = ("eV", 0)
        gT1, gT2 = [Z, Z], [Z, Z]
        gLq, gLk, gLz, gLgb = [Z, Z], [Z, Z], [Z, Z], [Z, Z]
        ogst = {}
        for cg in range(nchg):
            j = cg % 2
            rw = rs_done if cg < 2 else []
            gLgb[j] = P.dma("scalar", lambda e, j=j, cg=cg: e.dma_start(
                out=gbt2[j][:], in_=dr["gb_loc"].ap()[cg * 128:(cg + 1) * 128, :]),
                waits=[c_gbt[j], rw])
            gLq[j] = P.dma("scalar", lambda e, j=j, cg=cg: e.dma_start(
                out=qf2[j][:].rearrange("p (h t) -> p h t", h=NKHL),
                in_=dr["scan_loc"].ap()[cg, 0:KW, :].rearrange("(h f) t -> f h t", f=128)),
                waits=[c_qf[j], rw])
            gLk[j] = P.dma("scalar", lambda e, j=j, cg=cg: e.dma_start(
                out=kf2[j][:].rearrange("p (h t) -> p h t", h=NKHL),
                in_=dr["scan_loc"].ap()[cg, KW:2 * KW, :].rearrange("(h f) t -> f h t", f=128)),
                waits=[c_kf[j], rw])
            gLz[j] = P.dma("scalar", lambda e, j=j, cg=cg: e.dma_start(
                out=zf2[j][:].rearrange("p (h t) -> p h t", h=VHL),
                in_=dr["scan_loc"].ap()[cg, 2 * KW + VW:SECL, :].rearrange("(h f) t -> f h t", f=128)),
                waits=[c_zf[j], rw])
            if cfg.get("sk", 99) <= -0.5:
                P.kill()
            gT1[j] = P.dma("sync", lambda e, j=j, cg=cg: e.dma_start(
                out=ktm2[j][:], in_=dr["scan_loc"].ap()[cg, KW:2 * KW, :], transpose=True),
                waits=[c_ktm[j], rw])
            gT2[j] = P.dma("sync", lambda e, j=j, cg=cg: e.dma_start(
                out=vtm2[j][:], in_=dr["scan_loc"].ap()[cg, 2 * KW:2 * KW + VW, :], transpose=True),
                waits=[c_vtm[j], rw])

            if cfg.get("sk", 99) <= 0:
                P.kill()
            M1 = P.op("tensor", lambda e, j=j: e.matmul(
                psH[:, 0:VHL], ctri[:], gbt2[j][:, VHL:2 * VHL], start=True, stop=True),
                waits=[gLgb[j], prev_psH_cons])
            M2 = P.op("tensor", lambda e, j=j: e.matmul(
                psH[0:1, 256:256 + VHL], c1cf[:], gbt2[j][:, VHL:2 * VHL],
                start=True, stop=True))
            V1 = P.op("vector", lambda e: e.tensor_copy(lgv[:], psH[:, 0:VHL]),
                      waits=[("eT", M2)])
            V2 = P.op("vector", lambda e: e.tensor_copy(trow[:], psH[0:1, 256:256 + VHL]))
            M3 = P.op("tensor", lambda e: e.matmul(
                psH[:, 512:512 + VHL], c1r[0:1, :], trow[0:1, :], start=True, stop=True),
                waits=[("eV", V2)])
            V3 = P.op("vector", lambda e, j=j: e.tensor_tensor(
                lgbv[:], lgv[:], gbt2[j][:, 2 * VHL:3 * VHL], OP.add))
            A1 = P.op("scalar", lambda e: e.activation(lamCv[:], psH[:, 512:512 + VHL], AF.Exp),
                      waits=[("eT", M3)])
            V4 = P.op("vector", lambda e: e.tensor_tensor(
                t1f[:, 0:VHL], lgv[:], psH[:, 512:512 + VHL], OP.subtract),
                waits=[("eT", M3)])
            A2 = P.op("scalar", lambda e: e.activation(escv[:], t1f[:, 0:VHL], AF.Exp,
                                                       scale=-1.0),
                      waits=[("eV", V4)])
            A3 = P.op("scalar", lambda e: e.activation(lamE[:], lgv[:], AF.Exp),
                      waits=[("eV", V3)])
            V5 = P.op("vector", lambda e, j=j: e.tensor_tensor(
                blamn[:], gbt2[j][:, 0:VHL], lamE[:], OP.mult), waits=[("eA", A3)])
            V6 = P.op("vector", lambda e: e.tensor_scalar(
                blamn[:], blamn[:], -1.0, None, OP.mult, OP.bypass))

            if cfg.get("sk", 99) <= 1:
                P.kill()
            V7 = P.op("vector", lambda e: e.tensor_copy(Blg[:], bcv(lgv)))
            V8 = P.op("vector", lambda e: e.tensor_tensor(
                Blgb[:], bcv(lgbv), tile128(cmM), OP.add))
            if cfg.get("sk", 99) <= 1.2:
                P.kill()
            trl = 0
            for vh in range(VHL):
                trl = P.op("tensor", lambda e, vh=vh: e.transpose(
                    psF[:, ilvcol(vh):ilvcol(vh) + 128], Blg[:, vh * 128:(vh + 1) * 128], cI[:]),
                    waits=[("eV", V7), prev_psF_cons] if vh == 0 else [])
            if cfg.get("sk", 99) <= 1.3:
                P.kill()
            trl2 = 0
            for vh in range(VHL):
                trl2 = P.op("tensor", lambda e, vh=vh: e.transpose(
                    psH[:, ilvcol(vh):ilvcol(vh) + 128], Blgb[:, vh * 128:(vh + 1) * 128], cI[:]),
                    waits=[("eV", V8), ("eV", V4), ("eA", A1), ("eV", V1)] if vh == 0 else [])
            if cfg.get("sk", 99) <= 1.4:
                P.kill()
            V9 = P.op("vector", lambda e: e.tensor_tensor(
                t1f[:], Blgb[:], ilv(psF), OP.subtract), waits=[("eT", trl)])
            A4 = P.op("scalar", lambda e: e.activation(eMZ[:], t1f[:], AF.Exp),
                      waits=[("eV", V9), ("eA", A2)])
            if cfg.get("sk", 99) <= 1.5:
                P.kill()
            V10 = P.op("vector", lambda e: e.tensor_tensor(
                t2f[:], Blg[:], ilv(psH), OP.subtract), waits=[("eT", trl2)])
            A5 = P.op("scalar", lambda e: e.activation(eMTZ[:], t2f[:], AF.Exp, scale=-1.0),
                      waits=[("eV", V10)])
            if cfg.get("sk", 99) <= 1.6:
                P.kill()
            A6 = P.op("scalar", lambda e: e.activation(eLam[:], ilv(psF), AF.Exp),
                      waits=[("eT", trl)])
            if cfg.get("sk", 99) <= 1.7:
                P.kill()
            V11 = P.op("vector", lambda e: e.tensor_tensor(
                Blgp[:], Blg[:], tile128(cmPT), OP.add))
            if cfg.get("sk", 99) <= 1.8:
                P.kill()
            V12 = P.op("vector", lambda e: e.tensor_tensor(
                t1f[:], Blgp[:], ilv(psF), OP.subtract),
                waits=[("eA", A6)])
            if cfg.get("sk", 99) <= 1.9:
                P.kill()
            A7 = P.op("scalar", lambda e: e.activation(eDin[:], t1f[:], AF.Exp, scale=-1.0),
                      waits=[("eV", V12)])
            if cfg.get("sk", 99) <= 2:
                P.kill()
            agl = 0
            for kl in range(NKHL):
                agl = P.op("tensor", lambda e, kl=kl, j=j: e.matmul(
                    psF[:, kl * 128:(kl + 1) * 128], kf2[j][:, kl * 128:(kl + 1) * 128],
                    kf2[j][:, kl * 128:(kl + 1) * 128], start=True, stop=True),
                    waits=[("eA", A6), ("eV", V12), gLk[j], gLq[j]]
                    if kl == 0 else [])
            for kl in range(NKHL):
                agl = P.op("tensor", lambda e, kl=kl, j=j: e.matmul(
                    psF[:, KW + kl * 128:KW + (kl + 1) * 128],
                    kf2[j][:, kl * 128:(kl + 1) * 128],
                    qf2[j][:, kl * 128:(kl + 1) * 128], start=True, stop=True))
            V13 = P.op("vector", lambda e: e.tensor_tensor(
                Mb[:], eMZ[:], repA(psF, 0), OP.mult), waits=[("eT", agl), ("eA", A5)])
            V14 = P.op("vector", lambda e: e.tensor_tensor(
                MTb[:], eMTZ[:], repA(psF, 0), OP.mult))
            V15 = P.op("vector", lambda e: e.tensor_tensor(
                Ptb[:], eDin[:], repA(psF, KW), OP.mult), waits=[("eA", A7)])
            V16 = P.op("vector", lambda e, j=j: e.tensor_tensor(
                Qlb[:], repK(qf2[j]), eLam[:], OP.mult), waits=[("eA", A6)])
            V17 = P.op("vector", lambda e: e.scalar_tensor_tensor(
                WTb[0][:], MTb[:], -1.0, tile128(cI), OP.mult, OP.add))
            c_qf[j] = ("eV", V16)
            if cfg.get("sk", 99) <= 3:
                P.kill()
            cur = 0
            lastPTev = V15
            lastW = V17
            lastPn = 0
            psHfree = ("eV", V10)
            for lev in range(1, NLEV + 1):
                nxt = lev % 2
                Ps = Mb if lev == 1 else Pb[cur]
                PTs = MTb if lev == 1 else PTb[cur]
                ml = 0
                for vh in range(VHL):
                    ml = P.op("tensor", lambda e, vh=vh, Ps=Ps, PTs=PTs: e.matmul(
                        psH[:, vh * 128:(vh + 1) * 128], PTs[:, vh * 128:(vh + 1) * 128],
                        Ps[:, vh * 128:(vh + 1) * 128], start=True, stop=True),
                        waits=[psHfree, ("eV", V14 if lev == 1 else lastW)] if vh == 0 else [])
                vPn = P.op("vector", lambda e, nxt=nxt: e.tensor_copy(Pb[nxt][:], psH[:, 0:VW]),
                           waits=[("eT", ml)])
                if lev < NLEV:
                    ml2 = 0
                    for vh in range(VHL):
                        ml2 = P.op("tensor", lambda e, vh=vh, Ps=Ps, PTs=PTs: e.matmul(
                            psF[:, vh * 128:(vh + 1) * 128], Ps[:, vh * 128:(vh + 1) * 128],
                            PTs[:, vh * 128:(vh + 1) * 128], start=True, stop=True),
                            waits=[("eV", V15 if lev == 1 else lastPTev), ("eV", V16),
                                   ("eA", A6)] if vh == 0 else [])
                    lastPTev_ = P.op("vector", lambda e, nxt=nxt: e.tensor_copy(
                        PTb[nxt][:], psF[:, 0:VW]), waits=[("eT", ml2)])
                    lastPTev = lastPTev_
                wl = 0
                for vh in range(VHL):
                    wl = P.op("tensor", lambda e, vh=vh, nxt=nxt, cur=cur: e.matmul(
                        psH[:, vh * 128:(vh + 1) * 128], Pb[nxt][:, vh * 128:(vh + 1) * 128],
                        WTb[cur][:, vh * 128:(vh + 1) * 128], start=True, stop=True),
                        waits=[("eV", vPn)] if vh == 0 else [])
                lastW = P.op("vector", lambda e, nxt=nxt, cur=cur: e.tensor_tensor(
                    WTb[nxt][:], WTb[cur][:], psH[:, 0:VW], OP.add), waits=[("eT", wl)])
                psHfree = ("eV", lastW)
                cur = nxt
            WTf = WTb[cur]
            if cfg.get("sk", 99) <= 4:
                P.kill()
            ksl = 0
            for vh in range(VHL):
                kl = vh // 2
                ksl = P.op("tensor", lambda e, vh=vh, kl=kl, j=j: e.matmul(
                    psF[:, vh * 128:(vh + 1) * 128], kf2[j][:, kl * 128:(kl + 1) * 128],
                    Sbf[:, vh * 128:(vh + 1) * 128], start=True, stop=True),
                    waits=[("eV", lastPTev), ("eV", prev_Sbf), ("eV", V16)] if vh == 0 else [])
            V18 = P.op("vector", lambda e, j=j: e.tensor_tensor(
                t1f[:], vtm2[j][:], bcv(bass_col(gbt2[j], 0, VHL)) if False else
                bass.AP(gbt2[j], 0, [[GBW, 128], [1, VHL], [0, 128]]), OP.mult),
                waits=[gT2[j], ("eA", A7)])
            c_vtm[j] = ("eV", V18)
            c_gbt[j] = ("eV", V18)
            V19 = P.op("vector", lambda e: e.tensor_tensor(
                t2f[:], bcv(blamn), psF[:, 0:VW], OP.mult),
                waits=[("eT", ksl), ("eA", A7)])
            V20 = P.op("vector", lambda e: e.tensor_tensor(Rb[:], t1f[:], t2f[:], OP.add))
            ul = 0
            for vh in range(VHL):
                ul = P.op("tensor", lambda e, vh=vh, WTf=WTf: e.matmul(
                    psH[:, vh * 128:(vh + 1) * 128], WTf[:, vh * 128:(vh + 1) * 128],
                    Rb[:, vh * 128:(vh + 1) * 128], start=True, stop=True),
                    waits=[("eV", V20), psHfree] if vh == 0 else [])
            V21 = P.op("vector", lambda e: e.tensor_copy(ub[:], psH[:, 0:VW]), waits=[("eT", ul)])
            ol = 0
            for vh in range(VHL):
                P.op("tensor", lambda e, vh=vh: e.matmul(
                    psF[:, vh * 128:(vh + 1) * 128], ub[:, vh * 128:(vh + 1) * 128],
                    Ptb[:, vh * 128:(vh + 1) * 128], start=True, stop=False),
                    waits=[("eV", V21), ("eV", V19)] if vh == 0 else [])
                ol = P.op("tensor", lambda e, vh=vh: e.matmul(
                    psF[:, vh * 128:(vh + 1) * 128], Sbf[:, vh * 128:(vh + 1) * 128],
                    Qlb[:, vh * 128:(vh + 1) * 128], start=False, stop=True))
            if cfg.get("sk", 99) <= 5:
                P.kill()
            V22 = P.op("vector", lambda e, j=j: e.tensor_tensor(
                ogz[:], zf2[j][:], psF[:, 0:VW], OP.mult), waits=[("eT", ol), gLz[j]])
            c_zf[j] = ("eV", V22)
            A8 = P.op("scalar", lambda e: e.activation(sqz[:], ogz[:], AF.Square),
                      waits=[("eV", V22)])
            rl = 0
            for vh in range(VHL):
                rl = P.op("tensor", lambda e, vh=vh: e.matmul(
                    psH[0:1, vh * 128:(vh + 1) * 128], c1cb[:],
                    sqz[:, vh * 128:(vh + 1) * 128], start=True, stop=True),
                    waits=[("eA", A8), ("eV", V21)] if vh == 0 else [])
            A9 = P.op("scalar", lambda e: e.activation(
                rfr[0:1, :], psH[0:1, 0:VW], AF.Ln, scale=1.0 / 128, bias=1e-6),
                waits=[("eT", rl)])
            A10 = P.op("scalar", lambda e: e.activation(
                rfr[0:1, :], rfr[0:1, :], AF.Exp, scale=-0.5))
            bl = 0
            for vh in range(VHL):
                bl = P.op("tensor", lambda e, vh=vh: e.matmul(
                    psH[:, vh * 128:(vh + 1) * 128], c1r[0:1, :],
                    rfr[0:1, vh * 128:(vh + 1) * 128], start=True, stop=True),
                    waits=[("eA", A10)] if vh == 0 else [])
            V23a = P.op("vector", lambda e: e.tensor_tensor(
                t1f[:], ogz[:], psH[:, 0:VW], OP.mult), waits=[("eT", bl)])
            V23 = P.op("vector", lambda e: e.tensor_scalar(
                ogb[:], t1f[:], nwc[:, 0:1], None, OP.mult, OP.bypass),
                waits=[prev_STog])
            prev_STog = P.dma("scalar", lambda e, cg=cg: e.dma_start(
                out=dr["og"].ap()[:, cg * 128:(cg + 1) * 128]
                .rearrange("(vh f) t -> f vh t", vh=VHL), in_=ogb[:]
                .rearrange("p (vh t) -> p vh t", vh=VHL)), waits=[("eV", V23)])
            ogst[cg] = prev_STog
            if cfg.get("sk", 99) <= 6:
                P.kill()
            V24 = P.op("vector", lambda e, j=j: e.tensor_tensor(
                kbb[:], repK(ktm2[j]), bcv(escv), OP.mult),
                waits=[gT1[j], ("eA", A2)])
            c_ktm[j] = ("eV", V24)
            c_kf[j] = ("eT", ksl)
            sul = 0
            for vh in range(VHL):
                sul = P.op("tensor", lambda e, vh=vh: e.matmul(
                    psF[:, vh * 128:(vh + 1) * 128], kbb[:, vh * 128:(vh + 1) * 128],
                    ub[:, vh * 128:(vh + 1) * 128], start=True, stop=True),
                    waits=[("eV", V24), ("eV", V22)] if vh == 0 else [])
            V25 = P.op("vector", lambda e: e.tensor_tensor(
                t2f[:], Sf[:], bcv(lamCv), OP.mult), waits=[("eA", A1)])
            V26 = P.op("vector", lambda e: e.tensor_tensor(
                Sf[:], t2f[:], psF[:, 0:VW], OP.add), waits=[("eT", sul)])
            prev_Sbf = P.op("vector", lambda e: e.tensor_copy(Sbf[:], Sf[:]))
            prev_psF_cons = ("eV", V26)
            prev_psH_cons = ("eT", bl) if False else ("eV", V23)
        og_done = P.qcov("scalar")
        if dbg:
            P.dma("gpsimd", lambda e: e.dma_start(out=db["d_og"].ap(), in_=dr["og"].ap()),
                  waits=P.qcov("scalar"))

        if cfg["phases"] <= 4:
            P.kill()
        P.barrier()
        close_phase()
        sb = newphase()
        wout_sb = sb("w_wout", [128, VHL * D], BF16)
        ogt = [sb(f"w_ogt{j}", [128, VW], BF16) for j in range(2)]
        ssb = [sb(f"w_ssb{j}", [128, D], F32) for j in range(2)]
        wol = P.dma("gpsimd", lambda e: e.dma_start(
            out=wout_sb[:].rearrange("p (vh d) -> p vh d", vh=VHL),
            in_=io["Wout_my"].ap().rearrange("(vh f) d -> f vh d", f=128)))
        ps_of = [psF, psH]
        wev = [Z, Z]
        sst = [Z, Z]
        wst_last = Z
        TH = Tloc // 2
        nhl = nchl // 2
        order = ([tc for tc in range(2 * nchl) if (tc % nchl) // nhl == 0]
                 + [tc for tc in range(2 * nchl) if (tc % nchl) // nhl == 1])
        ccC_h = []
        for ii, tc in enumerate(order):
            jj = ii % 2
            ogl = P.dma("sync", lambda e, tc=tc, jj=jj: e.dma_start(
                out=ogt[jj][:].rearrange("p (vh t) -> p vh t", vh=VHL),
                in_=dr["og"].ap()[:, tc * 128:(tc + 1) * 128]
                .rearrange("(vh f) t -> f vh t", f=128)),
                waits=[og_done, wev[jj]])
            mml = 0
            pst = ps_of[jj]
            for dt in range(D // 512):
                for vh in range(VHL):
                    mml = P.op("tensor", lambda e, dt=dt, vh=vh, jj=jj, pst=pst: e.matmul(
                        pst[:, dt * 512:(dt + 1) * 512],
                        ogt[jj][:, vh * 128:(vh + 1) * 128],
                        wout_sb[:, vh * D + dt * 512:vh * D + dt * 512 + 512],
                        start=(vh == 0), stop=(vh == VHL - 1)),
                        waits=[ogl, wol, wev[jj]] if (dt == 0 and vh == 0) else [])
            vev = P.op("vector", lambda e, jj=jj, pst=pst: e.tensor_copy(ssb[jj][:], pst[:, 0:D]),
                       waits=[("eT", mml), sst[jj]])
            wev[jj] = ("eV", vev)
            h = (tc % nchl) // nhl
            sp_row = h * Tloc + (tc // nchl) * TH + ((tc % nchl) - h * nhl) * 128
            wst_last = P.dma("scalar", lambda e, sp_row=sp_row, jj=jj: e.dma_start(
                out=dr["sp"].ap()[sp_row:sp_row + 128, :], in_=ssb[jj][:]),
                waits=[("eV", vev)])
            sst[jj] = wst_last
            if ii == nchl - 1:
                ccC_h.append(P.cc(lambda e: e.collective_compute(
                    "ReduceScatter", OP.add,
                    replica_groups=[[2 * i, 2 * i + 1] for i in range(B)],
                    ins=[dr["sp"].ap()[0:Tloc, :].opt()],
                    outs=[dr["smy"].ap()[0:TH, :].opt()]),
                    waits=P.qcov("scalar")))
        if dbg:
            P.dma("gpsimd", lambda e: e.dma_start(out=db["d_sp"].ap(), in_=dr["sp"].ap()),
                  waits=P.qcov("scalar"))
        ccC_h.append(P.cc(lambda e: e.collective_compute(
            "ReduceScatter", OP.add, replica_groups=[[2 * i, 2 * i + 1] for i in range(B)],
            ins=[dr["sp"].ap()[Tloc:2 * Tloc, :].opt()],
            outs=[dr["smy"].ap()[TH:Tloc, :].opt()]),
            waits=P.qcov("scalar")))
        ccC = ccC_h[-1]
        if dbg:
            P.dma("gpsimd", lambda e: e.dma_start(out=db["d_smy"].ap(), in_=dr["smy"].ap()),
                  waits=[("cc", ccC)])

        if cfg["phases"] <= 5:
            P.kill()
        P.barrier(exclude=("cc",))
        close_phase()
        sb = newphase()
        fx = [sb(f"fi_x{j}", [128, D], F32) for j in range(2)]
        fs = [sb(f"fi_s{j}", [128, D], F32) for j in range(2)]
        fg = [sb(f"fi_g{j}", [128, D], F32) for j in range(2)]
        fo = [sb(f"fi_o{j}", [128, D], F32) for j in range(2)]
        fev = [Z, Z]
        fst = [Z, Z]
        for tc in range(Tloc // 128):
            jj = tc % 2
            lx = P.dma("sync", lambda e, tc=tc, jj=jj: e.dma_start(
                out=fx[jj][:], in_=io["x_slice"].ap()[3 + tc * 128:3 + (tc + 1) * 128, :]),
                waits=[fev[jj]])
            ls = P.dma("sync", lambda e, tc=tc, jj=jj: e.dma_start(
                out=fs[jj][:], in_=dr["smy"].ap()[tc * 128:(tc + 1) * 128, :]),
                waits=[("cc", ccC_h[0] if tc < nhl else ccC_h[1])])
            lg_ = P.dma("scalar", lambda e, tc=tc, jj=jj: e.dma_start(
                out=fg[jj][:], in_=dr["gmy"].ap()[tc * 128:(tc + 1) * 128, :]),
                waits=[gmy_done])
            v1_ = P.op("vector", lambda e, jj=jj: e.tensor_tensor(
                fo[jj][:], fs[jj][:], fg[jj][:], OP.mult),
                waits=[ls, lg_, fst[jj]])
            v2_ = P.op("vector", lambda e, jj=jj: e.tensor_tensor(
                fo[jj][:], fo[jj][:], fx[jj][:], OP.add), waits=[lx])
            fev[jj] = ("eV", v2_)
            fst[jj] = P.dma("scalar", lambda e, tc=tc, jj=jj: e.dma_start(
                out=io["out_my"].ap()[tc * 128:(tc + 1) * 128, :], in_=fo[jj][:]),
                waits=[("eV", v2_)])
        if getattr(P, "dead", False):
            P.dead = False
            P.dma("gpsimd", lambda e: e.dma_start(
                out=io["out_my"].ap(), in_=io["x_slice"].ap()[3:, :]))
        P.barrier()

        if cfg.get("dbgcnt"):
            print("PLAN cnt:", {k: v for k, v in P.cnt.items() if v},
                  "steps:", {e: len(s) for e, s in P.steps.items()})
        P.emit(nc, semh)
        close_phase()
    return nc


def bass_col(t, a, b):
    return t[:, a:b]


_CACHE = {}


def kernel(**inputs):
    cfg = mkcfg(mini=False)
    key = "full"
    if key not in _CACHE:
        _CACHE[key] = build(cfg)
    nc = _CACHE[key]
    maps = prep_inputs(inputs, cfg)
    res = run_bass_kernel_spmd(nc, maps, core_ids=list(range(cfg["ncores"])))
    B, Tloc, D = cfg["B"], cfg["Tloc"], cfg["D"]
    out = np.empty((B, 2 * Tloc, D), np.float32)
    for c in range(cfg["ncores"]):
        b, p = c // 2, c % 2
        out[b, p * Tloc:(p + 1) * Tloc] = np.asarray(res.results[c]["out_my"]).reshape(Tloc, D)
    return out

